# version 21
# baseline (speedup 1.0000x reference)
import math
import sys
from contextlib import ExitStack

sys.path.insert(0, "/opt/trn_rl_repo")
import numpy as np
import ml_dtypes
import concourse.bass as bass
import concourse.mybir as mybir
from concourse.bass_utils import run_bass_kernel_spmd

F32 = mybir.dt.float32
BF16 = mybir.dt.bfloat16
AF = mybir.ActivationFunctionType
ALU = mybir.AluOpType
AXX = mybir.AxisListType.X

H = 2048
NH = 16
S_FULL = 4096
HID = 5461
HIDP = 5504
NCH = 43
T = 2176
KEYN = 2050
NM = 205
NG = 5
GR = 41
GK = 410
RSQ = 1.0 / math.sqrt(128.0)
EPS = 1e-5
INV_H = 1.0 / float(H)
W1CNT = float(HID * H)
W2CNT = float(H * HID)
VCHUNK = [128, 128, 128, 26]
FFN_HALVES = [(0, 1024), (1024, 1152)]


def build_program():
    nc = bass.Bass("TRN2", target_bir_lowering=False)

    x_sh = nc.dram_tensor("x_sh", [T, H], F32, kind="ExternalInput").ap()
    wq = nc.dram_tensor("wq", [H, H], F32, kind="ExternalInput").ap()
    wk = nc.dram_tensor("wk", [H, H], F32, kind="ExternalInput").ap()
    wv = nc.dram_tensor("wv", [H, H], F32, kind="ExternalInput").ap()
    wo = nc.dram_tensor("wo", [H, H], F32, kind="ExternalInput").ap()
    bq = nc.dram_tensor("bq", [H], F32, kind="ExternalInput").ap()
    bk = nc.dram_tensor("bk", [H], F32, kind="ExternalInput").ap()
    bv = nc.dram_tensor("bv", [H], F32, kind="ExternalInput").ap()
    bo = nc.dram_tensor("bo", [H], F32, kind="ExternalInput").ap()
    w1s = nc.dram_tensor("w1s", [HID, 256], F32, kind="ExternalInput").ap()
    w3s = nc.dram_tensor("w3s", [HID, 256], F32, kind="ExternalInput").ap()
    w2s = nc.dram_tensor("w2s", [H, 688], F32, kind="ExternalInput").ap()
    maskin = nc.dram_tensor("maskin", [GR, KEYN], BF16, kind="ExternalInput").ap()
    id32in = nc.dram_tensor("id32in", [128, 128], F32, kind="ExternalInput").ap()
    ones32in = nc.dram_tensor("ones32in", [128, 128], F32,
                              kind="ExternalInput").ap()
    onesbin = nc.dram_tensor("onesbin", [128, 1], BF16, kind="ExternalInput").ap()
    onesrbin = nc.dram_tensor("onesrbin", [1, 128], BF16,
                              kind="ExternalInput").ap()

    out_sh = nc.dram_tensor("out_sh", [T, H], F32, kind="ExternalOutput").ap()

    xn_scr = nc.dram_tensor("xn_scr", [T, H], BF16, kind="Internal").ap()
    x1_scr = nc.dram_tensor("x1_scr", [T, H], F32, kind="Internal").ap()
    x1n_scr = nc.dram_tensor("x1n_scr", [T, H], BF16, kind="Internal").ap()
    xs_in = nc.dram_tensor("xs_in", [128, 16], F32, kind="Internal").ap()
    xs_out = nc.dram_tensor("xs_out", [128, 16], F32, kind="Internal").ap()
    ab_in = nc.dram_tensor("ab_in", [1, 4], F32, kind="Internal").ap()
    ab_out = nc.dram_tensor("ab_out", [1, 4], F32, kind="Internal").ap()
    t1sh = nc.dram_tensor("t1sh", [256, HIDP], BF16, kind="Internal").ap()
    t3sh = nc.dram_tensor("t3sh", [256, HIDP], BF16, kind="Internal").ap()
    t2sh = nc.dram_tensor("t2sh", [688, H], BF16, kind="Internal").ap()
    t1T = nc.dram_tensor("t1T", [H, HIDP], BF16, kind="Internal").ap()
    t3T = nc.dram_tensor("t3T", [H, HIDP], BF16, kind="Internal").ap()
    t2T = nc.dram_tensor("t2T", [HIDP, H], BF16, kind="Internal").ap()
    t2B = nc.dram_tensor("t2B", [4 * HIDP, 512], BF16, kind="Internal").ap()

    es = ExitStack()
    sem_names = ["sT", "sV", "sA", "sC", "sW", "sLc", "sGc", "sP", "sQw",
                 "sXa", "sXb", "sBa", "sBb", "sW13a", "sW13b", "sW2a", "sW2b",
                 "sW2c", "sW2d",
                 "sWoa", "sWob", "sQa", "sQb", "sFa", "sFb", "sOa", "sOb",
                 "sNa", "sNb"]
    sem = {n: es.enter_context(nc.semaphore(n)) for n in sem_names}
    cnt = {n: 0 for n in sem_names}
    cur = {e: {} for e in "TVASG"}

    def w(ec, eng, s, val=None):
        v = cnt[s] if val is None else val
        if v <= 0:
            return
        if cur[ec].get(s, -1) < v:
            eng.wait_ge(sem[s], v)
            cur[ec][s] = v

    def inc(instr, s, amt=1):
        if s in ("sV", "sA"):
            return cnt[s]
        instr.then_inc(sem[s], amt)
        cnt[s] += amt
        return cnt[s]

    PB = [es.enter_context(nc.psum_tensor(f"pb{i}", [128, 512], F32))
          for i in range(8)]
    bankg = {i: [] for i in range(8)}

    def bank_wait(t, i):
        for (s, v) in bankg[i]:
            w("T", t, s, v)
        bankg[i] = []

    id32 = es.enter_context(nc.sbuf_tensor("id32", [128, 128], F32))
    ones32 = es.enter_context(nc.sbuf_tensor("ones32", [128, 128], F32))
    onesb = es.enter_context(nc.sbuf_tensor("onesb", [128, 1], BF16))
    onesr = es.enter_context(nc.sbuf_tensor("onesr", [1, 128], BF16))
    sc_b = es.enter_context(nc.sbuf_tensor("sc_b", [128, 14], F32))
    stats = es.enter_context(nc.sbuf_tensor("stats", [128, 8], F32))
    scr2k = es.enter_context(nc.sbuf_tensor("scr2k", [128, H], F32))
    qrow = es.enter_context(nc.sbuf_tensor("qrow", [1, 8], F32))
    epsc = es.enter_context(nc.sbuf_tensor("epsc", [128, 1], F32))

    blk = es.enter_context(nc.Block())

    _CHAIN_OPS = {
        "tensor_scalar", "scalar_tensor_tensor", "tensor_tensor",
        "tensor_copy", "reduce_max", "reduce_sum", "tensor_reduce",
        "tensor_single_scalar", "tensor_scalar_min", "tensor_scalar_mul",
        "tensor_scalar_add", "tensor_add", "tensor_sub", "tensor_mul",
        "reciprocal", "memset", "memzero", "activation",
    }

    class ChainEng:

        def __init__(self, eng, sname):
            self._eng = eng
            self._sname = sname

        def __getattr__(self, name):
            fn = getattr(self._eng, name)
            if name not in _CHAIN_OPS:
                return fn
            sname = self._sname

            def wrapped(*a, **k):
                self._eng.wait_ge(sem[sname], cnt[sname])
                instr = fn(*a, **k)
                instr.then_inc(sem[sname], 1)
                cnt[sname] += 1
                return instr
            return wrapped

    def S_(fn):
        blk.sync(fn)

    def G_(fn):
        blk.gpsimd(fn)

    def P_(fn):
        blk.gpsimd(lambda g: fn(ChainEng(g, "sP")))

    def VP_(fn):
        blk.vector(lambda v: fn(ChainEng(v, "sP")))

    def T_(fn):
        blk.tensor(fn)

    def V_(fn):
        blk.vector(lambda v: fn(ChainEng(v, "sV")))

    def A_(fn):
        blk.scalar(lambda a: fn(ChainEng(a, "sA")))

    ESEC = {"V": V_, "A": A_}
    ESEM = {"V": "sV", "A": "sA"}

    def mm_evac_loop(n_iter, banks, emit_mm, emit_evac, evac_eng="A",
                     t_waits=None):
        esem = ESEM[evac_eng]
        e_base = cnt[esem]
        t_base = cnt["sT"]
        nb = len(banks)

        def tp(t):
            if t_waits:
                for (s, v) in t_waits:
                    w("T", t, s, v)
            for i in range(n_iter):
                b = banks[i % nb]
                if i >= nb:
                    w("T", t, esem, e_base + (i - nb) + 1)
                else:
                    bank_wait(t, b)
                emit_mm(t, i, b)
        T_(tp)
        assert cnt["sT"] == t_base + n_iter, "emit_mm must inc sT once"

        def ep(e):
            for i in range(n_iter):
                b = banks[i % nb]
                w(evac_eng, e, "sT", t_base + i + 1)
                emit_evac(e, i, b)
        ESEC[evac_eng](ep)
        assert cnt[esem] == e_base + n_iter, "emit_evac must inc once"
        for i in range(max(0, n_iter - nb), n_iter):
            bankg[banks[i % nb]].append((esem, e_base + i + 1))
        return cnt[esem]

    def ln_v1(v, src_ap, n, p=128):
        v.tensor_scalar(scr2k[0:p, 0:n], src_ap, 1.0, None, ALU.mult, ALU.add,
                        accum_out=stats[0:p, 0:1])
        v.scalar_tensor_tensor(scr2k[0:p, 0:n], src_ap, 1.0, src_ap,
                               ALU.mult, ALU.mult, accum_out=stats[0:p, 1:2])
        invn = 1.0 / float(n)
        v.tensor_scalar(stats[0:p, 2:3], stats[0:p, 0:1], invn, None, ALU.mult)
        v.tensor_scalar(stats[0:p, 3:4], stats[0:p, 1:2], invn, None, ALU.mult)
        v.tensor_tensor(stats[0:p, 6:7], stats[0:p, 2:3], stats[0:p, 2:3],
                        ALU.mult)
        return v.tensor_tensor(stats[0:p, 3:4], stats[0:p, 3:4],
                               stats[0:p, 6:7], ALU.subtract)

    def ln_full(src_ap, n, pre_v=None, p=128):
        def v1(v):
            if pre_v is not None:
                pre_v(v)
            inc(ln_v1(v, src_ap, n, p=p), "sV")
        V_(v1)
        stm = cnt["sV"]

        def a1(a, stm=stm):
            w("A", a, "sV", stm)
            inc(a.activation(stats[0:p, 4:5], stats[0:p, 3:4], AF.Sqrt,
                             bias=epsc[0:p, 0:1]), "sA")
        A_(a1)
        sqm = cnt["sA"]

        def v2(v, sqm=sqm):
            w("V", v, "sA", sqm)
            v.reciprocal(stats[0:p, 5:6], stats[0:p, 4:5])
        V_(v2)

    att = ExitStack()
    mask_sb = att.enter_context(nc.sbuf_tensor("mask_sb", [GR, KEYN], BF16))
    bqs = att.enter_context(nc.sbuf_tensor("bqs", [128, 16], F32))
    bkc = att.enter_context(nc.sbuf_tensor("bkc", [128, 16], F32))
    rows = att.enter_context(nc.sbuf_tensor("rows", [1, 2 * H], F32))
    bvbf = att.enter_context(nc.sbuf_tensor("bvbf", [1, H], BF16))
    xnsT = att.enter_context(nc.sbuf_tensor("xnsT", [128, 16], F32))
    xnsTb = att.enter_context(nc.sbuf_tensor("xnsTb", [128, 16], BF16))
    VSTb = att.enter_context(nc.sbuf_tensor("VSTb", [128, 16], BF16))
    QT = att.enter_context(nc.sbuf_tensor("QT", [128, NH * NM], BF16))
    ctxT = att.enter_context(nc.sbuf_tensor("ctxT", [128, 16 * 206], BF16))
    qacc = att.enter_context(nc.sbuf_tensor("qacc", [128, 4], F32))
    qtiles = []
    for _mi, _ap in ((0, w1s), (1, w3s)):
        for _ht in range(43):
            _r0 = _ht * 128
            qtiles.append((_mi, _ap, _r0, min(128, HID - _r0), 256))
    for _dt in range(16):
        qtiles.append((2, w2s, _dt * 128, 128, 688))
    q1tiles = []
    for _mi, _ap in ((0, w1s), (1, w3s)):
        for _ht in range(43):
            _r0 = _ht * 128
            q1tiles.append((_mi, _ap, _r0, min(128, HID - _r0), 0, 256))
    for _dt in range(16):
        q1tiles.append((2, w2s, _dt * 128, 128, 0, 344))
        q1tiles.append((2, w2s, _dt * 128, 128, 344, 344))
    q_state = {}

    q1_lptr = [0]

    def emit_q1_loads(kmax):
        n = len(q1tiles)
        j0 = q1_lptr[0]
        j1 = min(n, j0 + kmax)
        if j0 >= j1:
            return
        q1_lptr[0] = j1

        def qp1(g, j0=j0, j1=j1):
            if j0 > 0:
                g.wait_ge(sem["sQa"], 16 * j0)
            for j in range(j0, j1):
                mi, ap_, r0, nr, c0, ncol = q1tiles[j]
                r = j % 8
                if j >= 8:
                    g.wait_ge(sem["sP"], 2 * j - 13)
                inc(g.dma_start(out=qtl[r][0:nr, 0:ncol],
                                in_=ap_[r0:r0 + nr, c0:c0 + ncol]),
                    "sQa", 16)
        G_(qp1)

    q1_ptr = [0]

    def emit_q1_reduces(kmax):
        n = len(q1tiles)
        i0 = q1_ptr[0]
        i1 = min(n, i0 + kmax)
        if i0 >= i1:
            return
        q1_ptr[0] = i1

        def qred(v, i0=i0, i1=i1):
            if i0 == 0:
                v.memset(qacc[:], 0.0)
            v._eng.wait_ge(sem["sQa"], 16 * i1)
            for i in range(i0, i1):
                mi, ap_, r0, nr, c0, ncol = q1tiles[i]
                r = i % 8
                v.reduce_sum(qacc[0:nr, 3:4], qtl[r][0:nr, 0:ncol],
                             axis=AXX, apply_absolute_value=True)
                v.tensor_tensor(qacc[0:nr, mi:mi + 1],
                                qacc[0:nr, mi:mi + 1],
                                qacc[0:nr, 3:4], ALU.add)
        VP_(qred)

    def emit_q1_chain():
        emit_q1_loads(len(q1tiles))
        emit_q1_reduces(len(q1tiles))

        def qtot(t, pm=cnt["sP"]):
            w("T", t, "sP", pm)
            bank_wait(t, 7)
            for mi in range(3):
                m = t.matmul(PB[7][0:1, mi:mi + 1], qacc[:, mi:mi + 1],
                             ones32[0:128, 0:1], start=True, stop=True)
            inc(m, "sT")
        T_(qtot)
        qtm = cnt["sT"]

        def qrowv(v, qtm=qtm):
            w("V", v, "sT", qtm)
            v.memset(qrow[0:1, 0:4], 0.0)
            v.tensor_scalar(qrow[0:1, 0:1], PB[7][0:1, 0:1], 1.0 / W1CNT,
                            None, ALU.mult)
            v.tensor_scalar(qrow[0:1, 1:2], PB[7][0:1, 1:2], 1.0 / W1CNT,
                            None, ALU.mult)
            v.tensor_scalar(qrow[0:1, 2:3], PB[7][0:1, 2:3], 1.0 / W2CNT,
                            None, ALU.mult)
            bankg[7].append(("sV", cnt["sV"]))
        V_(qrowv)
        qrm = cnt["sV"]

        def qar(g, qrm=qrm):
            w("G", g, "sV", qrm)
            w("G", g, "sGc", cnt["sGc"])
            inc(g.dma_start(out=ab_in[:, :], in_=qrow[0:1, 0:4]), "sGc", 16)
            g.wait_ge(sem["sGc"], cnt["sGc"])
            inc(g.collective_compute(
                "AllReduce", ALU.add, replica_groups=[list(range(8))],
                ins=[ab_in.opt()], outs=[ab_out.opt()]), "sC")
            g.wait_ge(sem["sC"], cnt["sC"])
            inc(g.dma_start(out=qrow[0:1, 0:4], in_=ab_out[:, :]), "sGc", 16)
        G_(qar)
        arm = cnt["sGc"]

        def qsbt(t, arm=arm):
            w("T", t, "sGc", arm)
            bank_wait(t, 7)
            inc(t.matmul(PB[7][0:128, 0:4], ones32[0:1, 0:128],
                         qrow[0:1, 0:4], start=True, stop=True), "sT")
        T_(qsbt)
        qsb_m = cnt["sT"]

        def qsbv(v, qsb_m=qsb_m):
            w("V", v, "sT", qsb_m)
            v.tensor_copy(sc_b[:, 0:3], PB[7][0:128, 0:3])
            bankg[7].append(("sV", cnt["sV"]))
            v.tensor_scalar(sc_b[:, 3:6], sc_b[:, 0:3], 0.5, None, ALU.mult)
            v.tensor_scalar(sc_b[:, 6:9], sc_b[:, 0:3], -0.5, None, ALU.mult)
            v.tensor_tensor(sc_b[:, 9:10], sc_b[:, 2:3], sc_b[:, 2:3],
                            ALU.mult)
            v.tensor_scalar(sc_b[:, 10:13], sc_b[:, 0:3], -1.0, None, ALU.mult)
        V_(qsbv)
        q_state["scb_mark"] = cnt["sV"]

    def emit_quant_pass2():
        qtl2 = [qt2, qt3]
        scb_mark = q_state["scb_mark"]
        pool_marks = {}
        trm_prev = [0]
        wprev = [0]
        tidx = [0]
        passes = []
        for _mi, _ap, _sh in ((0, w1s, t1sh), (1, w3s, t3sh)):
            passes.append((_mi, _ap, _sh, 0, 22, 0, 2816))
            passes.append((_mi, _ap, _sh, 22, 21, 2816, 2688))
        passes.append((2, w2s, t2sh, 0, 8, 0, 1024))
        passes.append((2, w2s, t2sh, 8, 8, 1024, 1024))
        for (mi, ap_, sh_ap, k0, nt, c0, cw) in passes:
            if mi < 2:
                def qmz(a, wp=wprev[0]):
                    if wp:
                        w("A", a, "sW", wp)
                    a.memzero(asm[:, 0:5632])
                A_(qmz)
            for k in range(nt):
                i = tidx[0]
                tidx[0] += 1
                r = i % 2
                sq_ = "sQa" if r == 0 else "sQb"
                if mi < 2:
                    r0 = (k0 + k) * 128
                    nr, ncol = min(128, HID - r0), 256
                else:
                    r0, nr, ncol = (k0 + k) * 128, 128, 688

                def qld2(s, r=r, ap_=ap_, r0=r0, nr=nr, ncol=ncol, sq_=sq_,
                         pm=pool_marks.get(i - 2, 0)):
                    if pm:
                        w("S", s, "sV", pm)
                    inc(s.dma_start(out=qtl2[r][0:nr, 0:ncol],
                                    in_=ap_[r0:r0 + nr, :]), sq_, 16)
                S_(qld2)
                ldm = cnt[sq_]

                def qq(v, r=r, mi=mi, nr=nr, ncol=ncol, ldm=ldm, sq_=sq_,
                       trm=trm_prev[0], scb=scb_mark):
                    w("V", v, sq_, ldm)
                    w("V", v, "sV", scb)
                    if trm:
                        w("V", v, "sT", trm)
                    v.tensor_single_scalar(qa[0:nr, 0:ncol],
                                           qtl2[r][0:nr, 0:ncol],
                                           sc_b[0:nr, 3 + mi:4 + mi],
                                           ALU.is_gt)
                    v.scalar_tensor_tensor(tq[0:nr, 0:ncol],
                                           qtl2[r][0:nr, 0:ncol],
                                           sc_b[0:nr, 6 + mi:7 + mi],
                                           qa[0:nr, 0:ncol],
                                           ALU.is_lt, ALU.subtract)
                V_(qq)
                pool_marks[i] = cnt["sV"]

                if mi < 2:
                    def qtr(t, nr=nr, pm=pool_marks[i]):
                        w("T", t, "sV", pm)
                        for db in range(2):
                            bank_wait(t, db)
                            m = t.transpose(PB[db][0:128, 0:nr],
                                            tq[0:nr, db * 128:(db + 1) * 128],
                                            id32[0:nr, 0:nr])
                        inc(m, "sT")
                    T_(qtr)
                    trm_prev[0] = cnt["sT"]

                    def qte(a, k=k, nr=nr, trm=cnt["sT"]):
                        w("A", a, "sT", trm)
                        for db in range(2):
                            m = a.activation(
                                asm[:, db * 2816 + k * 128:
                                    db * 2816 + k * 128 + nr],
                                PB[db][0:128, 0:nr], AF.Copy)
                        ia = inc(m, "sA")
                        bankg[0].append(("sA", ia))
                        bankg[1].append(("sA", ia))
                    A_(qte)
                else:
                    a_base2 = cnt["sA"]
                    t_base2 = cnt["sT"]

                    def q2tr(t, pm=pool_marks[i], a_base2=a_base2):
                        w("T", t, "sV", pm)
                        for hb in range(6):
                            cntk = min(128, 688 - hb * 128)
                            if hb >= 2:
                                w("T", t, "sA", a_base2 + hb - 1)
                            else:
                                bank_wait(t, hb % 2)
                            inc(t.transpose(
                                PB[hb % 2][0:cntk, 0:128],
                                tq[0:128, hb * 128:hb * 128 + cntk],
                                id32[0:128, 0:128]), "sT")
                    T_(q2tr)
                    trm_prev[0] = cnt["sT"]

                    def q2te(a, k=k, t_base2=t_base2, wp=wprev[0]):
                        if k == 0 and wp:
                            w("A", a, "sW", wp)
                        for hb in range(6):
                            cntk = min(128, 688 - hb * 128)
                            w("A", a, "sT", t_base2 + hb + 1)
                            inc(a.activation(
                                asm[0:cntk, hb * 1024 + k * 128:
                                    hb * 1024 + (k + 1) * 128],
                                PB[hb % 2][0:cntk, 0:128], AF.Copy), "sA")
                        bankg[0].append(("sA", cnt["sA"]))
                        bankg[1].append(("sA", cnt["sA"]))
                    A_(q2te)
            pam = cnt["sA"]

            def qwr(g, mi=mi, sh_ap=sh_ap, c0=c0, cw=cw, pam=pam):
                w("G", g, "sA", pam)
                w("G", g, "sW")
                if mi < 2:
                    for db in range(2):
                        inc(g.dma_start(
                            out=sh_ap[db * 128:(db + 1) * 128, c0:c0 + cw],
                            in_=asm[:, db * 2816:db * 2816 + cw]), "sW", 16)
                else:
                    for hb in range(6):
                        cntk = min(128, 688 - hb * 128)
                        inc(g.dma_start(
                            out=sh_ap[hb * 128:hb * 128 + cntk,
                                      c0:c0 + 1024],
                            in_=asm[0:cntk, hb * 1024:(hb + 1) * 1024]),
                            "sW", 16)
            G_(qwr)
            wprev[0] = cnt["sW"]

        def qag(g):
            w("G", g, "sW")
            for (si, so) in ((t1sh, t1T), (t3sh, t3T), (t2sh, t2T)):
                inc(g.collective_compute(
                    "AllGather", ALU.bypass, replica_groups=[list(range(8))],
                    ins=[si.opt()], outs=[so.opt()]), "sC")
        G_(qag)
        q_state["ag_done"] = cnt["sC"]
        q_state["p2_end"] = [("sA", cnt["sA"]), ("sP", cnt["sP"]),
                             ("sT", cnt["sT"]), ("sW", cnt["sW"]),
                             ("sQa", cnt["sQa"]), ("sQb", cnt["sQb"])]

    def p0(s):
        for (dst, src) in ((id32[:], id32in[:, :]), (ones32[:], ones32in[:, :]),
                           (onesb[:], onesbin[:, :]), (onesr[:], onesrbin[:, :]),
                           (mask_sb[:], maskin[:, :])):
            inc(s.dma_start(out=dst, in_=src), "sLc", 16)
        for c in range(16):
            inc(s.dma_start(out=bqs[:, c:c + 1], in_=bq[c * 128:(c + 1) * 128]),
                "sLc", 16)
            inc(s.dma_start(out=bkc[:, c:c + 1], in_=bk[c * 128:(c + 1) * 128]),
                "sLc", 16)
        inc(s.dma_start(out=rows[0:1, 0:H], in_=bv[:]), "sLc", 16)
    S_(p0)
    p0_mark = cnt["sLc"]

    def p0g(g):
        inc(g.dma_start(out=bvbf[0:1, :], in_=bv[:]), "sGc", 16)
    G_(p0g)
    bvbf_mark = cnt["sGc"]

    def p0v(v):
        w("V", v, "sLc", p0_mark)
        v.memset(epsc[:], EPS)
        v.tensor_scalar(bqs[:], bqs[:], RSQ, None, ALU.mult)
        inc(v.tensor_scalar(rows[0:1, H:2 * H], rows[0:1, 0:H],
                            float(S_FULL), None, ALU.mult), "sV")
    V_(p0v)
    sbv_mark = cnt["sV"]

    p2x = ExitStack()
    xnT = p2x.enter_context(nc.sbuf_tensor("xnT", [128, 16 * T], BF16,
                                           is_transpose_dest=True))
    qwstk = ExitStack()
    wqb4 = qwstk.enter_context(nc.sbuf_tensor("wqb4", [128, 2 * 8192], BF16))
    qstk = ExitStack()
    qt2 = qstk.enter_context(nc.sbuf_tensor("qt2", [128, 688], F32))
    qt3 = qstk.enter_context(nc.sbuf_tensor("qt3", [128, 688], F32))
    qa = qstk.enter_context(nc.sbuf_tensor("qa", [128, 688], F32))
    tq = qstk.enter_context(nc.sbuf_tensor("tq", [128, 688], F32))
    asm = qstk.enter_context(nc.sbuf_tensor("asm", [128, 6144], BF16))
    qt8 = qstk.enter_context(nc.sbuf_tensor("qt8", [128, 8 * 344], F32))
    qtl = [qt8[:, _r * 344:(_r + 1) * 344] for _r in range(8)]

    wqm = []

    def wq_load(g, qh, sl, twait=0):
        if twait:
            w("G", g, "sT", twait)
        inc(g.dma_start(
            out=wqb4[:, sl * 8192:(sl + 1) * 8192].rearrange(
                "p (kc n) -> p kc n", n=512),
            in_=wq[:, qh * 512:(qh + 1) * 512].rearrange(
                "(kc p) n -> p kc n", p=128)), "sQw", 16)
        wqm.append(cnt["sQw"])

    def wq01(g):
        wq_load(g, 0, 0)
        g.wait_ge(sem["sQw"], 16)
        wq_load(g, 1, 1)
    G_(wq01)

    with (
        nc.sbuf_tensor("xt0", [128, H], F32) as xt0,
        nc.sbuf_tensor("xt1", [128, H], F32) as xt1,
        nc.sbuf_tensor("xnb0", [128, H], BF16) as xnb0,
        nc.sbuf_tensor("xnb1", [128, H], BF16) as xnb1,
    ):
        xt = [xt0, xt1]
        xnb = [xnb0, xnb1]
        xread = [[], []]
        xnread = [[], []]
        for ti in range(17):
            r = ti % 2
            row0 = ti * 128
            emit_q1_loads(7)

            sx = "sXa" if r == 0 else "sXb"

            def p1s(s, r=r, row0=row0, rd=tuple(xread[r]), sx=sx):
                for (sn, sv) in rd:
                    s.wait_ge(sem[sn], sv)
                inc(s.dma_start(out=xt[r][:], in_=x_sh[row0:row0 + 128, :]),
                    sx, 16)
            S_(p1s)
            ld = cnt[sx]

            def pre(v, ld=ld, sx=sx):
                w("V", v, sx, ld)
            ln_full(xt[r][:], H, pre_v=pre)

            def p1v2(v, r=r, rd=tuple(xnread[r])):
                for (sn, sv) in rd:
                    w("V", v, sn, sv)
                inc(v.tensor_scalar(xnb[r][:], xt[r][:], stats[:, 2:3],
                                    stats[:, 5:6], ALU.subtract, ALU.mult), "sV")
            V_(p1v2)
            xnm = cnt["sV"]
            xread[r] = [("sV", xnm)]

            def p1t(t, r=r, ti=ti, xnm=xnm):
                w("T", t, "sV", xnm)
                rc = 128 if ti < 16 else 2
                for n in range(4):
                    if ti == 0:
                        bank_wait(t, n)
                    m = t.matmul(PB[n][0:1, 0:512], onesb[0:rc, 0:1],
                                 xnb[r][0:rc, n * 512:(n + 1) * 512],
                                 start=(ti == 0), stop=(ti == 16))
                inc(m, "sT")
            T_(p1t)
            tsm = cnt["sT"]

            sn_ = "sNa" if r == 0 else "sNb"

            def p1g(g, r=r, row0=row0, xnm=xnm, tsm=tsm, sn_=sn_):
                w("G", g, "sV", xnm)
                w("G", g, "sT", tsm)
                inc(g.dma_start(out=xn_scr[row0:row0 + 128, :], in_=xnb[r][:]),
                    sn_, 16)
            G_(p1g)
            xnread[r] = [(sn_, cnt[sn_]), ("sT", tsm)]
            emit_q1_reduces(7)
        xn_written = [("sNa", cnt["sNa"]), ("sNb", cnt["sNb"])]
        p1_marks = [("sV", cnt["sV"]), ("sT", cnt["sT"]), ("sA", cnt["sA"]),
                    ("sNa", cnt["sNa"]), ("sNb", cnt["sNb"])]

        xnsum_t = cnt["sT"]

        def p1xv(v, xnsum_t=xnsum_t):
            w("V", v, "sT", xnsum_t)
            for n in range(4):
                m = v.tensor_copy(rows[0:1, n * 512:(n + 1) * 512],
                                  PB[n][0:1, 0:512])
            iv = inc(m, "sV")
            for b in range(4):
                bankg[b].append(("sV", iv))
        V_(p1xv)
        xsrow = cnt["sV"]

        def p1xt(t, xsrow=xsrow):
            w("T", t, "sV", xsrow)
            bank_wait(t, 6)
            for c in range(16):
                m = t.transpose(PB[6][0:128, c:c + 1],
                                rows[0:1, c * 128:(c + 1) * 128],
                                id32[0:1, 0:1])
            inc(m, "sT")
        T_(p1xt)
        xst = cnt["sT"]

        def p1xv2(v, xst=xst):
            w("V", v, "sT", xst)
            iv = inc(v.tensor_copy(xnsT[:], PB[6][0:128, 0:16]), "sV")
            bankg[6].append(("sV", iv))
        V_(p1xv2)
        xstv = cnt["sV"]

        def p1g2(g, xstv=xstv):
            w("G", g, "sV", xstv)
            w("G", g, "sGc", cnt["sGc"])
            inc(g.dma_start(out=xs_in[:, :], in_=xnsT[:]), "sGc", 16)
            g.wait_ge(sem["sGc"], cnt["sGc"])
            inc(g.collective_compute(
                "AllReduce", ALU.add,
                replica_groups=[[0, 1], [2, 3], [4, 5], [6, 7]],
                ins=[xs_in.opt()], outs=[xs_out.opt()]), "sC")
            g.wait_ge(sem["sC"], cnt["sC"])
            inc(g.dma_start(out=xnsT[:], in_=xs_out[:, :]), "sGc", 16)
        G_(p1g2)
        ar1 = cnt["sGc"]

        def p1v3(v, ar1=ar1):
            w("V", v, "sGc", ar1)
            inc(v.tensor_copy(xnsTb[:], xnsT[:]), "sV")
        V_(p1v3)
        xnsb_mark = cnt["sV"]

    emit_q1_chain()

    if True:
        def p2s(s):
            for (sn, sv) in xn_written:
                s.wait_ge(sem[sn], sv)
            for (sn, sv) in p1_marks:
                w("S", s, sn, sv)
            w("S", s, "sLc", cnt["sLc"])
            for dc in range(16):
                inc(s.dma_start(out=xnT[:, dc * T:(dc + 1) * T],
                                in_=xn_scr[:, dc * 128:(dc + 1) * 128],
                                transpose=True), "sLc", 16)
        S_(p2s)
        xnT_mark = cnt["sLc"]

        a_base_q = cnt["sA"]
        t_base_q = cnt["sT"]

        def qpt(t):
            w("T", t, "sLc", xnT_mark)
            for hc in range(16):
                qh, b = hc // 4, hc % 4
                sl = qh % 2
                if hc % 4 == 0:
                    w("T", t, "sQw", wqm[qh] if qh < len(wqm)
                      else 16 * (qh + 1))
                if hc >= 4:
                    w("T", t, "sA", a_base_q + hc - 3)
                else:
                    bank_wait(t, b)
                for kc in range(16):
                    m = t.matmul(
                        PB[b][0:128, 0:NM],
                        wqb4[:, sl * 8192 + kc * 512 + (hc % 4) * 128:
                             sl * 8192 + kc * 512 + (hc % 4) * 128 + 128],
                        xnT[:, kc * T:kc * T + 2041:10],
                        start=(kc == 0), stop=(kc == 15))
                inc(m, "sT")
        T_(qpt)

        def qpa(a):
            for hc in range(16):
                b = hc % 4
                w("A", a, "sT", t_base_q + hc + 1)
                ia = inc(a.activation(QT[:, hc * NM:(hc + 1) * NM],
                                      PB[b][0:128, 0:NM], AF.Identity,
                                      bias=bqs[:, hc:hc + 1], scale=RSQ),
                         "sA")
                bankg[b].append(("sA", ia))
        A_(qpa)
        q_done = cnt["sA"]

        def wq23(g):
            wq_load(g, 2, 0, t_base_q + 4)
            g.wait_ge(sem["sQw"], 48)
            wq_load(g, 3, 1, t_base_q + 8)
        G_(wq23)

        emit_quant_pass2()
        ag_done = q_state["ag_done"]
        qstk.close()
        qwstk.close()

        def qbarV(v):
            w("V", v, "sP")
            w("V", v, "sW")
        V_(qbarV)

        at2 = ExitStack()
        ctx_g = at2.enter_context(nc.sbuf_tensor("ctx_g", [GR, 1024], F32))
        Xs = at2.enter_context(nc.sbuf_tensor("Xs", [GR, 2 * GK], F32))
        Es = Xs
        ETs = at2.enter_context(nc.sbuf_tensor("ETs", [128, 2 * 4 * GR], BF16))
        nm16 = at2.enter_context(nc.sbuf_tensor("nm16", [GR, 16], F32))
        tmpc = scr2k
        VS_B = at2.enter_context(nc.sbuf_tensor("VS_B", [64, H], F32))

        vs_done = 0
        for hp in range(2):
            hps = ExitStack()
            KTh = hps.enter_context(nc.sbuf_tensor(f"KT{hp}",
                                                   [128, 8 * KEYN], BF16))
            with nc.sbuf_tensor(f"wkb{hp}", [128, 16 * 1024], BF16) as wkb:
                def pkg(g, hp=hp, pa=cnt["sA"], pt=cnt["sT"]):
                    w("G", g, "sA", pa)
                    w("G", g, "sT", pt)
                    w("G", g, "sW")
                    w("G", g, "sQa")
                    w("G", g, "sQb")
                    w("G", g, "sQw")
                    w("G", g, "sP")
                    w("G", g, "sGc", cnt["sGc"])
                    for kc in range(16):
                        inc(g.dma_start(
                            out=wkb[:, kc * 1024:(kc + 1) * 1024],
                            in_=wk[kc * 128:(kc + 1) * 128,
                                   hp * 1024:(hp + 1) * 1024]), "sGc", 16)
                G_(pkg)
                wk_mark = cnt["sGc"]

                def kmm(t, i, b, hp=hp):
                    hl, ns = divmod(i, 5)
                    for kc in range(16):
                        m = t.matmul(
                            PB[b][0:128, 0:GK],
                            wkb[:, kc * 1024 + hl * 128:
                                kc * 1024 + (hl + 1) * 128],
                            xnT[:, kc * T + ns * GK:kc * T + (ns + 1) * GK],
                            start=(kc == 0), stop=(kc == 15))
                    inc(m, "sT")

                def kev(a, i, b, hp=hp):
                    hl, ns = divmod(i, 5)
                    inc(a.activation(
                        KTh[:, hl * KEYN + ns * GK:hl * KEYN + (ns + 1) * GK],
                        PB[b][0:128, 0:GK], AF.Identity,
                        bias=bkc[:, hp * 8 + hl:hp * 8 + hl + 1]), "sA")

                mm_evac_loop(40, [0, 1, 2, 3], kmm, kev, "A",
                             t_waits=[("sGc", wk_mark), ("sLc", xnT_mark)])

            wvb = hps.enter_context(nc.sbuf_tensor(f"wvb{hp}",
                                                   [128, 16 * 1024], BF16))

            def pvg(g, hp=hp, pa=cnt["sA"], pt=cnt["sT"]):
                w("G", g, "sA", pa)
                w("G", g, "sT", pt)
                w("G", g, "sGc", cnt["sGc"])
                for kc in range(16):
                    inc(g.dma_start(
                        out=wvb[:, kc * 1024:(kc + 1) * 1024],
                        in_=wv[kc * 128:(kc + 1) * 128,
                               hp * 1024:(hp + 1) * 1024]), "sGc", 16)
            G_(pvg)
            wv_mark = cnt["sGc"]

            def vsmm(t, n, b, hp=hp):
                for kc in range(16):
                    m = t.matmul(PB[b][0:1, 0:512], xnsTb[:, kc:kc + 1],
                                 wvb[:, kc * 1024 + n * 512:
                                     kc * 1024 + (n + 1) * 512],
                                 start=(kc == 0), stop=(kc == 15))
                inc(m, "sT")

            def vsev(v, n, b, hp=hp):
                off = H + hp * 1024 + n * 512
                inc(v.scalar_tensor_tensor(
                    rows[0:1, off:off + 512], rows[0:1, off:off + 512], 1.0,
                    PB[b][0:1, 0:512], ALU.mult, ALU.add), "sV")

            mm_evac_loop(2, [4, 5], vsmm, vsev, "V",
                         t_waits=[("sGc", wv_mark), ("sV", xnsb_mark),
                                  ("sV", sbv_mark)])
            vs_done = cnt["sV"]

            def vbmm(t, n, b, hp=hp):
                off = H + hp * 1024 + n * 512
                inc(t.matmul(PB[b][0:64, 0:512], ones32[0:1, 0:64],
                             rows[0:1, off:off + 512], start=True, stop=True),
                    "sT")

            def vbev(v, n, b, hp=hp):
                off = hp * 1024 + n * 512
                inc(v.tensor_copy(VS_B[:, off:off + 512], PB[b][0:64, 0:512]),
                    "sV")

            mm_evac_loop(2, [4, 5], vbmm, vbev, "V",
                         t_waits=[("sV", vs_done)])

            Vg = hps.enter_context(nc.sbuf_tensor(f"Vg{hp}", [128, 4096], BF16))
            ctxg_read = []
            for g_i in range(NG):
                def vgmm(t, i, b, g_i=g_i, hp=hp):
                    c, n = divmod(i, 2)
                    cntk = VCHUNK[c]
                    key0 = GK * g_i + 128 * c
                    for kc in range(16):
                        t.matmul(
                            PB[b][0:cntk, 0:512],
                            xnT[:, kc * T + key0:kc * T + key0 + cntk],
                            wvb[:, kc * 1024 + n * 512:kc * 1024 + (n + 1) * 512],
                            start=(kc == 0), stop=False)
                    m = t.matmul(
                        PB[b][0:cntk, 0:512], onesr[0:1, 0:cntk],
                        bvbf[0:1, hp * 1024 + n * 512:hp * 1024 + (n + 1) * 512],
                        start=False, stop=True)
                    inc(m, "sT")

                def vgev(v, i, b, g_i=g_i, hp=hp):
                    c, n = divmod(i, 2)
                    cntk = VCHUNK[c]
                    inc(v.tensor_copy(
                        Vg[0:cntk, c * 1024 + n * 512:c * 1024 + (n + 1) * 512],
                        PB[b][0:cntk, 0:512]), "sV")

                tw = [("sA", q_done), ("sGc", bvbf_mark)]
                mm_evac_loop(8, [4, 5], vgmm, vgev, "V", t_waits=tw)
                vg_mark = cnt["sV"]

                for hb in range(4):
                    hl2 = (2 * hb, 2 * hb + 1)

                    def hsc(t, hl2=hl2, g_i=g_i, hp=hp, vg_mark=vg_mark):
                        w("T", t, "sV", vg_mark)
                        for j, hl in enumerate(hl2):
                            h = hp * 8 + hl
                            bank_wait(t, j)
                            inc(t.matmul(
                                PB[j][0:GR, 0:GK],
                                QT[:, h * NM + GR * g_i:h * NM + GR * (g_i + 1)],
                                KTh[:, hl * KEYN + GK * g_i:
                                    hl * KEYN + GK * (g_i + 1)],
                                start=True, stop=True), "sT")
                    T_(hsc)
                    scm = cnt["sT"]

                    def hmx(v, scm=scm):
                        w("V", v, "sT", scm)
                        for j in range(2):
                            v.reduce_max(nm16[:, 8 * j:8 * j + 1],
                                         PB[j][0:GR, 0:GK], axis=AXX,
                                         negate=True)
                            m = v.tensor_scalar_min(nm16[:, 8 * j + 1:8 * j + 2],
                                                    nm16[:, 8 * j:8 * j + 1],
                                                    0.0)
                        inc(m, "sV")
                    V_(hmx)
                    mxm = cnt["sV"]

                    def hex_(a, mxm=mxm):
                        w("A", a, "sV", mxm)
                        for j in range(2):
                            a.activation(nm16[:, 8 * j + 2:8 * j + 3],
                                         nm16[:, 8 * j + 1:8 * j + 2], AF.Exp)
                            m = a.activation(Xs[0:GR, j * GK:(j + 1) * GK],
                                             PB[j][0:GR, 0:GK], AF.Exp,
                                             bias=nm16[0:GR, 8 * j + 1:8 * j + 2])
                        ia = inc(m, "sA")
                        bankg[0].append(("sA", ia))
                        bankg[1].append(("sA", ia))
                    A_(hex_)
                    xm = cnt["sA"]

                    def he(v, g_i=g_i, xm=xm):
                        w("V", v, "sA", xm)
                        for j in range(2):
                            m = v.scalar_tensor_tensor(
                                Es[0:GR, j * GK:(j + 1) * GK],
                                Xs[0:GR, j * GK:(j + 1) * GK],
                                nm16[0:GR, 8 * j + 2:8 * j + 3],
                                mask_sb[0:GR, GK * g_i:GK * (g_i + 1)],
                                ALU.subtract, ALU.mult)
                        inc(m, "sV")
                    V_(he)
                    em_ = cnt["sV"]

                    def htr(t, em_=em_):
                        w("T", t, "sV", em_)
                        bank_wait(t, 6)
                        for j in range(2):
                            for c in range(4):
                                cntk = VCHUNK[c]
                                m = t.transpose(
                                    PB[6][0:cntk, (j * 4 + c) * GR:
                                          (j * 4 + c + 1) * GR],
                                    Es[0:GR, j * GK + c * 128:
                                       j * GK + c * 128 + cntk],
                                    id32[0:GR, 0:GR])
                        inc(m, "sT")
                    T_(htr)
                    trm = cnt["sT"]

                    def het(v, trm=trm):
                        w("V", v, "sT", trm)
                        for j in range(2):
                            for c in range(4):
                                cntk = VCHUNK[c]
                                m = v.tensor_copy(
                                    ETs[0:cntk, (j * 4 + c) * GR:
                                        (j * 4 + c + 1) * GR],
                                    PB[6][0:cntk, (j * 4 + c) * GR:
                                          (j * 4 + c + 1) * GR])
                        iv = inc(m, "sV")
                        bankg[6].append(("sV", iv))
                    V_(het)
                    etm = cnt["sV"]

                    def hco(t, hl2=hl2, etm=etm):
                        w("T", t, "sV", etm)
                        for j in range(2):
                            bank_wait(t, 2 + j)
                            bank_wait(t, 4 + j)
                            hl = hl2[j]
                            for c in range(4):
                                cntk = VCHUNK[c]
                                t.matmul(
                                    PB[2 + j][0:GR, 0:128],
                                    ETs[0:cntk, (j * 4 + c) * GR:
                                        (j * 4 + c + 1) * GR],
                                    Vg[0:cntk, c * 1024 + hl * 128:
                                       c * 1024 + (hl + 1) * 128],
                                    start=(c == 0), stop=(c == 3))
                                me_ = t.matmul(
                                    PB[4 + j][0:GR, 0:1],
                                    ETs[0:cntk, (j * 4 + c) * GR:
                                        (j * 4 + c + 1) * GR],
                                    onesb[0:cntk, 0:1],
                                    start=(c == 0), stop=(c == 3))
                        inc(me_, "sT")
                    T_(hco)
                    com = cnt["sT"]

                    def hz(v, hl2=hl2, hp=hp, com=com):
                        w("V", v, "sT", com)
                        for j in range(2):
                            hl = hl2[j]
                            h = hp * 8 + hl
                            o = 8 * j
                            v.scalar_tensor_tensor(
                                nm16[:, o + 3:o + 4], nm16[:, o + 2:o + 3],
                                float(S_FULL), PB[4 + j][0:GR, 0:1],
                                ALU.mult, ALU.add)
                            v.reciprocal(nm16[:, o + 4:o + 5],
                                         nm16[:, o + 3:o + 4])
                            v.tensor_tensor(nm16[:, o + 5:o + 6],
                                            nm16[:, o + 2:o + 3],
                                            nm16[:, o + 4:o + 5], ALU.mult)
                            v.tensor_scalar(tmpc[0:GR, j * 128:(j + 1) * 128],
                                            PB[2 + j][0:GR, 0:128],
                                            nm16[:, o + 4:o + 5], None,
                                            ALU.mult)
                            m = v.scalar_tensor_tensor(
                                ctx_g[0:GR, hl * 128:(hl + 1) * 128],
                                VS_B[0:GR, h * 128:(h + 1) * 128],
                                nm16[0:GR, o + 5:o + 6],
                                tmpc[0:GR, j * 128:(j + 1) * 128],
                                ALU.mult, ALU.add)
                        iv = inc(m, "sV")
                        for b in (2, 3, 4, 5):
                            bankg[b].append(("sV", iv))
                    V_(hz)
                ctx_mark = cnt["sV"]

                a_base = cnt["sV"]
                t_base7 = cnt["sT"]

                def ctp(t, ctx_mark=ctx_mark, a_base=a_base):
                    w("T", t, "sV", ctx_mark)
                    bank_wait(t, 6)
                    bank_wait(t, 7)
                    for i in range(8):
                        if i >= 2:
                            w("T", t, "sV", a_base + i - 1)
                        inc(t.transpose(PB[6 + (i % 2)][0:128, 0:GR],
                                        ctx_g[0:GR, i * 128:(i + 1) * 128],
                                        id32[0:GR, 0:GR]), "sT")
                T_(ctp)

                def cap(v, g_i=g_i, hp=hp, t_base7=t_base7):
                    for i in range(8):
                        w("V", v, "sT", t_base7 + i + 1)
                        h = hp * 8 + i
                        inc(v.tensor_copy(
                            ctxT[:, h * 206 + g_i * GR:h * 206 + (g_i + 1) * GR],
                            PB[6 + (i % 2)][0:128, 0:GR]), "sV")
                V_(cap)
                bankg[6].append(("sV", cnt["sV"]))
                bankg[7].append(("sV", cnt["sV"]))
                ctxg_read = [("sT", cnt["sT"])]
            hps.close()
        at2.close()

        def vstt(t, vs_done=vs_done):
            w("T", t, "sV", vs_done)
            bank_wait(t, 6)
            for c in range(16):
                m = t.transpose(PB[6][0:128, c:c + 1],
                                rows[0:1, H + c * 128:H + (c + 1) * 128],
                                id32[0:1, 0:1])
            inc(m, "sT")
        T_(vstt)
        vst_m = cnt["sT"]

        def vstv(v, vst_m=vst_m):
            w("V", v, "sT", vst_m)
            iv = inc(v.tensor_scalar(VSTb[:], PB[6][0:128, 0:16],
                                     1.0 / float(S_FULL), None, ALU.mult), "sV")
            bankg[6].append(("sV", iv))
            inc(v.tensor_copy(ctxT[:, 205:16 * 206:206], VSTb[:]), "sV")
        V_(vstv)
        ctxT_done = cnt["sV"]
    p2x.close()

    p2_done = [("sA", cnt["sA"]), ("sV", cnt["sV"]), ("sT", cnt["sT"])]

    bcastB = att.enter_context(nc.sbuf_tensor("bcastB", [128, H], F32))
    anl_sb = att.enter_context(nc.sbuf_tensor("anl_sb", [128, 2 * H], F32))
    x1m_sb = att.enter_context(nc.sbuf_tensor("x1m_sb", [128, 2 * H], F32))
    x1nm_sb = att.enter_context(nc.sbuf_tensor("x1nm_sb", [128, 2 * H], BF16))
    xm_sb = att.enter_context(nc.sbuf_tensor("xm_sb", [128, 2 * H], F32))
    MC = [(0, 128), (1, 77)]

    def p2bar(v, p2d=tuple(p2_done)):
        for (sn, sv) in p2d:
            w("V", v, sn, sv)
    V_(p2bar)

    def pbo(s):
        s.wait_ge(sem["sT"], cnt["sT"])
        w("S", s, "sLc", cnt["sLc"])
        inc(s.dma_start(out=rows[0:1, 0:H], in_=bo[:]), "sLc", 16)
    S_(pbo)
    bo_ld = cnt["sLc"]

    def bomm(t, i, b):
        inc(t.matmul(PB[b][0:128, 0:512], ones32[0:1, 0:128],
                     rows[0:1, i * 512:(i + 1) * 512], start=True, stop=True),
            "sT")

    def boev(v, i, b):
        inc(v.tensor_copy(bcastB[:, i * 512:(i + 1) * 512],
                          PB[b][0:128, 0:512]), "sV")

    mm_evac_loop(4, [6, 7], bomm, boev, "V", t_waits=[("sLc", bo_ld)])
    bo_b = cnt["sV"]

    def pxm(s, pt=cnt["sT"], p2d=tuple(p2_done)):
        w("S", s, "sT", pt)
        for (sn, sv) in p2d:
            w("S", s, sn, sv)
        w("S", s, "sLc", cnt["sLc"])
        inc(s.dma_start(out=xm_sb[:, 0:H], in_=x_sh[0:1280:10, :]), "sLc", 16)
        inc(s.dma_start(out=xm_sb[0:77, H:2 * H], in_=x_sh[1280:2050:10, :]),
            "sLc", 16)
    S_(pxm)
    xm_mark = cnt["sLc"]

    with nc.sbuf_tensor("wob", [128, 2 * H], BF16) as wob:
        wo_marks = []
        t_base_o = cnt["sT"]

        def pog(g):
            w("G", g, "sT", t_base_o)
            for kc in range(16):
                swo = "sWoa" if kc % 2 == 0 else "sWob"
                if kc >= 2:
                    g.wait_ge(sem["sT"], t_base_o + kc - 1)
                inc(g.dma_start(out=wob[:, (kc % 2) * H:(kc % 2 + 1) * H],
                                in_=wo[kc * 128:(kc + 1) * 128, :]), swo, 16)
                wo_marks.append((swo, cnt[swo]))
        G_(pog)

        def pot(t):
            w("T", t, "sV", ctxT_done)
            for b in range(8):
                bank_wait(t, b)
            for kc in range(16):
                w("T", t, wo_marks[kc][0], wo_marks[kc][1])
                for mc, cw in MC:
                    rcnt = cw + (1 if mc == 1 else 0)
                    for n in range(4):
                        m = t.matmul(
                            PB[mc * 4 + n][0:rcnt, 0:512],
                            ctxT[:, kc * 206 + mc * 128:
                                 kc * 206 + mc * 128 + rcnt],
                            wob[:, (kc % 2) * H + n * 512:
                                (kc % 2) * H + (n + 1) * 512],
                            start=(kc == 0), stop=(kc == 15))
                inc(m, "sT")
        T_(pot)
        oproj_t = cnt["sT"]

        def poev(v):
            w("V", v, "sT", oproj_t)
            w("V", v, "sV", bo_b)
            for mc, cw in MC:
                rcnt = cw + (1 if mc == 1 else 0)
                for n in range(4):
                    m = v.tensor_tensor(
                        anl_sb[0:rcnt, mc * H + n * 512:mc * H + (n + 1) * 512],
                        PB[mc * 4 + n][0:rcnt, 0:512],
                        bcastB[0:rcnt, n * 512:(n + 1) * 512], ALU.add)
            iv = inc(m, "sV")
            for b in range(8):
                bankg[b].append(("sV", iv))
        V_(poev)

        for mc, cw in MC:
            rcnt = cw + (1 if mc == 1 else 0)
            ln_full(anl_sb[0:rcnt, mc * H:(mc + 1) * H], H, p=rcnt)

            def pnorm(v, mc=mc, rcnt=rcnt):
                inc(v.tensor_scalar(anl_sb[0:rcnt, mc * H:(mc + 1) * H],
                                    anl_sb[0:rcnt, mc * H:(mc + 1) * H],
                                    stats[0:rcnt, 2:3], stats[0:rcnt, 5:6],
                                    ALU.subtract, ALU.mult), "sV")
            V_(pnorm)
        anl_mark = cnt["sV"]

    def pgbase(g, anl_mark=anl_mark):
        w("G", g, "sV", anl_mark)
        w("G", g, "sGc", cnt["sGc"])
        inc(g.dma_start(out=rows[0:1, 0:H], in_=anl_sb[77:78, H:2 * H]),
            "sGc", 16)
    G_(pgbase)
    lnb_row = cnt["sGc"]

    def lbmm(t, i, b, lnb_row=lnb_row):
        if i == 0:
            w("T", t, "sGc", lnb_row)
        inc(t.matmul(PB[b][0:128, 0:512], ones32[0:1, 0:128],
                     rows[0:1, i * 512:(i + 1) * 512], start=True, stop=True),
            "sT")

    def lbev(v, i, b):
        inc(v.tensor_copy(bcastB[:, i * 512:(i + 1) * 512],
                          PB[b][0:128, 0:512]), "sV")

    mm_evac_loop(4, [6, 7], lbmm, lbev, "V", t_waits=[("sV", cnt["sV"])])
    lnbase_b = cnt["sV"]

    for mc, cw in MC:
        def pm1(v, mc=mc, cw=cw):
            w("V", v, "sLc", xm_mark)
            inc(v.tensor_tensor(x1m_sb[0:cw, mc * H:(mc + 1) * H],
                                anl_sb[0:cw, mc * H:(mc + 1) * H],
                                xm_sb[0:cw, mc * H:(mc + 1) * H], ALU.add),
                "sV")
        V_(pm1)
        ln_full(x1m_sb[0:cw, mc * H:(mc + 1) * H], H, p=cw)

        def pm2(v, mc=mc, cw=cw):
            inc(v.tensor_scalar(x1nm_sb[0:cw, mc * H:(mc + 1) * H],
                                x1m_sb[0:cw, mc * H:(mc + 1) * H],
                                stats[0:cw, 2:3], stats[0:cw, 5:6],
                                ALU.subtract, ALU.mult), "sV")
        V_(pm2)
    x1m_mark = cnt["sV"]

    with (
        nc.sbuf_tensor("bxt0", [128, H], F32) as bxt0,
        nc.sbuf_tensor("bxt1", [128, H], F32) as bxt1,
        nc.sbuf_tensor("bx10", [128, H], F32) as bx10,
        nc.sbuf_tensor("bx11", [128, H], F32) as bx11,
        nc.sbuf_tensor("bxn0", [128, H], BF16) as bxn0,
        nc.sbuf_tensor("bxn1", [128, H], BF16) as bxn1,
    ):
        bxt = [bxt0, bxt1]
        bx1 = [bx10, bx11]
        bxn = [bxn0, bxn1]
        bread = [[], []]
        bwrit = [[], []]
        for ti in range(17):
            r = ti % 2
            row0 = ti * 128

            sb_ = "sBa" if r == 0 else "sBb"

            def pbs(s, r=r, row0=row0, rd=tuple(bread[r]), sb_=sb_,
                    pt=cnt["sT"], p2d=tuple(p2_done)):
                if not rd:
                    w("S", s, "sT", pt)
                    for (sn, sv) in p2d:
                        w("S", s, sn, sv)
                for (sn, sv) in rd:
                    s.wait_ge(sem[sn], sv)
                inc(s.dma_start(out=bxt[r][:], in_=x_sh[row0:row0 + 128, :]),
                    sb_, 16)
            S_(pbs)
            ld = cnt[sb_]

            def pbv(v, r=r, ld=ld, rd=tuple(bwrit[r]), sb_=sb_):
                w("V", v, sb_, ld)
                w("V", v, "sV", lnbase_b)
                for (sn, sv) in rd:
                    w("V", v, sn, sv)
                inc(v.tensor_tensor(bx1[r][:], bxt[r][:], bcastB[:], ALU.add),
                    "sV")
            V_(pbv)
            bread[r] = [("sV", cnt["sV"])]
            ln_full(bx1[r][:], H)

            def pbv2(v, r=r):
                inc(v.tensor_scalar(bxn[r][:], bx1[r][:], stats[:, 2:3],
                                    stats[:, 5:6], ALU.subtract, ALU.mult),
                    "sV")
            V_(pbv2)
            vm = cnt["sV"]

            sn_ = "sNa" if r == 0 else "sNb"

            def pbg(g, r=r, row0=row0, vm=vm, sn_=sn_):
                w("G", g, "sV", vm)
                inc(g.dma_start(out=x1_scr[row0:row0 + 128, :], in_=bx1[r][:]),
                    sn_, 16)
                inc(g.dma_start(out=x1n_scr[row0:row0 + 128, :], in_=bxn[r][:]),
                    sn_, 16)
            G_(pbg)
            bwrit[r] = [(sn_, cnt[sn_])]
        base_written = [("sNa", cnt["sNa"]), ("sNb", cnt["sNb"])]

        def pmw(g):
            for (sn, sv) in base_written:
                g.wait_ge(sem[sn], sv)
            w("G", g, "sV", x1m_mark)
            inc(g.dma_start(out=x1_scr[0:1280:10, :], in_=x1m_sb[:, 0:H]),
                "sW", 16)
            inc(g.dma_start(out=x1_scr[1280:2050:10, :],
                            in_=x1m_sb[0:77, H:2 * H]), "sW", 16)
            inc(g.dma_start(out=x1n_scr[0:1280:10, :], in_=x1nm_sb[:, 0:H]),
                "sW", 16)
            inc(g.dma_start(out=x1n_scr[1280:2050:10, :],
                            in_=x1nm_sb[0:77, H:2 * H]), "sW", 16)
        G_(pmw)
    x1_all_written = [("sNa", cnt["sNa"]), ("sNb", cnt["sNb"]),
                      ("sW", cnt["sW"])]
    att_done = [("sV", cnt["sV"]), ("sT", cnt["sT"]), ("sA", cnt["sA"]),
                ("sLc", cnt["sLc"]), ("sGc", cnt["sGc"])]
    att.close()
    att_end = [(n, cnt[n]) for n in
               ("sV", "sT", "sA", "sLc", "sGc", "sW", "sNa", "sNb",
                "sQa", "sQb", "sWoa", "sWob", "sBa", "sBb")]


    SC1 = sc_b[:, 10:11]
    SC3 = sc_b[:, 11:12]
    SC2 = sc_b[:, 12:13]
    SC2SQ = sc_b[:, 9:10]
    out_marks = []
    gsub = [0]
    Y_TBS = [[(0, 1024)], [(0, 640), (640, 512)]]

    def t2b_copy(a):
        w("A", a, "sC", ag_done)
        for q in range(4):
            for hh in range(2):
                r0 = hh * 2752
                inc(a.dma_start(
                    out=t2B[q * HIDP + r0:q * HIDP + r0 + 2752, :],
                    in_=t2T[r0:r0 + 2752, q * 512:(q + 1) * 512]),
                    "sQw", 16)
    A_(t2b_copy)
    t2b_mark = cnt["sQw"]

    with nc.sbuf_tensor("gbuf", [128, NCH * 1152], BF16) as gbuf:
        g_read_t = [0]
        silu_marks = []
        hgrp_ctr = [0]
        prev_fim = [0]
        for half, (tok0, tw_half) in enumerate(FFN_HALVES):
            nts = (tw_half + 511) // 512
            ts_slices = [(k * 512, min(512, tw_half - k * 512))
                         for k in range(nts)]
            hstk = ExitStack()
            x1nT = hstk.enter_context(
                nc.sbuf_tensor(f"x1nT{half}", [128, 16 * tw_half], BF16,
                               is_transpose_dest=True))
            w1r = hstk.enter_context(
                nc.sbuf_tensor(f"w1r{half}", [128, 2 * 2048], BF16))
            w3r = hstk.enter_context(
                nc.sbuf_tensor(f"w3r{half}", [128, 2 * 2048], BF16))
            silu0 = hstk.enter_context(
                nc.sbuf_tensor(f"silu0{half}", [128, 512], BF16))
            silu1 = hstk.enter_context(
                nc.sbuf_tensor(f"silu1{half}", [128, 512], BF16))
            silub = [silu0, silu1]
            w13_guard = [0, 0]
            if True:
                def f0s(s, half=half, tok0=tok0, tw_half=tw_half,
                        pv_=cnt["sV"], pa_=cnt["sA"], poa_=cnt["sOa"],
                        pob_=cnt["sOb"]):
                    if half == 0:
                        for (sn, sv) in att_end:
                            s.wait_ge(sem[sn], sv)
                        for (sn, sv) in x1_all_written:
                            s.wait_ge(sem[sn], sv)
                    else:
                        s.wait_ge(sem["sT"], g_read_t[0])
                        w("S", s, "sV", pv_)
                        w("S", s, "sA", pa_)
                        w("S", s, "sOa", poa_)
                        w("S", s, "sOb", pob_)
                    w("S", s, "sLc", cnt["sLc"])
                    for dc in range(16):
                        inc(s.dma_start(
                            out=x1nT[:, dc * tw_half:(dc + 1) * tw_half],
                            in_=x1n_scr[tok0:tok0 + tw_half,
                                        dc * 128:(dc + 1) * 128],
                            transpose=True), "sLc", 16)
                S_(f0s)
                x1nT_m = cnt["sLc"]

                for ch in range(NCH):
                    r = ch % 2

                    sw13 = "sW13a" if r == 0 else "sW13b"

                    def fls(s, ch=ch, r=r, half=half,
                            wg=w13_guard[r], ag_done=ag_done, sw13=sw13):
                        if ch < 2 and half == 0:
                            s.wait_ge(sem["sC"], ag_done)
                        if wg:
                            s.wait_ge(sem["sT"], wg)
                        inc(s.dma_start(
                            out=w1r[:, r * 2048:(r + 1) * 2048].rearrange(
                                "p (dc h) -> p dc h", h=128),
                            in_=t1T[:, ch * 128:(ch + 1) * 128].rearrange(
                                "(dc p) h -> p dc h", p=128)), sw13, 16)
                        inc(s.dma_start(
                            out=w3r[:, r * 2048:(r + 1) * 2048].rearrange(
                                "p (dc h) -> p dc h", h=128),
                            in_=t3T[:, ch * 128:(ch + 1) * 128].rearrange(
                                "(dc p) h -> p dc h", p=128)), sw13, 16)
                    S_(fls)
                    wm_ch = cnt[sw13]

                    groups = [ts_slices[0:2]]
                    if len(ts_slices) > 2:
                        groups.append(ts_slices[2:])
                    for gi, grp in enumerate(groups):
                        base = (hgrp_ctr[0] % 2) * 4
                        hgrp_ctr[0] += 1

                        def fh(t, r=r, grp=tuple(grp), base=base,
                               x1nT=x1nT, tw_half=tw_half, wm_ch=wm_ch,
                               x1nT_m=x1nT_m, sw13=sw13):
                            w("T", t, sw13, wm_ch)
                            w("T", t, "sLc", x1nT_m)
                            for k in range(len(grp)):
                                bank_wait(t, base + k)
                                bank_wait(t, base + 2 + k)
                            for dc in range(16):
                                for k, (t0, tw_) in enumerate(grp):
                                    m3 = t.matmul(
                                        PB[base + k][0:128, 0:tw_],
                                        w1r[:, r * 2048 + dc * 128:
                                            r * 2048 + (dc + 1) * 128],
                                        x1nT[:, dc * tw_half + t0:
                                             dc * tw_half + t0 + tw_],
                                        start=(dc == 0), stop=(dc == 15))
                                for k, (t0, tw_) in enumerate(grp):
                                    m3 = t.matmul(
                                        PB[base + 2 + k][0:128, 0:tw_],
                                        w3r[:, r * 2048 + dc * 128:
                                            r * 2048 + (dc + 1) * 128],
                                        x1nT[:, dc * tw_half + t0:
                                             dc * tw_half + t0 + tw_],
                                        start=(dc == 0), stop=(dc == 15))
                            inc(m3, "sT")
                        T_(fh)
                        hm = cnt["sT"]

                        def fsi(a, grp=tuple(grp), base=base, hm=hm,
                                sg=(silu_marks[-1] if silu_marks else 0)):
                            w("A", a, "sT", hm)
                            if sg:
                                w("A", a, "sV", sg)
                            for k, (t0, tw_) in enumerate(grp):
                                m = a.activation(silub[k][0:128, 0:tw_],
                                                 PB[base + k][0:128, 0:tw_],
                                                 AF.Silu, scale=SC1)
                            inc(m, "sA")
                        A_(fsi)
                        am = cnt["sA"]

                        def fg(v, ch=ch, grp=tuple(grp), base=base, am=am,
                               tw_half=tw_half):
                            w("V", v, "sA", am)
                            for k, (t0, tw_) in enumerate(grp):
                                m = v.scalar_tensor_tensor(
                                    gbuf[:, ch * tw_half + t0:
                                         ch * tw_half + t0 + tw_],
                                    PB[base + 2 + k][0:128, 0:tw_], SC3,
                                    silub[k][0:128, 0:tw_],
                                    ALU.mult, ALU.mult)
                            iv = inc(m, "sV")
                            for k in range(len(grp)):
                                bankg[base + k].append(("sV", iv))
                                bankg[base + 2 + k].append(("sV", iv))
                        V_(fg)
                        silu_marks.append(cnt["sV"])
                    w13_guard[r] = cnt["sT"]
                g_done = cnt["sV"]
                hTdone = cnt["sT"]
            hstk.close()

            tbs = Y_TBS[half]
            nsub_max = tbs[0][1] // 128
            ystk = ExitStack()
            ysb = ystk.enter_context(
                nc.sbuf_tensor(f"ysb{half}", [128, nsub_max * H], F32))
            x1ta = ystk.enter_context(
                nc.sbuf_tensor(f"x1ta{half}", [128, H], F32))
            x1tb = ystk.enter_context(
                nc.sbuf_tensor(f"x1tb{half}", [128, H], F32))
            outa = ystk.enter_context(
                nc.sbuf_tensor(f"outa{half}", [128, H], F32))
            outc = ystk.enter_context(
                nc.sbuf_tensor(f"outc{half}", [128, H], F32))
            w2r = ystk.enter_context(
                nc.sbuf_tensor(f"w2r{half}", [128, 4 * 512], BF16))
            if True:
                x1t = [x1ta, x1tb]
                outb = [outa, outc]
                w2_guard = [0, 0, 0, 0]
                first_q = [True, True]
                W2SEMS = ["sW2a", "sW2b", "sW2c", "sW2d"]
                for tb0, tbw in tbs:
                    nsub = tbw // 128
                    ev_mark = 0
                    for q in range(4):
                        for ch in range(NCH):
                            r = ch % 2
                            sl = ch % 4

                            sw2 = W2SEMS[sl]
                            EMIT = S_ if r == 0 else G_
                            ec = "S" if r == 0 else "G"

                            def fws(s, ch=ch, sl=sl, q=q, wg=w2_guard[sl],
                                    fq=first_q[r], sw2=sw2, ec=ec,
                                    gd=g_done, ht=hTdone):
                                if fq:
                                    w(ec, s, "sV", gd)
                                    w(ec, s, "sT", ht)
                                    w(ec, s, "sQw", t2b_mark)
                                if wg:
                                    s.wait_ge(sem["sT"], wg)
                                inc(s.dma_start(
                                    out=w2r[:, sl * 512:(sl + 1) * 512],
                                    in_=t2B[(q * HIDP + ch * 128):
                                            (q * HIDP + (ch + 1) * 128), :]),
                                    sw2, 16)
                            EMIT(fws)
                            first_q[r] = False
                            w2m = cnt[sw2]

                            def fy(t, ch=ch, sl=sl, tb0=tb0, nsub=nsub,
                                   w2m=w2m, g_done=g_done, tw_half=tw_half,
                                   sw2=sw2):
                                w("T", t, sw2, w2m)
                                if ch == 0:
                                    w("T", t, "sV", g_done)
                                    for s_ in range(nsub):
                                        bank_wait(t, s_)
                                for s_ in range(nsub):
                                    m = t.matmul(
                                        PB[s_][0:128, 0:512],
                                        gbuf[:, ch * tw_half + tb0 +
                                             s_ * 128:
                                             ch * tw_half + tb0 +
                                             s_ * 128 + 128],
                                        w2r[:, sl * 512:(sl + 1) * 512],
                                        start=(ch == 0),
                                        stop=(ch == NCH - 1))
                                inc(m, "sT")
                            T_(fy)
                            w2_guard[sl] = cnt["sT"]
                        ym = cnt["sT"]

                        def fyev(a, q=q, nsub=nsub, ym=ym, pf=prev_fim[0]):
                            w("A", a, "sT", ym)
                            if pf:
                                w("A", a, "sV", pf)
                            for s_ in range(nsub):
                                m = a.activation(
                                    ysb[:, s_ * H + q * 512:
                                        s_ * H + (q + 1) * 512],
                                    PB[s_][0:128, 0:512], AF.Copy)
                            ia = inc(m, "sA")
                            for s_ in range(nsub):
                                bankg[s_].append(("sA", ia))
                        A_(fyev)
                        ev_mark = cnt["sA"]

                    for s_ in range(nsub):
                        k = gsub[0]
                        gsub[0] += 1
                        r = k % 2
                        tokg = tok0 + tb0 + s_ * 128

                        sf = "sFa" if r == 0 else "sFb"

                        def fx1(s, r=r, tokg=tokg, sf=sf,
                                om=(out_marks[k - 2] if k >= 2 else None)):
                            if om is not None:
                                s.wait_ge(sem[om[0]], om[1])
                            inc(s.dma_start(out=x1t[r][:],
                                            in_=x1_scr[tokg:tokg + 128, :]),
                                sf, 16)
                        S_(fx1)
                        x1lm = cnt[sf]

                        yslice = ysb[:, s_ * H:(s_ + 1) * H]

                        def fst(v, yslice=yslice, em=ev_mark):
                            w("V", v, "sA", em)
                            v.tensor_scalar(scr2k[:], yslice, 1.0, None,
                                            ALU.mult, ALU.add,
                                            accum_out=stats[:, 0:1])
                            v.scalar_tensor_tensor(scr2k[:], yslice, SC2SQ,
                                                   yslice, ALU.mult, ALU.mult,
                                                   accum_out=stats[:, 1:2])
                            v.tensor_scalar(stats[:, 2:3], stats[:, 0:1],
                                            INV_H, None, ALU.mult)
                            v.tensor_tensor(stats[:, 2:3], stats[:, 2:3], SC2,
                                            ALU.mult)
                            v.tensor_scalar(stats[:, 3:4], stats[:, 1:2],
                                            INV_H, None, ALU.mult)
                            v.tensor_tensor(stats[:, 6:7], stats[:, 2:3],
                                            stats[:, 2:3], ALU.mult)
                            inc(v.tensor_tensor(stats[:, 3:4], stats[:, 3:4],
                                                stats[:, 6:7], ALU.subtract),
                                "sV")
                        V_(fst)
                        svm = cnt["sV"]

                        def fsq(a, svm=svm):
                            w("A", a, "sV", svm)
                            inc(a.activation(stats[:, 4:5], stats[:, 3:4],
                                             AF.Sqrt, bias=epsc[:, 0:1]),
                                "sA")
                        A_(fsq)
                        sqm = cnt["sA"]

                        def ffin(v, r=r, yslice=yslice, sqm=sqm, x1lm=x1lm,
                                 sf=sf):
                            w("V", v, "sA", sqm)
                            v.reciprocal(stats[:, 5:6], stats[:, 4:5])
                            v.tensor_tensor(stats[:, 6:7], SC2, stats[:, 5:6],
                                            ALU.mult)
                            v.tensor_tensor(stats[:, 7:8], stats[:, 2:3],
                                            stats[:, 5:6], ALU.mult)
                            v.tensor_scalar(scr2k[:], yslice, stats[:, 6:7],
                                            stats[:, 7:8], ALU.mult,
                                            ALU.subtract)
                            w("V", v, sf, x1lm)
                            inc(v.tensor_tensor(outb[r][:], scr2k[:],
                                                x1t[r][:], ALU.add), "sV")
                        V_(ffin)
                        fim = cnt["sV"]

                        so = "sOa" if r == 0 else "sOb"

                        def fwr(g, r=r, tokg=tokg, fim=fim, so=so):
                            w("G", g, "sV", fim)
                            inc(g.dma_start(out=out_sh[tokg:tokg + 128, :],
                                            in_=outb[r][:]), so, 16)
                        G_(fwr)
                        out_marks.append((so, cnt[so]))
                    prev_fim[0] = cnt["sV"]
                g_read_t[0] = cnt["sT"]
            ystk.close()

    foa, fob = cnt["sOa"], cnt["sOb"]

    def fend(g):
        g.wait_ge(sem["sOa"], foa)
        g.wait_ge(sem["sOb"], fob)
    G_(fend)

    es.close()
    return nc


_BF = ml_dtypes.bfloat16


def _build_core_inputs(inputs):
    xb = np.asarray(inputs["L0_attn_hidden_states"], dtype=np.float32)
    base = {
        "wq": np.ascontiguousarray(inputs["L0_attn_wq"], dtype=np.float32),
        "wk": np.ascontiguousarray(inputs["L0_attn_wk"], dtype=np.float32),
        "wv": np.ascontiguousarray(inputs["L0_attn_wv"], dtype=np.float32),
        "wo": np.ascontiguousarray(inputs["L0_attn_wo"], dtype=np.float32),
        "bq": np.ascontiguousarray(inputs["L0_attn_bq"], dtype=np.float32),
        "bk": np.ascontiguousarray(inputs["L0_attn_bk"], dtype=np.float32),
        "bv": np.ascontiguousarray(inputs["L0_attn_bv"], dtype=np.float32),
        "bo": np.ascontiguousarray(inputs["L0_attn_bo"], dtype=np.float32),
        "id32in": np.eye(128, dtype=np.float32),
        "ones32in": np.ones((128, 128), dtype=np.float32),
        "onesbin": np.ones((128, 1), dtype=_BF),
        "onesrbin": np.ones((1, 128), dtype=_BF),
    }
    w1 = np.asarray(inputs["L0_ffn_w1"], dtype=np.float32)
    w3 = np.asarray(inputs["L0_ffn_w3"], dtype=np.float32)
    w2 = np.asarray(inputs["L0_ffn_w2"], dtype=np.float32)

    masks = []
    for half in range(2):
        m = np.zeros((GR, KEYN), dtype=_BF)
        off = 0 if half == 0 else 2050
        for g in range(NG):
            for i in range(GR):
                q_local = 10 * (GR * g + i)
                q_glob = off + q_local
                wlen = min(10, S_FULL - q_glob)
                c0 = g * GK + (q_local - GK * g)
                m[i, c0:c0 + wlen] = 1.0
        masks.append(m)

    per_core = []
    for c in range(8):
        b, half = divmod(c, 2)
        xs = np.zeros((T, H), dtype=np.float32)
        if half == 0:
            xs[0:2050] = xb[b, 0:2050]
        else:
            xs[0:2046] = xb[b, 2050:4096]
        d = dict(base)
        d["x_sh"] = xs
        d["maskin"] = masks[half]
        d["w1s"] = np.ascontiguousarray(w1[:, 256 * c:256 * (c + 1)])
        d["w3s"] = np.ascontiguousarray(w3[:, 256 * c:256 * (c + 1)])
        w2c = np.zeros((H, 688), dtype=np.float32)
        lo = 688 * c
        hi = min(688 * (c + 1), HID)
        w2c[:, 0:hi - lo] = w2[:, lo:hi]
        d["w2s"] = w2c
        per_core.append(d)
    return per_core


_NC = None


def kernel(**inputs):
    global _NC
    per_core = _build_core_inputs(inputs)
    if _NC is None:
        _NC = build_program()
    res = run_bass_kernel_spmd(_NC, per_core, core_ids=list(range(8)))
    out = np.empty((4, S_FULL, H), dtype=np.float32)
    for c in range(8):
        b, half = divmod(c, 2)
        o = res.results[c]["out_sh"]
        if half == 0:
            out[b, 0:2050] = o[0:2050]
        else:
            out[b, 2050:4096] = o[0:2046]
    return out


if __name__ == "__main__":
    nc = build_program()
    n_inst = sum(1 for _ in nc.all_instructions())
    print("build ok, instructions:", n_inst)


# version 22
# speedup vs baseline: 1.0202x; 1.0202x over previous
import math
import sys
from contextlib import ExitStack

sys.path.insert(0, "/opt/trn_rl_repo")
import numpy as np
import ml_dtypes
import concourse.bass as bass
import concourse.mybir as mybir
from concourse.bass_utils import run_bass_kernel_spmd

F32 = mybir.dt.float32
BF16 = mybir.dt.bfloat16
AF = mybir.ActivationFunctionType
ALU = mybir.AluOpType
AXX = mybir.AxisListType.X

H = 2048
NH = 16
S_FULL = 4096
HID = 5461
HIDP = 5504
NCH = 43
T = 2176
KEYN = 2050
NM = 205
NG = 5
GR = 41
GK = 410
RSQ = 1.0 / math.sqrt(128.0)
EPS = 1e-5
INV_H = 1.0 / float(H)
W1CNT = float(HID * H)
W2CNT = float(H * HID)
VCHUNK = [128, 128, 128, 26]
FFN_HALVES = [(0, 1024), (1024, 1152)]


def build_program():
    nc = bass.Bass("TRN2", target_bir_lowering=False)

    x_sh = nc.dram_tensor("x_sh", [T, H], F32, kind="ExternalInput").ap()
    wq = nc.dram_tensor("wq", [H, H], F32, kind="ExternalInput").ap()
    wk = nc.dram_tensor("wk", [H, H], F32, kind="ExternalInput").ap()
    wv = nc.dram_tensor("wv", [H, H], F32, kind="ExternalInput").ap()
    wo = nc.dram_tensor("wo", [H, H], F32, kind="ExternalInput").ap()
    bq = nc.dram_tensor("bq", [H], F32, kind="ExternalInput").ap()
    bk = nc.dram_tensor("bk", [H], F32, kind="ExternalInput").ap()
    bv = nc.dram_tensor("bv", [H], F32, kind="ExternalInput").ap()
    bo = nc.dram_tensor("bo", [H], F32, kind="ExternalInput").ap()
    w1s = nc.dram_tensor("w1s", [HID, 256], F32, kind="ExternalInput").ap()
    w3s = nc.dram_tensor("w3s", [HID, 256], F32, kind="ExternalInput").ap()
    w2s = nc.dram_tensor("w2s", [H, 688], F32, kind="ExternalInput").ap()
    maskin = nc.dram_tensor("maskin", [GR, KEYN], BF16, kind="ExternalInput").ap()
    id32in = nc.dram_tensor("id32in", [128, 128], F32, kind="ExternalInput").ap()
    ones32in = nc.dram_tensor("ones32in", [128, 128], F32,
                              kind="ExternalInput").ap()
    onesbin = nc.dram_tensor("onesbin", [128, 1], BF16, kind="ExternalInput").ap()
    onesrbin = nc.dram_tensor("onesrbin", [1, 128], BF16,
                              kind="ExternalInput").ap()

    out_sh = nc.dram_tensor("out_sh", [T, H], F32, kind="ExternalOutput").ap()

    xn_scr = nc.dram_tensor("xn_scr", [T, H], BF16, kind="Internal").ap()
    x1_scr = nc.dram_tensor("x1_scr", [T, H], F32, kind="Internal").ap()
    x1n_scr = nc.dram_tensor("x1n_scr", [T, H], BF16, kind="Internal").ap()
    xs_in = nc.dram_tensor("xs_in", [128, 16], F32, kind="Internal").ap()
    xs_out = nc.dram_tensor("xs_out", [128, 16], F32, kind="Internal").ap()
    ab_in = nc.dram_tensor("ab_in", [1, 4], F32, kind="Internal").ap()
    ab_out = nc.dram_tensor("ab_out", [1, 4], F32, kind="Internal").ap()
    t1sh = nc.dram_tensor("t1sh", [256, HIDP], BF16, kind="Internal").ap()
    t3sh = nc.dram_tensor("t3sh", [256, HIDP], BF16, kind="Internal").ap()
    t2sh = nc.dram_tensor("t2sh", [688, H], BF16, kind="Internal").ap()
    t1T = nc.dram_tensor("t1T", [H, HIDP], BF16, kind="Internal").ap()
    t3T = nc.dram_tensor("t3T", [H, HIDP], BF16, kind="Internal").ap()
    t2T = nc.dram_tensor("t2T", [HIDP, H], BF16, kind="Internal").ap()
    t2B = nc.dram_tensor("t2B", [4 * HIDP, 512], BF16, kind="Internal").ap()

    es = ExitStack()
    sem_names = ["sT", "sV", "sA", "sC", "sW", "sLc", "sGc", "sP", "sQw",
                 "sXa", "sXb", "sBa", "sBb", "sW13a", "sW13b", "sW2a", "sW2b",
                 "sW2c", "sW2d",
                 "sWoa", "sWob", "sQa", "sQb", "sFa", "sFb", "sOa", "sOb",
                 "sNa", "sNb"]
    sem = {n: es.enter_context(nc.semaphore(n)) for n in sem_names}
    cnt = {n: 0 for n in sem_names}
    cur = {e: {} for e in "TVASG"}

    def w(ec, eng, s, val=None):
        v = cnt[s] if val is None else val
        if v <= 0:
            return
        if cur[ec].get(s, -1) < v:
            eng.wait_ge(sem[s], v)
            cur[ec][s] = v

    def inc(instr, s, amt=1):
        if s in ("sV", "sA"):
            return cnt[s]
        instr.then_inc(sem[s], amt)
        cnt[s] += amt
        return cnt[s]

    def dma_settled(eng, s):
        if cnt[s]:
            eng.wait_ge(sem[s], cnt[s])

    PB = [es.enter_context(nc.psum_tensor(f"pb{i}", [128, 512], F32))
          for i in range(8)]
    bankg = {i: [] for i in range(8)}

    def bank_wait(t, i):
        for (s, v) in bankg[i]:
            w("T", t, s, v)
        bankg[i] = []

    id32 = es.enter_context(nc.sbuf_tensor("id32", [128, 128], F32))
    ones32 = es.enter_context(nc.sbuf_tensor("ones32", [128, 128], F32))
    onesb = es.enter_context(nc.sbuf_tensor("onesb", [128, 1], BF16))
    onesr = es.enter_context(nc.sbuf_tensor("onesr", [1, 128], BF16))
    sc_b = es.enter_context(nc.sbuf_tensor("sc_b", [128, 14], F32))
    stats = es.enter_context(nc.sbuf_tensor("stats", [128, 8], F32))
    scr2k = es.enter_context(nc.sbuf_tensor("scr2k", [128, H], F32))
    qrow = es.enter_context(nc.sbuf_tensor("qrow", [1, 8], F32))
    epsc = es.enter_context(nc.sbuf_tensor("epsc", [128, 1], F32))

    blk = es.enter_context(nc.Block())

    _CHAIN_OPS = {
        "tensor_scalar", "scalar_tensor_tensor", "tensor_tensor",
        "tensor_copy", "reduce_max", "reduce_sum", "tensor_reduce",
        "tensor_single_scalar", "tensor_scalar_min", "tensor_scalar_mul",
        "tensor_scalar_add", "tensor_add", "tensor_sub", "tensor_mul",
        "reciprocal", "memset", "memzero", "activation",
    }

    class ChainEng:

        def __init__(self, eng, sname):
            self._eng = eng
            self._sname = sname

        def __getattr__(self, name):
            fn = getattr(self._eng, name)
            if name not in _CHAIN_OPS:
                return fn
            sname = self._sname

            def wrapped(*a, **k):
                self._eng.wait_ge(sem[sname], cnt[sname])
                instr = fn(*a, **k)
                instr.then_inc(sem[sname], 1)
                cnt[sname] += 1
                return instr
            return wrapped

    def S_(fn):
        blk.sync(fn)

    def G_(fn):
        blk.gpsimd(fn)

    def P_(fn):
        blk.gpsimd(lambda g: fn(ChainEng(g, "sP")))

    def VP_(fn):
        blk.vector(lambda v: fn(ChainEng(v, "sP")))

    def T_(fn):
        blk.tensor(fn)

    def V_(fn):
        blk.vector(lambda v: fn(ChainEng(v, "sV")))

    def A_(fn):
        blk.scalar(lambda a: fn(ChainEng(a, "sA")))

    ESEC = {"V": V_, "A": A_}
    ESEM = {"V": "sV", "A": "sA"}

    def mm_evac_loop(n_iter, banks, emit_mm, emit_evac, evac_eng="A",
                     t_waits=None):
        esem = ESEM[evac_eng]
        e_base = cnt[esem]
        t_base = cnt["sT"]
        nb = len(banks)

        def tp(t):
            if t_waits:
                for (s, v) in t_waits:
                    w("T", t, s, v)
            for i in range(n_iter):
                b = banks[i % nb]
                if i >= nb:
                    w("T", t, esem, e_base + (i - nb) + 1)
                else:
                    bank_wait(t, b)
                emit_mm(t, i, b)
        T_(tp)
        assert cnt["sT"] == t_base + n_iter, "emit_mm must inc sT once"

        def ep(e):
            for i in range(n_iter):
                b = banks[i % nb]
                w(evac_eng, e, "sT", t_base + i + 1)
                emit_evac(e, i, b)
        ESEC[evac_eng](ep)
        assert cnt[esem] == e_base + n_iter, "emit_evac must inc once"
        for i in range(max(0, n_iter - nb), n_iter):
            bankg[banks[i % nb]].append((esem, e_base + i + 1))
        return cnt[esem]

    def ln_v1(v, src_ap, n, p=128):
        v.tensor_scalar(scr2k[0:p, 0:n], src_ap, 1.0, None, ALU.mult, ALU.add,
                        accum_out=stats[0:p, 0:1])
        v.scalar_tensor_tensor(scr2k[0:p, 0:n], src_ap, 1.0, src_ap,
                               ALU.mult, ALU.mult, accum_out=stats[0:p, 1:2])
        invn = 1.0 / float(n)
        v.tensor_scalar(stats[0:p, 2:3], stats[0:p, 0:1], invn, None, ALU.mult)
        v.tensor_scalar(stats[0:p, 3:4], stats[0:p, 1:2], invn, None, ALU.mult)
        v.tensor_tensor(stats[0:p, 6:7], stats[0:p, 2:3], stats[0:p, 2:3],
                        ALU.mult)
        return v.tensor_tensor(stats[0:p, 3:4], stats[0:p, 3:4],
                               stats[0:p, 6:7], ALU.subtract)

    def ln_full(src_ap, n, pre_v=None, p=128):
        def v1(v):
            if pre_v is not None:
                pre_v(v)
            inc(ln_v1(v, src_ap, n, p=p), "sV")
        V_(v1)
        stm = cnt["sV"]

        def a1(a, stm=stm):
            w("A", a, "sV", stm)
            inc(a.activation(stats[0:p, 4:5], stats[0:p, 3:4], AF.Sqrt,
                             bias=epsc[0:p, 0:1]), "sA")
        A_(a1)
        sqm = cnt["sA"]

        def v2(v, sqm=sqm):
            w("V", v, "sA", sqm)
            v.reciprocal(stats[0:p, 5:6], stats[0:p, 4:5])
        V_(v2)

    att = ExitStack()
    mask_sb = att.enter_context(nc.sbuf_tensor("mask_sb", [GR, KEYN], BF16))
    bqs = att.enter_context(nc.sbuf_tensor("bqs", [128, 16], F32))
    bkc = att.enter_context(nc.sbuf_tensor("bkc", [128, 16], F32))
    rows = att.enter_context(nc.sbuf_tensor("rows", [1, 2 * H], F32))
    bvbf = att.enter_context(nc.sbuf_tensor("bvbf", [1, H], BF16))
    xnsT = att.enter_context(nc.sbuf_tensor("xnsT", [128, 16], F32))
    xnsTb = att.enter_context(nc.sbuf_tensor("xnsTb", [128, 16], BF16))
    VSTb = att.enter_context(nc.sbuf_tensor("VSTb", [128, 16], BF16))
    QT = att.enter_context(nc.sbuf_tensor("QT", [128, NH * NM], BF16))
    ctxT = att.enter_context(nc.sbuf_tensor("ctxT", [128, 16 * 206], BF16))
    qacc = att.enter_context(nc.sbuf_tensor("qacc", [128, 4], F32))
    qtiles = []
    for _mi, _ap in ((0, w1s), (1, w3s)):
        for _ht in range(43):
            _r0 = _ht * 128
            qtiles.append((_mi, _ap, _r0, min(128, HID - _r0), 256))
    for _dt in range(16):
        qtiles.append((2, w2s, _dt * 128, 128, 688))
    q1tiles = []
    for _mi, _ap in ((0, w1s), (1, w3s)):
        for _ht in range(43):
            _r0 = _ht * 128
            q1tiles.append((_mi, _ap, _r0, min(128, HID - _r0), 0, 256))
    for _dt in range(16):
        q1tiles.append((2, w2s, _dt * 128, 128, 0, 344))
        q1tiles.append((2, w2s, _dt * 128, 128, 344, 344))
    q_state = {}

    q1_lptr = [0]

    def emit_q1_loads(kmax):
        n = len(q1tiles)
        j0 = q1_lptr[0]
        j1 = min(n, j0 + kmax)
        if j0 >= j1:
            return
        q1_lptr[0] = j1

        def qp1(g, j0=j0, j1=j1):
            if j0 > 0:
                g.wait_ge(sem["sQa"], 16 * j0)
            for j in range(j0, j1):
                mi, ap_, r0, nr, c0, ncol = q1tiles[j]
                r = j % 8
                if j >= 8:
                    g.wait_ge(sem["sP"], 2 * j - 13)
                inc(g.dma_start(out=qtl[r][0:nr, 0:ncol],
                                in_=ap_[r0:r0 + nr, c0:c0 + ncol]),
                    "sQa", 16)
        G_(qp1)

    q1_ptr = [0]

    def emit_q1_reduces(kmax):
        n = len(q1tiles)
        i0 = q1_ptr[0]
        i1 = min(n, i0 + kmax)
        if i0 >= i1:
            return
        q1_ptr[0] = i1

        def qred(v, i0=i0, i1=i1):
            if i0 == 0:
                v.memset(qacc[:], 0.0)
            v._eng.wait_ge(sem["sQa"], 16 * i1)
            for i in range(i0, i1):
                mi, ap_, r0, nr, c0, ncol = q1tiles[i]
                r = i % 8
                v.reduce_sum(qacc[0:nr, 3:4], qtl[r][0:nr, 0:ncol],
                             axis=AXX, apply_absolute_value=True)
                v.tensor_tensor(qacc[0:nr, mi:mi + 1],
                                qacc[0:nr, mi:mi + 1],
                                qacc[0:nr, 3:4], ALU.add)
        VP_(qred)

    def emit_q1_chain():
        emit_q1_loads(len(q1tiles))
        emit_q1_reduces(len(q1tiles))

        def qtot(t, pm=cnt["sP"]):
            w("T", t, "sP", pm)
            bank_wait(t, 7)
            for mi in range(3):
                m = t.matmul(PB[7][0:1, mi:mi + 1], qacc[:, mi:mi + 1],
                             ones32[0:128, 0:1], start=True, stop=True)
            inc(m, "sT")
        T_(qtot)
        qtm = cnt["sT"]

        def qrowv(v, qtm=qtm):
            w("V", v, "sT", qtm)
            v.memset(qrow[0:1, 0:4], 0.0)
            v.tensor_scalar(qrow[0:1, 0:1], PB[7][0:1, 0:1], 1.0 / W1CNT,
                            None, ALU.mult)
            v.tensor_scalar(qrow[0:1, 1:2], PB[7][0:1, 1:2], 1.0 / W1CNT,
                            None, ALU.mult)
            v.tensor_scalar(qrow[0:1, 2:3], PB[7][0:1, 2:3], 1.0 / W2CNT,
                            None, ALU.mult)
            bankg[7].append(("sV", cnt["sV"]))
        V_(qrowv)
        qrm = cnt["sV"]

        def qar(g, qrm=qrm):
            w("G", g, "sV", qrm)
            w("G", g, "sGc", cnt["sGc"])
            inc(g.dma_start(out=ab_in[:, :], in_=qrow[0:1, 0:4]), "sGc", 16)
            g.wait_ge(sem["sGc"], cnt["sGc"])
            inc(g.collective_compute(
                "AllReduce", ALU.add, replica_groups=[list(range(8))],
                ins=[ab_in.opt()], outs=[ab_out.opt()]), "sC")
            g.wait_ge(sem["sC"], cnt["sC"])
            inc(g.dma_start(out=qrow[0:1, 0:4], in_=ab_out[:, :]), "sGc", 16)
        G_(qar)
        arm = cnt["sGc"]

        def qsbt(t, arm=arm):
            w("T", t, "sGc", arm)
            bank_wait(t, 7)
            inc(t.matmul(PB[7][0:128, 0:4], ones32[0:1, 0:128],
                         qrow[0:1, 0:4], start=True, stop=True), "sT")
        T_(qsbt)
        qsb_m = cnt["sT"]

        def qsbv(v, qsb_m=qsb_m):
            w("V", v, "sT", qsb_m)
            v.tensor_copy(sc_b[:, 0:3], PB[7][0:128, 0:3])
            bankg[7].append(("sV", cnt["sV"]))
            v.tensor_scalar(sc_b[:, 3:6], sc_b[:, 0:3], 0.5, None, ALU.mult)
            v.tensor_scalar(sc_b[:, 6:9], sc_b[:, 0:3], -0.5, None, ALU.mult)
            v.tensor_tensor(sc_b[:, 9:10], sc_b[:, 2:3], sc_b[:, 2:3],
                            ALU.mult)
            v.tensor_scalar(sc_b[:, 10:13], sc_b[:, 0:3], -1.0, None, ALU.mult)
        V_(qsbv)
        q_state["scb_mark"] = cnt["sV"]

    def emit_quant_pass2():
        qtl2 = [qt2, qt3]
        scb_mark = q_state["scb_mark"]
        pool_marks = {}
        trm_prev = [0]
        wprev = [0]
        tidx = [0]
        passes = []
        for _mi, _ap, _sh in ((0, w1s, t1sh), (1, w3s, t3sh)):
            passes.append((_mi, _ap, _sh, 0, 22, 0, 2816))
            passes.append((_mi, _ap, _sh, 22, 21, 2816, 2688))
        passes.append((2, w2s, t2sh, 0, 8, 0, 1024))
        passes.append((2, w2s, t2sh, 8, 8, 1024, 1024))
        for (mi, ap_, sh_ap, k0, nt, c0, cw) in passes:
            if mi < 2:
                def qmz(a, wp=wprev[0]):
                    if wp:
                        w("A", a, "sW", wp)
                    a.memzero(asm[:, 0:5632])
                A_(qmz)
            for k in range(nt):
                i = tidx[0]
                tidx[0] += 1
                r = i % 2
                sq_ = "sQa" if r == 0 else "sQb"
                if mi < 2:
                    r0 = (k0 + k) * 128
                    nr, ncol = min(128, HID - r0), 256
                else:
                    r0, nr, ncol = (k0 + k) * 128, 128, 688

                def qld2(s, r=r, ap_=ap_, r0=r0, nr=nr, ncol=ncol, sq_=sq_,
                         pm=pool_marks.get(i - 2, 0), first=(i == 0),
                         base=cnt["sQa"]):
                    if first:
                        s.wait_ge(sem["sQa"], base)
                    if pm:
                        w("S", s, "sV", pm)
                    dma_settled(s, sq_)
                    inc(s.dma_start(out=qtl2[r][0:nr, 0:ncol],
                                    in_=ap_[r0:r0 + nr, :]), sq_, 16)
                S_(qld2)
                ldm = cnt[sq_]

                def qq(v, r=r, mi=mi, nr=nr, ncol=ncol, ldm=ldm, sq_=sq_,
                       trm=trm_prev[0], scb=scb_mark):
                    w("V", v, sq_, ldm)
                    w("V", v, "sV", scb)
                    if trm:
                        w("V", v, "sT", trm)
                    v.tensor_single_scalar(qa[0:nr, 0:ncol],
                                           qtl2[r][0:nr, 0:ncol],
                                           sc_b[0:nr, 3 + mi:4 + mi],
                                           ALU.is_gt)
                    v.scalar_tensor_tensor(tq[0:nr, 0:ncol],
                                           qtl2[r][0:nr, 0:ncol],
                                           sc_b[0:nr, 6 + mi:7 + mi],
                                           qa[0:nr, 0:ncol],
                                           ALU.is_lt, ALU.subtract)
                V_(qq)
                pool_marks[i] = cnt["sV"]

                if mi < 2:
                    def qtr(t, nr=nr, pm=pool_marks[i]):
                        w("T", t, "sV", pm)
                        for db in range(2):
                            bank_wait(t, db)
                            m = t.transpose(PB[db][0:128, 0:nr],
                                            tq[0:nr, db * 128:(db + 1) * 128],
                                            id32[0:nr, 0:nr])
                        inc(m, "sT")
                    T_(qtr)
                    trm_prev[0] = cnt["sT"]

                    def qte(a, k=k, nr=nr, trm=cnt["sT"]):
                        w("A", a, "sT", trm)
                        for db in range(2):
                            m = a.activation(
                                asm[:, db * 2816 + k * 128:
                                    db * 2816 + k * 128 + nr],
                                PB[db][0:128, 0:nr], AF.Copy)
                        ia = inc(m, "sA")
                        bankg[0].append(("sA", ia))
                        bankg[1].append(("sA", ia))
                    A_(qte)
                else:
                    a_base2 = cnt["sA"]
                    t_base2 = cnt["sT"]

                    def q2tr(t, pm=pool_marks[i], a_base2=a_base2):
                        w("T", t, "sV", pm)
                        for hb in range(6):
                            cntk = min(128, 688 - hb * 128)
                            if hb >= 2:
                                w("T", t, "sA", a_base2 + hb - 1)
                            else:
                                bank_wait(t, hb % 2)
                            inc(t.transpose(
                                PB[hb % 2][0:cntk, 0:128],
                                tq[0:128, hb * 128:hb * 128 + cntk],
                                id32[0:128, 0:128]), "sT")
                    T_(q2tr)
                    trm_prev[0] = cnt["sT"]

                    def q2te(a, k=k, t_base2=t_base2, wp=wprev[0]):
                        if k == 0 and wp:
                            w("A", a, "sW", wp)
                        for hb in range(6):
                            cntk = min(128, 688 - hb * 128)
                            w("A", a, "sT", t_base2 + hb + 1)
                            inc(a.activation(
                                asm[0:cntk, hb * 1024 + k * 128:
                                    hb * 1024 + (k + 1) * 128],
                                PB[hb % 2][0:cntk, 0:128], AF.Copy), "sA")
                        bankg[0].append(("sA", cnt["sA"]))
                        bankg[1].append(("sA", cnt["sA"]))
                    A_(q2te)
            pam = cnt["sA"]

            def qwr(g, mi=mi, sh_ap=sh_ap, c0=c0, cw=cw, pam=pam):
                w("G", g, "sA", pam)
                w("G", g, "sW")
                if mi < 2:
                    for db in range(2):
                        inc(g.dma_start(
                            out=sh_ap[db * 128:(db + 1) * 128, c0:c0 + cw],
                            in_=asm[:, db * 2816:db * 2816 + cw]), "sW", 16)
                else:
                    for hb in range(6):
                        cntk = min(128, 688 - hb * 128)
                        inc(g.dma_start(
                            out=sh_ap[hb * 128:hb * 128 + cntk,
                                      c0:c0 + 1024],
                            in_=asm[0:cntk, hb * 1024:(hb + 1) * 1024]),
                            "sW", 16)
            G_(qwr)
            wprev[0] = cnt["sW"]

        def qag(g):
            w("G", g, "sW")
            for (si, so) in ((t1sh, t1T), (t3sh, t3T), (t2sh, t2T)):
                inc(g.collective_compute(
                    "AllGather", ALU.bypass, replica_groups=[list(range(8))],
                    ins=[si.opt()], outs=[so.opt()]), "sC")
        G_(qag)
        q_state["ag_done"] = cnt["sC"]
        q_state["p2_end"] = [("sA", cnt["sA"]), ("sP", cnt["sP"]),
                             ("sT", cnt["sT"]), ("sW", cnt["sW"]),
                             ("sQa", cnt["sQa"]), ("sQb", cnt["sQb"])]

    def p0(s):
        for (dst, src) in ((id32[:], id32in[:, :]), (ones32[:], ones32in[:, :]),
                           (onesb[:], onesbin[:, :]), (onesr[:], onesrbin[:, :]),
                           (mask_sb[:], maskin[:, :])):
            inc(s.dma_start(out=dst, in_=src), "sLc", 16)
        for c in range(16):
            inc(s.dma_start(out=bqs[:, c:c + 1], in_=bq[c * 128:(c + 1) * 128]),
                "sLc", 16)
            inc(s.dma_start(out=bkc[:, c:c + 1], in_=bk[c * 128:(c + 1) * 128]),
                "sLc", 16)
        inc(s.dma_start(out=rows[0:1, 0:H], in_=bv[:]), "sLc", 16)
    S_(p0)
    p0_mark = cnt["sLc"]

    def p0g(g):
        inc(g.dma_start(out=bvbf[0:1, :], in_=bv[:]), "sGc", 16)
    G_(p0g)
    bvbf_mark = cnt["sGc"]

    def p0v(v):
        w("V", v, "sLc", p0_mark)
        v.memset(epsc[:], EPS)
        v.tensor_scalar(bqs[:], bqs[:], RSQ, None, ALU.mult)
        inc(v.tensor_scalar(rows[0:1, H:2 * H], rows[0:1, 0:H],
                            float(S_FULL), None, ALU.mult), "sV")
    V_(p0v)
    sbv_mark = cnt["sV"]

    p2x = ExitStack()
    xnT = p2x.enter_context(nc.sbuf_tensor("xnT", [128, 16 * T], BF16,
                                           is_transpose_dest=True))
    qwstk = ExitStack()
    wqb4 = qwstk.enter_context(nc.sbuf_tensor("wqb4", [128, 2 * 8192], BF16))
    qstk = ExitStack()
    qt2 = qstk.enter_context(nc.sbuf_tensor("qt2", [128, 688], F32))
    qt3 = qstk.enter_context(nc.sbuf_tensor("qt3", [128, 688], F32))
    qa = qstk.enter_context(nc.sbuf_tensor("qa", [128, 688], F32))
    tq = qstk.enter_context(nc.sbuf_tensor("tq", [128, 688], F32))
    asm = qstk.enter_context(nc.sbuf_tensor("asm", [128, 6144], BF16))
    qt8 = qstk.enter_context(nc.sbuf_tensor("qt8", [128, 8 * 344], F32))
    qtl = [qt8[:, _r * 344:(_r + 1) * 344] for _r in range(8)]

    wqm = []

    def wq_load(g, qh, sl, twait=0):
        if twait:
            w("G", g, "sT", twait)
        inc(g.dma_start(
            out=wqb4[:, sl * 8192:(sl + 1) * 8192].rearrange(
                "p (kc n) -> p kc n", n=512),
            in_=wq[:, qh * 512:(qh + 1) * 512].rearrange(
                "(kc p) n -> p kc n", p=128)), "sQw", 16)
        wqm.append(cnt["sQw"])

    def wq01(g):
        wq_load(g, 0, 0)
        g.wait_ge(sem["sQw"], 16)
        wq_load(g, 1, 1)
    G_(wq01)

    with (
        nc.sbuf_tensor("xt0", [128, H], F32) as xt0,
        nc.sbuf_tensor("xt1", [128, H], F32) as xt1,
        nc.sbuf_tensor("xnb0", [128, H], BF16) as xnb0,
        nc.sbuf_tensor("xnb1", [128, H], BF16) as xnb1,
    ):
        xt = [xt0, xt1]
        xnb = [xnb0, xnb1]
        xread = [[], []]
        xnread = [[], []]
        for ti in range(17):
            r = ti % 2
            row0 = ti * 128
            emit_q1_loads(7)

            sx = "sXa" if r == 0 else "sXb"

            def p1s(s, r=r, row0=row0, rd=tuple(xread[r]), sx=sx):
                for (sn, sv) in rd:
                    s.wait_ge(sem[sn], sv)
                dma_settled(s, sx)
                inc(s.dma_start(out=xt[r][:], in_=x_sh[row0:row0 + 128, :]),
                    sx, 16)
            S_(p1s)
            ld = cnt[sx]

            def pre(v, ld=ld, sx=sx):
                w("V", v, sx, ld)
            ln_full(xt[r][:], H, pre_v=pre)

            def p1v2(v, r=r, rd=tuple(xnread[r])):
                for (sn, sv) in rd:
                    w("V", v, sn, sv)
                inc(v.tensor_scalar(xnb[r][:], xt[r][:], stats[:, 2:3],
                                    stats[:, 5:6], ALU.subtract, ALU.mult), "sV")
            V_(p1v2)
            xnm = cnt["sV"]
            xread[r] = [("sV", xnm)]

            def p1t(t, r=r, ti=ti, xnm=xnm):
                w("T", t, "sV", xnm)
                rc = 128 if ti < 16 else 2
                for n in range(4):
                    if ti == 0:
                        bank_wait(t, n)
                    m = t.matmul(PB[n][0:1, 0:512], onesb[0:rc, 0:1],
                                 xnb[r][0:rc, n * 512:(n + 1) * 512],
                                 start=(ti == 0), stop=(ti == 16))
                inc(m, "sT")
            T_(p1t)
            tsm = cnt["sT"]

            sn_ = "sNa" if r == 0 else "sNb"

            def p1g(g, r=r, row0=row0, xnm=xnm, tsm=tsm, sn_=sn_):
                w("G", g, "sV", xnm)
                w("G", g, "sT", tsm)
                dma_settled(g, sn_)
                inc(g.dma_start(out=xn_scr[row0:row0 + 128, :], in_=xnb[r][:]),
                    sn_, 16)
            G_(p1g)
            xnread[r] = [(sn_, cnt[sn_]), ("sT", tsm)]
            emit_q1_reduces(7)
        xn_written = [("sNa", cnt["sNa"]), ("sNb", cnt["sNb"])]
        p1_marks = [("sV", cnt["sV"]), ("sT", cnt["sT"]), ("sA", cnt["sA"]),
                    ("sNa", cnt["sNa"]), ("sNb", cnt["sNb"])]

        xnsum_t = cnt["sT"]

        def p1xv(v, xnsum_t=xnsum_t):
            w("V", v, "sT", xnsum_t)
            for n in range(4):
                m = v.tensor_copy(rows[0:1, n * 512:(n + 1) * 512],
                                  PB[n][0:1, 0:512])
            iv = inc(m, "sV")
            for b in range(4):
                bankg[b].append(("sV", iv))
        V_(p1xv)
        xsrow = cnt["sV"]

        def p1xt(t, xsrow=xsrow):
            w("T", t, "sV", xsrow)
            bank_wait(t, 6)
            for c in range(16):
                m = t.transpose(PB[6][0:128, c:c + 1],
                                rows[0:1, c * 128:(c + 1) * 128],
                                id32[0:1, 0:1])
            inc(m, "sT")
        T_(p1xt)
        xst = cnt["sT"]

        def p1xv2(v, xst=xst):
            w("V", v, "sT", xst)
            iv = inc(v.tensor_copy(xnsT[:], PB[6][0:128, 0:16]), "sV")
            bankg[6].append(("sV", iv))
        V_(p1xv2)
        xstv = cnt["sV"]

        def p1g2(g, xstv=xstv):
            w("G", g, "sV", xstv)
            w("G", g, "sGc", cnt["sGc"])
            inc(g.dma_start(out=xs_in[:, :], in_=xnsT[:]), "sGc", 16)
            g.wait_ge(sem["sGc"], cnt["sGc"])
            inc(g.collective_compute(
                "AllReduce", ALU.add,
                replica_groups=[[0, 1], [2, 3], [4, 5], [6, 7]],
                ins=[xs_in.opt()], outs=[xs_out.opt()]), "sC")
            g.wait_ge(sem["sC"], cnt["sC"])
            inc(g.dma_start(out=xnsT[:], in_=xs_out[:, :]), "sGc", 16)
        G_(p1g2)
        ar1 = cnt["sGc"]

        def p1v3(v, ar1=ar1):
            w("V", v, "sGc", ar1)
            inc(v.tensor_copy(xnsTb[:], xnsT[:]), "sV")
        V_(p1v3)
        xnsb_mark = cnt["sV"]

    emit_q1_chain()

    if True:
        def p2s(s):
            for (sn, sv) in xn_written:
                s.wait_ge(sem[sn], sv)
            for (sn, sv) in p1_marks:
                w("S", s, sn, sv)
            w("S", s, "sLc", cnt["sLc"])
            for dc in range(16):
                inc(s.dma_start(out=xnT[:, dc * T:(dc + 1) * T],
                                in_=xn_scr[:, dc * 128:(dc + 1) * 128],
                                transpose=True), "sLc", 16)
        S_(p2s)
        xnT_mark = cnt["sLc"]

        a_base_q = cnt["sA"]
        t_base_q = cnt["sT"]

        def qpt(t):
            w("T", t, "sLc", xnT_mark)
            for hc in range(16):
                qh, b = hc // 4, hc % 4
                sl = qh % 2
                if hc % 4 == 0:
                    w("T", t, "sQw", wqm[qh] if qh < len(wqm)
                      else 16 * (qh + 1))
                if hc >= 4:
                    w("T", t, "sA", a_base_q + hc - 3)
                else:
                    bank_wait(t, b)
                for kc in range(16):
                    m = t.matmul(
                        PB[b][0:128, 0:NM],
                        wqb4[:, sl * 8192 + kc * 512 + (hc % 4) * 128:
                             sl * 8192 + kc * 512 + (hc % 4) * 128 + 128],
                        xnT[:, kc * T:kc * T + 2041:10],
                        start=(kc == 0), stop=(kc == 15))
                inc(m, "sT")
        T_(qpt)

        def qpa(a):
            for hc in range(16):
                b = hc % 4
                w("A", a, "sT", t_base_q + hc + 1)
                ia = inc(a.activation(QT[:, hc * NM:(hc + 1) * NM],
                                      PB[b][0:128, 0:NM], AF.Identity,
                                      bias=bqs[:, hc:hc + 1], scale=RSQ),
                         "sA")
                bankg[b].append(("sA", ia))
        A_(qpa)
        q_done = cnt["sA"]

        def wq23(g):
            wq_load(g, 2, 0, t_base_q + 4)
            g.wait_ge(sem["sQw"], 48)
            wq_load(g, 3, 1, t_base_q + 8)
        G_(wq23)

        emit_quant_pass2()
        ag_done = q_state["ag_done"]
        qstk.close()
        qwstk.close()

        def qbarV(v):
            w("V", v, "sP")
            w("V", v, "sW")
        V_(qbarV)

        at2 = ExitStack()
        ctx_g = at2.enter_context(nc.sbuf_tensor("ctx_g", [GR, 1024], F32))
        Xs = at2.enter_context(nc.sbuf_tensor("Xs", [GR, 2 * GK], F32))
        Es = Xs
        ETs = at2.enter_context(nc.sbuf_tensor("ETs", [128, 2 * 4 * GR], BF16))
        nm16 = at2.enter_context(nc.sbuf_tensor("nm16", [GR, 16], F32))
        tmpc = scr2k
        VS_B = at2.enter_context(nc.sbuf_tensor("VS_B", [64, H], F32))

        vs_done = 0
        for hp in range(2):
            hps = ExitStack()
            KTh = hps.enter_context(nc.sbuf_tensor(f"KT{hp}",
                                                   [128, 8 * KEYN], BF16))
            with nc.sbuf_tensor(f"wkb{hp}", [128, 16 * 1024], BF16) as wkb:
                def pkg(g, hp=hp, pa=cnt["sA"], pt=cnt["sT"]):
                    w("G", g, "sA", pa)
                    w("G", g, "sT", pt)
                    w("G", g, "sW")
                    w("G", g, "sQa")
                    w("G", g, "sQb")
                    w("G", g, "sQw")
                    w("G", g, "sP")
                    w("G", g, "sGc", cnt["sGc"])
                    for kc in range(16):
                        inc(g.dma_start(
                            out=wkb[:, kc * 1024:(kc + 1) * 1024],
                            in_=wk[kc * 128:(kc + 1) * 128,
                                   hp * 1024:(hp + 1) * 1024]), "sGc", 16)
                G_(pkg)
                wk_mark = cnt["sGc"]

                def kmm(t, i, b, hp=hp):
                    hl, ns = divmod(i, 5)
                    for kc in range(16):
                        m = t.matmul(
                            PB[b][0:128, 0:GK],
                            wkb[:, kc * 1024 + hl * 128:
                                kc * 1024 + (hl + 1) * 128],
                            xnT[:, kc * T + ns * GK:kc * T + (ns + 1) * GK],
                            start=(kc == 0), stop=(kc == 15))
                    inc(m, "sT")

                def kev(a, i, b, hp=hp):
                    hl, ns = divmod(i, 5)
                    inc(a.activation(
                        KTh[:, hl * KEYN + ns * GK:hl * KEYN + (ns + 1) * GK],
                        PB[b][0:128, 0:GK], AF.Identity,
                        bias=bkc[:, hp * 8 + hl:hp * 8 + hl + 1]), "sA")

                mm_evac_loop(40, [0, 1, 2, 3], kmm, kev, "A",
                             t_waits=[("sGc", wk_mark), ("sLc", xnT_mark)])

            wvb = hps.enter_context(nc.sbuf_tensor(f"wvb{hp}",
                                                   [128, 16 * 1024], BF16))

            def pvg(g, hp=hp, pa=cnt["sA"], pt=cnt["sT"]):
                w("G", g, "sA", pa)
                w("G", g, "sT", pt)
                w("G", g, "sGc", cnt["sGc"])
                for kc in range(16):
                    inc(g.dma_start(
                        out=wvb[:, kc * 1024:(kc + 1) * 1024],
                        in_=wv[kc * 128:(kc + 1) * 128,
                               hp * 1024:(hp + 1) * 1024]), "sGc", 16)
            G_(pvg)
            wv_mark = cnt["sGc"]

            def vsmm(t, n, b, hp=hp):
                for kc in range(16):
                    m = t.matmul(PB[b][0:1, 0:512], xnsTb[:, kc:kc + 1],
                                 wvb[:, kc * 1024 + n * 512:
                                     kc * 1024 + (n + 1) * 512],
                                 start=(kc == 0), stop=(kc == 15))
                inc(m, "sT")

            def vsev(v, n, b, hp=hp):
                off = H + hp * 1024 + n * 512
                inc(v.scalar_tensor_tensor(
                    rows[0:1, off:off + 512], rows[0:1, off:off + 512], 1.0,
                    PB[b][0:1, 0:512], ALU.mult, ALU.add), "sV")

            mm_evac_loop(2, [4, 5], vsmm, vsev, "V",
                         t_waits=[("sGc", wv_mark), ("sV", xnsb_mark),
                                  ("sV", sbv_mark)])
            vs_done = cnt["sV"]

            def vbmm(t, n, b, hp=hp):
                off = H + hp * 1024 + n * 512
                inc(t.matmul(PB[b][0:64, 0:512], ones32[0:1, 0:64],
                             rows[0:1, off:off + 512], start=True, stop=True),
                    "sT")

            def vbev(v, n, b, hp=hp):
                off = hp * 1024 + n * 512
                inc(v.tensor_copy(VS_B[:, off:off + 512], PB[b][0:64, 0:512]),
                    "sV")

            mm_evac_loop(2, [4, 5], vbmm, vbev, "V",
                         t_waits=[("sV", vs_done)])

            Vg = hps.enter_context(nc.sbuf_tensor(f"Vg{hp}", [128, 4096], BF16))
            ctxg_read = []
            for g_i in range(NG):
                def vgmm(t, i, b, g_i=g_i, hp=hp):
                    c, n = divmod(i, 2)
                    cntk = VCHUNK[c]
                    key0 = GK * g_i + 128 * c
                    for kc in range(16):
                        t.matmul(
                            PB[b][0:cntk, 0:512],
                            xnT[:, kc * T + key0:kc * T + key0 + cntk],
                            wvb[:, kc * 1024 + n * 512:kc * 1024 + (n + 1) * 512],
                            start=(kc == 0), stop=False)
                    m = t.matmul(
                        PB[b][0:cntk, 0:512], onesr[0:1, 0:cntk],
                        bvbf[0:1, hp * 1024 + n * 512:hp * 1024 + (n + 1) * 512],
                        start=False, stop=True)
                    inc(m, "sT")

                def vgev(v, i, b, g_i=g_i, hp=hp):
                    c, n = divmod(i, 2)
                    cntk = VCHUNK[c]
                    inc(v.tensor_copy(
                        Vg[0:cntk, c * 1024 + n * 512:c * 1024 + (n + 1) * 512],
                        PB[b][0:cntk, 0:512]), "sV")

                tw = [("sA", q_done), ("sGc", bvbf_mark)]
                mm_evac_loop(8, [4, 5], vgmm, vgev, "V", t_waits=tw)
                vg_mark = cnt["sV"]

                for hb in range(4):
                    hl2 = (2 * hb, 2 * hb + 1)

                    def hsc(t, hl2=hl2, g_i=g_i, hp=hp, vg_mark=vg_mark):
                        w("T", t, "sV", vg_mark)
                        for j, hl in enumerate(hl2):
                            h = hp * 8 + hl
                            bank_wait(t, j)
                            inc(t.matmul(
                                PB[j][0:GR, 0:GK],
                                QT[:, h * NM + GR * g_i:h * NM + GR * (g_i + 1)],
                                KTh[:, hl * KEYN + GK * g_i:
                                    hl * KEYN + GK * (g_i + 1)],
                                start=True, stop=True), "sT")
                    T_(hsc)
                    scm = cnt["sT"]

                    def hmx(v, scm=scm):
                        w("V", v, "sT", scm)
                        for j in range(2):
                            v.reduce_max(nm16[:, 8 * j:8 * j + 1],
                                         PB[j][0:GR, 0:GK], axis=AXX,
                                         negate=True)
                            m = v.tensor_scalar_min(nm16[:, 8 * j + 1:8 * j + 2],
                                                    nm16[:, 8 * j:8 * j + 1],
                                                    0.0)
                        inc(m, "sV")
                    V_(hmx)
                    mxm = cnt["sV"]

                    def hex_(a, mxm=mxm):
                        w("A", a, "sV", mxm)
                        for j in range(2):
                            a.activation(nm16[:, 8 * j + 2:8 * j + 3],
                                         nm16[:, 8 * j + 1:8 * j + 2], AF.Exp)
                            m = a.activation(Xs[0:GR, j * GK:(j + 1) * GK],
                                             PB[j][0:GR, 0:GK], AF.Exp,
                                             bias=nm16[0:GR, 8 * j + 1:8 * j + 2])
                        ia = inc(m, "sA")
                        bankg[0].append(("sA", ia))
                        bankg[1].append(("sA", ia))
                    A_(hex_)
                    xm = cnt["sA"]

                    def he(v, g_i=g_i, xm=xm):
                        w("V", v, "sA", xm)
                        for j in range(2):
                            m = v.scalar_tensor_tensor(
                                Es[0:GR, j * GK:(j + 1) * GK],
                                Xs[0:GR, j * GK:(j + 1) * GK],
                                nm16[0:GR, 8 * j + 2:8 * j + 3],
                                mask_sb[0:GR, GK * g_i:GK * (g_i + 1)],
                                ALU.subtract, ALU.mult)
                        inc(m, "sV")
                    V_(he)
                    em_ = cnt["sV"]

                    def htr(t, em_=em_):
                        w("T", t, "sV", em_)
                        bank_wait(t, 6)
                        for j in range(2):
                            for c in range(4):
                                cntk = VCHUNK[c]
                                m = t.transpose(
                                    PB[6][0:cntk, (j * 4 + c) * GR:
                                          (j * 4 + c + 1) * GR],
                                    Es[0:GR, j * GK + c * 128:
                                       j * GK + c * 128 + cntk],
                                    id32[0:GR, 0:GR])
                        inc(m, "sT")
                    T_(htr)
                    trm = cnt["sT"]

                    def het(v, trm=trm):
                        w("V", v, "sT", trm)
                        for j in range(2):
                            for c in range(4):
                                cntk = VCHUNK[c]
                                m = v.tensor_copy(
                                    ETs[0:cntk, (j * 4 + c) * GR:
                                        (j * 4 + c + 1) * GR],
                                    PB[6][0:cntk, (j * 4 + c) * GR:
                                          (j * 4 + c + 1) * GR])
                        iv = inc(m, "sV")
                        bankg[6].append(("sV", iv))
                    V_(het)
                    etm = cnt["sV"]

                    def hco(t, hl2=hl2, etm=etm):
                        w("T", t, "sV", etm)
                        for j in range(2):
                            bank_wait(t, 2 + j)
                            bank_wait(t, 4 + j)
                            hl = hl2[j]
                            for c in range(4):
                                cntk = VCHUNK[c]
                                t.matmul(
                                    PB[2 + j][0:GR, 0:128],
                                    ETs[0:cntk, (j * 4 + c) * GR:
                                        (j * 4 + c + 1) * GR],
                                    Vg[0:cntk, c * 1024 + hl * 128:
                                       c * 1024 + (hl + 1) * 128],
                                    start=(c == 0), stop=(c == 3))
                                me_ = t.matmul(
                                    PB[4 + j][0:GR, 0:1],
                                    ETs[0:cntk, (j * 4 + c) * GR:
                                        (j * 4 + c + 1) * GR],
                                    onesb[0:cntk, 0:1],
                                    start=(c == 0), stop=(c == 3))
                        inc(me_, "sT")
                    T_(hco)
                    com = cnt["sT"]

                    def hz(v, hl2=hl2, hp=hp, com=com):
                        w("V", v, "sT", com)
                        for j in range(2):
                            hl = hl2[j]
                            h = hp * 8 + hl
                            o = 8 * j
                            v.scalar_tensor_tensor(
                                nm16[:, o + 3:o + 4], nm16[:, o + 2:o + 3],
                                float(S_FULL), PB[4 + j][0:GR, 0:1],
                                ALU.mult, ALU.add)
                            v.reciprocal(nm16[:, o + 4:o + 5],
                                         nm16[:, o + 3:o + 4])
                            v.tensor_tensor(nm16[:, o + 5:o + 6],
                                            nm16[:, o + 2:o + 3],
                                            nm16[:, o + 4:o + 5], ALU.mult)
                            v.tensor_scalar(tmpc[0:GR, j * 128:(j + 1) * 128],
                                            PB[2 + j][0:GR, 0:128],
                                            nm16[:, o + 4:o + 5], None,
                                            ALU.mult)
                            m = v.scalar_tensor_tensor(
                                ctx_g[0:GR, hl * 128:(hl + 1) * 128],
                                VS_B[0:GR, h * 128:(h + 1) * 128],
                                nm16[0:GR, o + 5:o + 6],
                                tmpc[0:GR, j * 128:(j + 1) * 128],
                                ALU.mult, ALU.add)
                        iv = inc(m, "sV")
                        for b in (2, 3, 4, 5):
                            bankg[b].append(("sV", iv))
                    V_(hz)
                ctx_mark = cnt["sV"]

                a_base = cnt["sV"]
                t_base7 = cnt["sT"]

                def ctp(t, ctx_mark=ctx_mark, a_base=a_base):
                    w("T", t, "sV", ctx_mark)
                    bank_wait(t, 6)
                    bank_wait(t, 7)
                    for i in range(8):
                        if i >= 2:
                            w("T", t, "sV", a_base + i - 1)
                        inc(t.transpose(PB[6 + (i % 2)][0:128, 0:GR],
                                        ctx_g[0:GR, i * 128:(i + 1) * 128],
                                        id32[0:GR, 0:GR]), "sT")
                T_(ctp)

                def cap(v, g_i=g_i, hp=hp, t_base7=t_base7):
                    for i in range(8):
                        w("V", v, "sT", t_base7 + i + 1)
                        h = hp * 8 + i
                        inc(v.tensor_copy(
                            ctxT[:, h * 206 + g_i * GR:h * 206 + (g_i + 1) * GR],
                            PB[6 + (i % 2)][0:128, 0:GR]), "sV")
                V_(cap)
                bankg[6].append(("sV", cnt["sV"]))
                bankg[7].append(("sV", cnt["sV"]))
                ctxg_read = [("sT", cnt["sT"])]
            hps.close()
        at2.close()

        def vstt(t, vs_done=vs_done):
            w("T", t, "sV", vs_done)
            bank_wait(t, 6)
            for c in range(16):
                m = t.transpose(PB[6][0:128, c:c + 1],
                                rows[0:1, H + c * 128:H + (c + 1) * 128],
                                id32[0:1, 0:1])
            inc(m, "sT")
        T_(vstt)
        vst_m = cnt["sT"]

        def vstv(v, vst_m=vst_m):
            w("V", v, "sT", vst_m)
            iv = inc(v.tensor_scalar(VSTb[:], PB[6][0:128, 0:16],
                                     1.0 / float(S_FULL), None, ALU.mult), "sV")
            bankg[6].append(("sV", iv))
            inc(v.tensor_copy(ctxT[:, 205:16 * 206:206], VSTb[:]), "sV")
        V_(vstv)
        ctxT_done = cnt["sV"]
    p2x.close()

    p2_done = [("sA", cnt["sA"]), ("sV", cnt["sV"]), ("sT", cnt["sT"])]

    bcastB = att.enter_context(nc.sbuf_tensor("bcastB", [128, H], F32))
    anl_sb = att.enter_context(nc.sbuf_tensor("anl_sb", [128, 2 * H], F32))
    x1m_sb = att.enter_context(nc.sbuf_tensor("x1m_sb", [128, 2 * H], F32))
    x1nm_sb = att.enter_context(nc.sbuf_tensor("x1nm_sb", [128, 2 * H], BF16))
    xm_sb = att.enter_context(nc.sbuf_tensor("xm_sb", [128, 2 * H], F32))
    MC = [(0, 128), (1, 77)]

    def p2bar(v, p2d=tuple(p2_done)):
        for (sn, sv) in p2d:
            w("V", v, sn, sv)
    V_(p2bar)

    def pbo(s):
        s.wait_ge(sem["sT"], cnt["sT"])
        w("S", s, "sLc", cnt["sLc"])
        inc(s.dma_start(out=rows[0:1, 0:H], in_=bo[:]), "sLc", 16)
    S_(pbo)
    bo_ld = cnt["sLc"]

    def bomm(t, i, b):
        inc(t.matmul(PB[b][0:128, 0:512], ones32[0:1, 0:128],
                     rows[0:1, i * 512:(i + 1) * 512], start=True, stop=True),
            "sT")

    def boev(v, i, b):
        inc(v.tensor_copy(bcastB[:, i * 512:(i + 1) * 512],
                          PB[b][0:128, 0:512]), "sV")

    mm_evac_loop(4, [6, 7], bomm, boev, "V", t_waits=[("sLc", bo_ld)])
    bo_b = cnt["sV"]

    def pxm(s, pt=cnt["sT"], p2d=tuple(p2_done)):
        w("S", s, "sT", pt)
        for (sn, sv) in p2d:
            w("S", s, sn, sv)
        w("S", s, "sLc", cnt["sLc"])
        inc(s.dma_start(out=xm_sb[:, 0:H], in_=x_sh[0:1280:10, :]), "sLc", 16)
        inc(s.dma_start(out=xm_sb[0:77, H:2 * H], in_=x_sh[1280:2050:10, :]),
            "sLc", 16)
    S_(pxm)
    xm_mark = cnt["sLc"]

    with nc.sbuf_tensor("wob", [128, 2 * H], BF16) as wob:
        wo_marks = []
        t_base_o = cnt["sT"]

        def pog(g):
            w("G", g, "sT", t_base_o)
            for kc in range(16):
                swo = "sWoa" if kc % 2 == 0 else "sWob"
                if kc >= 2:
                    g.wait_ge(sem["sT"], t_base_o + kc - 1)
                dma_settled(g, swo)
                inc(g.dma_start(out=wob[:, (kc % 2) * H:(kc % 2 + 1) * H],
                                in_=wo[kc * 128:(kc + 1) * 128, :]), swo, 16)
                wo_marks.append((swo, cnt[swo]))
        G_(pog)

        def pot(t):
            w("T", t, "sV", ctxT_done)
            for b in range(8):
                bank_wait(t, b)
            for kc in range(16):
                w("T", t, wo_marks[kc][0], wo_marks[kc][1])
                for mc, cw in MC:
                    rcnt = cw + (1 if mc == 1 else 0)
                    for n in range(4):
                        m = t.matmul(
                            PB[mc * 4 + n][0:rcnt, 0:512],
                            ctxT[:, kc * 206 + mc * 128:
                                 kc * 206 + mc * 128 + rcnt],
                            wob[:, (kc % 2) * H + n * 512:
                                (kc % 2) * H + (n + 1) * 512],
                            start=(kc == 0), stop=(kc == 15))
                inc(m, "sT")
        T_(pot)
        oproj_t = cnt["sT"]

        def poev(v):
            w("V", v, "sT", oproj_t)
            w("V", v, "sV", bo_b)
            for mc, cw in MC:
                rcnt = cw + (1 if mc == 1 else 0)
                for n in range(4):
                    m = v.tensor_tensor(
                        anl_sb[0:rcnt, mc * H + n * 512:mc * H + (n + 1) * 512],
                        PB[mc * 4 + n][0:rcnt, 0:512],
                        bcastB[0:rcnt, n * 512:(n + 1) * 512], ALU.add)
            iv = inc(m, "sV")
            for b in range(8):
                bankg[b].append(("sV", iv))
        V_(poev)

        for mc, cw in MC:
            rcnt = cw + (1 if mc == 1 else 0)
            ln_full(anl_sb[0:rcnt, mc * H:(mc + 1) * H], H, p=rcnt)

            def pnorm(v, mc=mc, rcnt=rcnt):
                inc(v.tensor_scalar(anl_sb[0:rcnt, mc * H:(mc + 1) * H],
                                    anl_sb[0:rcnt, mc * H:(mc + 1) * H],
                                    stats[0:rcnt, 2:3], stats[0:rcnt, 5:6],
                                    ALU.subtract, ALU.mult), "sV")
            V_(pnorm)
        anl_mark = cnt["sV"]

    def pgbase(g, anl_mark=anl_mark):
        w("G", g, "sV", anl_mark)
        w("G", g, "sGc", cnt["sGc"])
        inc(g.dma_start(out=rows[0:1, 0:H], in_=anl_sb[77:78, H:2 * H]),
            "sGc", 16)
    G_(pgbase)
    lnb_row = cnt["sGc"]

    def lbmm(t, i, b, lnb_row=lnb_row):
        if i == 0:
            w("T", t, "sGc", lnb_row)
        inc(t.matmul(PB[b][0:128, 0:512], ones32[0:1, 0:128],
                     rows[0:1, i * 512:(i + 1) * 512], start=True, stop=True),
            "sT")

    def lbev(v, i, b):
        inc(v.tensor_copy(bcastB[:, i * 512:(i + 1) * 512],
                          PB[b][0:128, 0:512]), "sV")

    mm_evac_loop(4, [6, 7], lbmm, lbev, "V", t_waits=[("sV", cnt["sV"])])
    lnbase_b = cnt["sV"]

    for mc, cw in MC:
        def pm1(v, mc=mc, cw=cw):
            w("V", v, "sLc", xm_mark)
            inc(v.tensor_tensor(x1m_sb[0:cw, mc * H:(mc + 1) * H],
                                anl_sb[0:cw, mc * H:(mc + 1) * H],
                                xm_sb[0:cw, mc * H:(mc + 1) * H], ALU.add),
                "sV")
        V_(pm1)
        ln_full(x1m_sb[0:cw, mc * H:(mc + 1) * H], H, p=cw)

        def pm2(v, mc=mc, cw=cw):
            inc(v.tensor_scalar(x1nm_sb[0:cw, mc * H:(mc + 1) * H],
                                x1m_sb[0:cw, mc * H:(mc + 1) * H],
                                stats[0:cw, 2:3], stats[0:cw, 5:6],
                                ALU.subtract, ALU.mult), "sV")
        V_(pm2)
    x1m_mark = cnt["sV"]

    with (
        nc.sbuf_tensor("bxt0", [128, H], F32) as bxt0,
        nc.sbuf_tensor("bxt1", [128, H], F32) as bxt1,
        nc.sbuf_tensor("bx10", [128, H], F32) as bx10,
        nc.sbuf_tensor("bx11", [128, H], F32) as bx11,
        nc.sbuf_tensor("bxn0", [128, H], BF16) as bxn0,
        nc.sbuf_tensor("bxn1", [128, H], BF16) as bxn1,
    ):
        bxt = [bxt0, bxt1]
        bx1 = [bx10, bx11]
        bxn = [bxn0, bxn1]
        bread = [[], []]
        bwrit = [[], []]
        for ti in range(17):
            r = ti % 2
            row0 = ti * 128

            sb_ = "sBa" if r == 0 else "sBb"

            def pbs(s, r=r, row0=row0, rd=tuple(bread[r]), sb_=sb_,
                    pt=cnt["sT"], p2d=tuple(p2_done)):
                if not rd:
                    w("S", s, "sT", pt)
                    for (sn, sv) in p2d:
                        w("S", s, sn, sv)
                for (sn, sv) in rd:
                    s.wait_ge(sem[sn], sv)
                dma_settled(s, sb_)
                inc(s.dma_start(out=bxt[r][:], in_=x_sh[row0:row0 + 128, :]),
                    sb_, 16)
            S_(pbs)
            ld = cnt[sb_]

            def pbv(v, r=r, ld=ld, rd=tuple(bwrit[r]), sb_=sb_):
                w("V", v, sb_, ld)
                w("V", v, "sV", lnbase_b)
                for (sn, sv) in rd:
                    w("V", v, sn, sv)
                inc(v.tensor_tensor(bx1[r][:], bxt[r][:], bcastB[:], ALU.add),
                    "sV")
            V_(pbv)
            bread[r] = [("sV", cnt["sV"])]
            ln_full(bx1[r][:], H)

            def pbv2(v, r=r):
                inc(v.tensor_scalar(bxn[r][:], bx1[r][:], stats[:, 2:3],
                                    stats[:, 5:6], ALU.subtract, ALU.mult),
                    "sV")
            V_(pbv2)
            vm = cnt["sV"]

            sn_ = "sNa" if r == 0 else "sNb"

            def pbg(g, r=r, row0=row0, vm=vm, sn_=sn_):
                w("G", g, "sV", vm)
                dma_settled(g, sn_)
                inc(g.dma_start(out=x1_scr[row0:row0 + 128, :], in_=bx1[r][:]),
                    sn_, 16)
                inc(g.dma_start(out=x1n_scr[row0:row0 + 128, :], in_=bxn[r][:]),
                    sn_, 16)
            G_(pbg)
            bwrit[r] = [(sn_, cnt[sn_])]
        base_written = [("sNa", cnt["sNa"]), ("sNb", cnt["sNb"])]

        def pmw(g):
            for (sn, sv) in base_written:
                g.wait_ge(sem[sn], sv)
            w("G", g, "sV", x1m_mark)
            inc(g.dma_start(out=x1_scr[0:1280:10, :], in_=x1m_sb[:, 0:H]),
                "sW", 16)
            inc(g.dma_start(out=x1_scr[1280:2050:10, :],
                            in_=x1m_sb[0:77, H:2 * H]), "sW", 16)
            inc(g.dma_start(out=x1n_scr[0:1280:10, :], in_=x1nm_sb[:, 0:H]),
                "sW", 16)
            inc(g.dma_start(out=x1n_scr[1280:2050:10, :],
                            in_=x1nm_sb[0:77, H:2 * H]), "sW", 16)
        G_(pmw)
    x1_all_written = [("sNa", cnt["sNa"]), ("sNb", cnt["sNb"]),
                      ("sW", cnt["sW"])]
    att_done = [("sV", cnt["sV"]), ("sT", cnt["sT"]), ("sA", cnt["sA"]),
                ("sLc", cnt["sLc"]), ("sGc", cnt["sGc"])]
    att.close()
    att_end = [(n, cnt[n]) for n in
               ("sV", "sT", "sA", "sLc", "sGc", "sW", "sNa", "sNb",
                "sQa", "sQb", "sWoa", "sWob", "sBa", "sBb")]


    SC1 = sc_b[:, 10:11]
    SC3 = sc_b[:, 11:12]
    SC2 = sc_b[:, 12:13]
    SC2SQ = sc_b[:, 9:10]
    out_marks = []
    gsub = [0]
    Y_TBS = [[(0, 1024)], [(0, 640), (640, 512)]]

    def t2b_copy(a):
        w("A", a, "sC", ag_done)
        dma_settled(a, "sQw")
        for q in range(4):
            for hh in range(2):
                r0 = hh * 2752
                inc(a.dma_start(
                    out=t2B[q * HIDP + r0:q * HIDP + r0 + 2752, :],
                    in_=t2T[r0:r0 + 2752, q * 512:(q + 1) * 512]),
                    "sQw", 16)
    A_(t2b_copy)
    t2b_mark = cnt["sQw"]

    with nc.sbuf_tensor("gbuf", [128, NCH * 1152], BF16) as gbuf:
        g_read_t = [0]
        silu_marks = []
        hgrp_ctr = [0]
        prev_fim = [0]
        for half, (tok0, tw_half) in enumerate(FFN_HALVES):
            nts = (tw_half + 511) // 512
            ts_slices = [(k * 512, min(512, tw_half - k * 512))
                         for k in range(nts)]
            hstk = ExitStack()
            x1nT = hstk.enter_context(
                nc.sbuf_tensor(f"x1nT{half}", [128, 16 * tw_half], BF16,
                               is_transpose_dest=True))
            w1r = hstk.enter_context(
                nc.sbuf_tensor(f"w1r{half}", [128, 2 * 2048], BF16))
            w3r = hstk.enter_context(
                nc.sbuf_tensor(f"w3r{half}", [128, 2 * 2048], BF16))
            silu0 = hstk.enter_context(
                nc.sbuf_tensor(f"silu0{half}", [128, 512], BF16))
            silu1 = hstk.enter_context(
                nc.sbuf_tensor(f"silu1{half}", [128, 512], BF16))
            silub = [silu0, silu1]
            w13_guard = [0, 0]
            if True:
                def f0s(s, half=half, tok0=tok0, tw_half=tw_half,
                        pv_=cnt["sV"], pa_=cnt["sA"], poa_=cnt["sOa"],
                        pob_=cnt["sOb"]):
                    if half == 0:
                        for (sn, sv) in att_end:
                            s.wait_ge(sem[sn], sv)
                        for (sn, sv) in x1_all_written:
                            s.wait_ge(sem[sn], sv)
                    else:
                        s.wait_ge(sem["sT"], g_read_t[0])
                        w("S", s, "sV", pv_)
                        w("S", s, "sA", pa_)
                        w("S", s, "sOa", poa_)
                        w("S", s, "sOb", pob_)
                    w("S", s, "sLc", cnt["sLc"])
                    for dc in range(16):
                        inc(s.dma_start(
                            out=x1nT[:, dc * tw_half:(dc + 1) * tw_half],
                            in_=x1n_scr[tok0:tok0 + tw_half,
                                        dc * 128:(dc + 1) * 128],
                            transpose=True), "sLc", 16)
                S_(f0s)
                x1nT_m = cnt["sLc"]

                for ch in range(NCH):
                    r = ch % 2

                    sw13 = "sW13a" if r == 0 else "sW13b"

                    def fls(s, ch=ch, r=r, half=half,
                            wg=w13_guard[r], ag_done=ag_done, sw13=sw13):
                        if ch < 2 and half == 0:
                            s.wait_ge(sem["sC"], ag_done)
                        if wg:
                            s.wait_ge(sem["sT"], wg)
                        dma_settled(s, sw13)
                        inc(s.dma_start(
                            out=w1r[:, r * 2048:(r + 1) * 2048].rearrange(
                                "p (dc h) -> p dc h", h=128),
                            in_=t1T[:, ch * 128:(ch + 1) * 128].rearrange(
                                "(dc p) h -> p dc h", p=128)), sw13, 16)
                        inc(s.dma_start(
                            out=w3r[:, r * 2048:(r + 1) * 2048].rearrange(
                                "p (dc h) -> p dc h", h=128),
                            in_=t3T[:, ch * 128:(ch + 1) * 128].rearrange(
                                "(dc p) h -> p dc h", p=128)), sw13, 16)
                    S_(fls)
                    wm_ch = cnt[sw13]

                    groups = [ts_slices[0:2]]
                    if len(ts_slices) > 2:
                        groups.append(ts_slices[2:])
                    for gi, grp in enumerate(groups):
                        base = (hgrp_ctr[0] % 2) * 4
                        hgrp_ctr[0] += 1

                        def fh(t, r=r, grp=tuple(grp), base=base,
                               x1nT=x1nT, tw_half=tw_half, wm_ch=wm_ch,
                               x1nT_m=x1nT_m, sw13=sw13):
                            w("T", t, sw13, wm_ch)
                            w("T", t, "sLc", x1nT_m)
                            for k in range(len(grp)):
                                bank_wait(t, base + k)
                                bank_wait(t, base + 2 + k)
                            for dc in range(16):
                                for k, (t0, tw_) in enumerate(grp):
                                    m3 = t.matmul(
                                        PB[base + k][0:128, 0:tw_],
                                        w1r[:, r * 2048 + dc * 128:
                                            r * 2048 + (dc + 1) * 128],
                                        x1nT[:, dc * tw_half + t0:
                                             dc * tw_half + t0 + tw_],
                                        start=(dc == 0), stop=(dc == 15))
                                for k, (t0, tw_) in enumerate(grp):
                                    m3 = t.matmul(
                                        PB[base + 2 + k][0:128, 0:tw_],
                                        w3r[:, r * 2048 + dc * 128:
                                            r * 2048 + (dc + 1) * 128],
                                        x1nT[:, dc * tw_half + t0:
                                             dc * tw_half + t0 + tw_],
                                        start=(dc == 0), stop=(dc == 15))
                            inc(m3, "sT")
                        T_(fh)
                        hm = cnt["sT"]

                        def fsi(a, grp=tuple(grp), base=base, hm=hm,
                                sg=(silu_marks[-1] if silu_marks else 0)):
                            w("A", a, "sT", hm)
                            if sg:
                                w("A", a, "sV", sg)
                            for k, (t0, tw_) in enumerate(grp):
                                m = a.activation(silub[k][0:128, 0:tw_],
                                                 PB[base + k][0:128, 0:tw_],
                                                 AF.Silu, scale=SC1)
                            inc(m, "sA")
                        A_(fsi)
                        am = cnt["sA"]

                        def fg(v, ch=ch, grp=tuple(grp), base=base, am=am,
                               tw_half=tw_half):
                            w("V", v, "sA", am)
                            for k, (t0, tw_) in enumerate(grp):
                                m = v.scalar_tensor_tensor(
                                    gbuf[:, ch * tw_half + t0:
                                         ch * tw_half + t0 + tw_],
                                    PB[base + 2 + k][0:128, 0:tw_], SC3,
                                    silub[k][0:128, 0:tw_],
                                    ALU.mult, ALU.mult)
                            iv = inc(m, "sV")
                            for k in range(len(grp)):
                                bankg[base + k].append(("sV", iv))
                                bankg[base + 2 + k].append(("sV", iv))
                        V_(fg)
                        silu_marks.append(cnt["sV"])
                    w13_guard[r] = cnt["sT"]
                g_done = cnt["sV"]
                hTdone = cnt["sT"]
            hstk.close()

            tbs = Y_TBS[half]
            nsub_max = tbs[0][1] // 128
            ystk = ExitStack()
            ysb = ystk.enter_context(
                nc.sbuf_tensor(f"ysb{half}", [128, nsub_max * H], F32))
            x1ta = ystk.enter_context(
                nc.sbuf_tensor(f"x1ta{half}", [128, H], F32))
            x1tb = ystk.enter_context(
                nc.sbuf_tensor(f"x1tb{half}", [128, H], F32))
            outa = ystk.enter_context(
                nc.sbuf_tensor(f"outa{half}", [128, H], F32))
            outc = ystk.enter_context(
                nc.sbuf_tensor(f"outc{half}", [128, H], F32))
            w2r = ystk.enter_context(
                nc.sbuf_tensor(f"w2r{half}", [128, 4 * 512], BF16))
            if True:
                x1t = [x1ta, x1tb]
                outb = [outa, outc]
                w2_guard = [0, 0, 0, 0]
                first_q = [True, True]
                W2SEMS = ["sW2a", "sW2b", "sW2c", "sW2d"]
                for tb0, tbw in tbs:
                    nsub = tbw // 128
                    ev_mark = 0
                    for q in range(4):
                        for ch in range(NCH):
                            r = ch % 2
                            sl = ch % 4

                            sw2 = W2SEMS[sl]
                            EMIT = S_ if r == 0 else G_
                            ec = "S" if r == 0 else "G"

                            def fws(s, ch=ch, sl=sl, q=q, wg=w2_guard[sl],
                                    fq=first_q[r], sw2=sw2, ec=ec,
                                    gd=g_done, ht=hTdone):
                                if fq:
                                    w(ec, s, "sV", gd)
                                    w(ec, s, "sT", ht)
                                    w(ec, s, "sQw", t2b_mark)
                                if wg:
                                    s.wait_ge(sem["sT"], wg)
                                dma_settled(s, sw2)
                                inc(s.dma_start(
                                    out=w2r[:, sl * 512:(sl + 1) * 512],
                                    in_=t2B[(q * HIDP + ch * 128):
                                            (q * HIDP + (ch + 1) * 128), :]),
                                    sw2, 16)
                            EMIT(fws)
                            first_q[r] = False
                            w2m = cnt[sw2]

                            def fy(t, ch=ch, sl=sl, tb0=tb0, nsub=nsub,
                                   w2m=w2m, g_done=g_done, tw_half=tw_half,
                                   sw2=sw2):
                                w("T", t, sw2, w2m)
                                if ch == 0:
                                    w("T", t, "sV", g_done)
                                    for s_ in range(nsub):
                                        bank_wait(t, s_)
                                for s_ in range(nsub):
                                    m = t.matmul(
                                        PB[s_][0:128, 0:512],
                                        gbuf[:, ch * tw_half + tb0 +
                                             s_ * 128:
                                             ch * tw_half + tb0 +
                                             s_ * 128 + 128],
                                        w2r[:, sl * 512:(sl + 1) * 512],
                                        start=(ch == 0),
                                        stop=(ch == NCH - 1))
                                inc(m, "sT")
                            T_(fy)
                            w2_guard[sl] = cnt["sT"]
                        ym = cnt["sT"]

                        def fyev(a, q=q, nsub=nsub, ym=ym, pf=prev_fim[0]):
                            w("A", a, "sT", ym)
                            if pf:
                                w("A", a, "sV", pf)
                            for s_ in range(nsub):
                                m = a.activation(
                                    ysb[:, s_ * H + q * 512:
                                        s_ * H + (q + 1) * 512],
                                    PB[s_][0:128, 0:512], AF.Copy)
                            ia = inc(m, "sA")
                            for s_ in range(nsub):
                                bankg[s_].append(("sA", ia))
                        A_(fyev)
                        ev_mark = cnt["sA"]

                    for s_ in range(nsub):
                        k = gsub[0]
                        gsub[0] += 1
                        r = k % 2
                        tokg = tok0 + tb0 + s_ * 128

                        sf = "sFa" if r == 0 else "sFb"

                        def fx1(s, r=r, tokg=tokg, sf=sf,
                                om=(out_marks[k - 2] if k >= 2 else None)):
                            if om is not None:
                                s.wait_ge(sem[om[0]], om[1])
                            dma_settled(s, sf)
                            inc(s.dma_start(out=x1t[r][:],
                                            in_=x1_scr[tokg:tokg + 128, :]),
                                sf, 16)
                        S_(fx1)
                        x1lm = cnt[sf]

                        yslice = ysb[:, s_ * H:(s_ + 1) * H]

                        def fst(v, yslice=yslice, em=ev_mark):
                            w("V", v, "sA", em)
                            v.tensor_scalar(scr2k[:], yslice, 1.0, None,
                                            ALU.mult, ALU.add,
                                            accum_out=stats[:, 0:1])
                            v.scalar_tensor_tensor(scr2k[:], yslice, SC2SQ,
                                                   yslice, ALU.mult, ALU.mult,
                                                   accum_out=stats[:, 1:2])
                            v.tensor_scalar(stats[:, 2:3], stats[:, 0:1],
                                            INV_H, None, ALU.mult)
                            v.tensor_tensor(stats[:, 2:3], stats[:, 2:3], SC2,
                                            ALU.mult)
                            v.tensor_scalar(stats[:, 3:4], stats[:, 1:2],
                                            INV_H, None, ALU.mult)
                            v.tensor_tensor(stats[:, 6:7], stats[:, 2:3],
                                            stats[:, 2:3], ALU.mult)
                            inc(v.tensor_tensor(stats[:, 3:4], stats[:, 3:4],
                                                stats[:, 6:7], ALU.subtract),
                                "sV")
                        V_(fst)
                        svm = cnt["sV"]

                        def fsq(a, svm=svm):
                            w("A", a, "sV", svm)
                            inc(a.activation(stats[:, 4:5], stats[:, 3:4],
                                             AF.Sqrt, bias=epsc[:, 0:1]),
                                "sA")
                        A_(fsq)
                        sqm = cnt["sA"]

                        def ffin(v, r=r, yslice=yslice, sqm=sqm, x1lm=x1lm,
                                 sf=sf):
                            w("V", v, "sA", sqm)
                            v.reciprocal(stats[:, 5:6], stats[:, 4:5])
                            v.tensor_tensor(stats[:, 6:7], SC2, stats[:, 5:6],
                                            ALU.mult)
                            v.tensor_tensor(stats[:, 7:8], stats[:, 2:3],
                                            stats[:, 5:6], ALU.mult)
                            v.tensor_scalar(scr2k[:], yslice, stats[:, 6:7],
                                            stats[:, 7:8], ALU.mult,
                                            ALU.subtract)
                            w("V", v, sf, x1lm)
                            inc(v.tensor_tensor(outb[r][:], scr2k[:],
                                                x1t[r][:], ALU.add), "sV")
                        V_(ffin)
                        fim = cnt["sV"]

                        so = "sOa" if r == 0 else "sOb"

                        def fwr(g, r=r, tokg=tokg, fim=fim, so=so):
                            w("G", g, "sV", fim)
                            dma_settled(g, so)
                            inc(g.dma_start(out=out_sh[tokg:tokg + 128, :],
                                            in_=outb[r][:]), so, 16)
                        G_(fwr)
                        out_marks.append((so, cnt[so]))
                    prev_fim[0] = cnt["sV"]
                g_read_t[0] = cnt["sT"]
            ystk.close()

    foa, fob = cnt["sOa"], cnt["sOb"]

    def fend(g):
        g.wait_ge(sem["sOa"], foa)
        g.wait_ge(sem["sOb"], fob)
    G_(fend)

    es.close()
    return nc


_BF = ml_dtypes.bfloat16


def _build_core_inputs(inputs):
    xb = np.asarray(inputs["L0_attn_hidden_states"], dtype=np.float32)
    base = {
        "wq": np.ascontiguousarray(inputs["L0_attn_wq"], dtype=np.float32),
        "wk": np.ascontiguousarray(inputs["L0_attn_wk"], dtype=np.float32),
        "wv": np.ascontiguousarray(inputs["L0_attn_wv"], dtype=np.float32),
        "wo": np.ascontiguousarray(inputs["L0_attn_wo"], dtype=np.float32),
        "bq": np.ascontiguousarray(inputs["L0_attn_bq"], dtype=np.float32),
        "bk": np.ascontiguousarray(inputs["L0_attn_bk"], dtype=np.float32),
        "bv": np.ascontiguousarray(inputs["L0_attn_bv"], dtype=np.float32),
        "bo": np.ascontiguousarray(inputs["L0_attn_bo"], dtype=np.float32),
        "id32in": np.eye(128, dtype=np.float32),
        "ones32in": np.ones((128, 128), dtype=np.float32),
        "onesbin": np.ones((128, 1), dtype=_BF),
        "onesrbin": np.ones((1, 128), dtype=_BF),
    }
    w1 = np.asarray(inputs["L0_ffn_w1"], dtype=np.float32)
    w3 = np.asarray(inputs["L0_ffn_w3"], dtype=np.float32)
    w2 = np.asarray(inputs["L0_ffn_w2"], dtype=np.float32)

    masks = []
    for half in range(2):
        m = np.zeros((GR, KEYN), dtype=_BF)
        off = 0 if half == 0 else 2050
        for g in range(NG):
            for i in range(GR):
                q_local = 10 * (GR * g + i)
                q_glob = off + q_local
                wlen = min(10, S_FULL - q_glob)
                c0 = g * GK + (q_local - GK * g)
                m[i, c0:c0 + wlen] = 1.0
        masks.append(m)

    per_core = []
    for c in range(8):
        b, half = divmod(c, 2)
        xs = np.zeros((T, H), dtype=np.float32)
        if half == 0:
            xs[0:2050] = xb[b, 0:2050]
        else:
            xs[0:2046] = xb[b, 2050:4096]
        d = dict(base)
        d["x_sh"] = xs
        d["maskin"] = masks[half]
        d["w1s"] = np.ascontiguousarray(w1[:, 256 * c:256 * (c + 1)])
        d["w3s"] = np.ascontiguousarray(w3[:, 256 * c:256 * (c + 1)])
        w2c = np.zeros((H, 688), dtype=np.float32)
        lo = 688 * c
        hi = min(688 * (c + 1), HID)
        w2c[:, 0:hi - lo] = w2[:, lo:hi]
        d["w2s"] = w2c
        per_core.append(d)
    return per_core


_NC = None


def kernel(**inputs):
    global _NC
    per_core = _build_core_inputs(inputs)
    if _NC is None:
        _NC = build_program()
    res = run_bass_kernel_spmd(_NC, per_core, core_ids=list(range(8)))
    out = np.empty((4, S_FULL, H), dtype=np.float32)
    for c in range(8):
        b, half = divmod(c, 2)
        o = res.results[c]["out_sh"]
        if half == 0:
            out[b, 0:2050] = o[0:2050]
        else:
            out[b, 2050:4096] = o[0:2046]
    return out


if __name__ == "__main__":
    nc = build_program()
    n_inst = sum(1 for _ in nc.all_instructions())
    print("build ok, instructions:", n_inst)


# version 23
# speedup vs baseline: 1.0300x; 1.0096x over previous
import math
import sys
from contextlib import ExitStack

sys.path.insert(0, "/opt/trn_rl_repo")
import numpy as np
import ml_dtypes
import concourse.bass as bass
import concourse.mybir as mybir
from concourse.bass_utils import run_bass_kernel_spmd

F32 = mybir.dt.float32
BF16 = mybir.dt.bfloat16
AF = mybir.ActivationFunctionType
ALU = mybir.AluOpType
AXX = mybir.AxisListType.X

H = 2048
NH = 16
S_FULL = 4096
HID = 5461
HIDP = 5504
NCH = 43
T = 2176
KEYN = 2050
NM = 205
NG = 5
GR = 41
GK = 410
RSQ = 1.0 / math.sqrt(128.0)
EPS = 1e-5
INV_H = 1.0 / float(H)
W1CNT = float(HID * H)
W2CNT = float(H * HID)
VCHUNK = [128, 128, 128, 26]
FFN_HALVES = [(0, 1024), (1024, 1152)]


def build_program():
    nc = bass.Bass("TRN2", target_bir_lowering=False)

    x_sh = nc.dram_tensor("x_sh", [T, H], F32, kind="ExternalInput").ap()
    wq = nc.dram_tensor("wq", [H, H], F32, kind="ExternalInput").ap()
    wk = nc.dram_tensor("wk", [H, H], F32, kind="ExternalInput").ap()
    wv = nc.dram_tensor("wv", [H, H], F32, kind="ExternalInput").ap()
    wo = nc.dram_tensor("wo", [H, H], F32, kind="ExternalInput").ap()
    bq = nc.dram_tensor("bq", [H], F32, kind="ExternalInput").ap()
    bk = nc.dram_tensor("bk", [H], F32, kind="ExternalInput").ap()
    bv = nc.dram_tensor("bv", [H], F32, kind="ExternalInput").ap()
    bo = nc.dram_tensor("bo", [H], F32, kind="ExternalInput").ap()
    w1s = nc.dram_tensor("w1s", [HID, 256], F32, kind="ExternalInput").ap()
    w3s = nc.dram_tensor("w3s", [HID, 256], F32, kind="ExternalInput").ap()
    w2s = nc.dram_tensor("w2s", [H, 688], F32, kind="ExternalInput").ap()
    maskin = nc.dram_tensor("maskin", [GR, KEYN], BF16, kind="ExternalInput").ap()
    id32in = nc.dram_tensor("id32in", [128, 128], F32, kind="ExternalInput").ap()
    ones32in = nc.dram_tensor("ones32in", [128, 128], F32,
                              kind="ExternalInput").ap()
    onesbin = nc.dram_tensor("onesbin", [128, 1], BF16, kind="ExternalInput").ap()
    onesrbin = nc.dram_tensor("onesrbin", [1, 128], BF16,
                              kind="ExternalInput").ap()

    out_sh = nc.dram_tensor("out_sh", [T, H], F32, kind="ExternalOutput").ap()

    xn_scr = nc.dram_tensor("xn_scr", [T, H], BF16, kind="Internal").ap()
    x1_scr = nc.dram_tensor("x1_scr", [T, H], F32, kind="Internal").ap()
    x1n_scr = nc.dram_tensor("x1n_scr", [T, H], BF16, kind="Internal").ap()
    xs_in = nc.dram_tensor("xs_in", [128, 16], F32, kind="Internal").ap()
    xs_out = nc.dram_tensor("xs_out", [128, 16], F32, kind="Internal").ap()
    ab_in = nc.dram_tensor("ab_in", [1, 4], F32, kind="Internal").ap()
    ab_out = nc.dram_tensor("ab_out", [1, 4], F32, kind="Internal").ap()
    t1sh = nc.dram_tensor("t1sh", [256, HIDP], BF16, kind="Internal").ap()
    t3sh = nc.dram_tensor("t3sh", [256, HIDP], BF16, kind="Internal").ap()
    t2sh = nc.dram_tensor("t2sh", [688, H], BF16, kind="Internal").ap()
    t1T = nc.dram_tensor("t1T", [H, HIDP], BF16, kind="Internal").ap()
    t3T = nc.dram_tensor("t3T", [H, HIDP], BF16, kind="Internal").ap()
    t2T = nc.dram_tensor("t2T", [HIDP, H], BF16, kind="Internal").ap()
    t2B = nc.dram_tensor("t2B", [4 * HIDP, 512], BF16, kind="Internal").ap()

    es = ExitStack()
    sem_names = ["sT", "sV", "sA", "sC", "sW", "sLc", "sGc", "sP", "sQw",
                 "sQ2", "sB2",
                 "sXa", "sXb", "sBa", "sBb", "sW13a", "sW13b", "sW2a", "sW2b",
                 "sW2c", "sW2d",
                 "sWoa", "sWob", "sQa", "sQb", "sFa", "sFb", "sOa", "sOb",
                 "sNa", "sNb"]
    sem = {n: es.enter_context(nc.semaphore(n)) for n in sem_names}
    cnt = {n: 0 for n in sem_names}
    cur = {e: {} for e in "TVASG"}

    def w(ec, eng, s, val=None):
        v = cnt[s] if val is None else val
        if v <= 0:
            return
        if cur[ec].get(s, -1) < v:
            eng.wait_ge(sem[s], v)
            cur[ec][s] = v

    def inc(instr, s, amt=1):
        if s in ("sV", "sA"):
            return cnt[s]
        instr.then_inc(sem[s], amt)
        cnt[s] += amt
        return cnt[s]

    def dma_settled(eng, s):
        if cnt[s]:
            eng.wait_ge(sem[s], cnt[s])

    PB = [es.enter_context(nc.psum_tensor(f"pb{i}", [128, 512], F32))
          for i in range(8)]
    bankg = {i: [] for i in range(8)}

    def bank_wait(t, i):
        for (s, v) in bankg[i]:
            w("T", t, s, v)
        bankg[i] = []

    id32 = es.enter_context(nc.sbuf_tensor("id32", [128, 128], F32))
    ones32 = es.enter_context(nc.sbuf_tensor("ones32", [128, 128], F32))
    onesb = es.enter_context(nc.sbuf_tensor("onesb", [128, 1], BF16))
    onesr = es.enter_context(nc.sbuf_tensor("onesr", [1, 128], BF16))
    sc_b = es.enter_context(nc.sbuf_tensor("sc_b", [128, 14], F32))
    stats = es.enter_context(nc.sbuf_tensor("stats", [128, 8], F32))
    scr2k = es.enter_context(nc.sbuf_tensor("scr2k", [128, H], F32))
    qrow = es.enter_context(nc.sbuf_tensor("qrow", [1, 8], F32))
    epsc = es.enter_context(nc.sbuf_tensor("epsc", [128, 1], F32))

    blk = es.enter_context(nc.Block())

    _CHAIN_OPS = {
        "tensor_scalar", "scalar_tensor_tensor", "tensor_tensor",
        "tensor_copy", "reduce_max", "reduce_sum", "tensor_reduce",
        "tensor_single_scalar", "tensor_scalar_min", "tensor_scalar_mul",
        "tensor_scalar_add", "tensor_add", "tensor_sub", "tensor_mul",
        "reciprocal", "memset", "memzero", "activation",
    }

    class ChainEng:

        def __init__(self, eng, sname):
            self._eng = eng
            self._sname = sname

        def __getattr__(self, name):
            fn = getattr(self._eng, name)
            if name not in _CHAIN_OPS:
                return fn
            sname = self._sname

            def wrapped(*a, **k):
                self._eng.wait_ge(sem[sname], cnt[sname])
                instr = fn(*a, **k)
                instr.then_inc(sem[sname], 1)
                cnt[sname] += 1
                return instr
            return wrapped

    def S_(fn):
        blk.sync(fn)

    def G_(fn):
        blk.gpsimd(fn)

    def P_(fn):
        blk.gpsimd(lambda g: fn(ChainEng(g, "sP")))

    def VP_(fn):
        blk.vector(lambda v: fn(ChainEng(v, "sP")))

    def T_(fn):
        blk.tensor(fn)

    def V_(fn):
        blk.vector(lambda v: fn(ChainEng(v, "sV")))

    def A_(fn):
        blk.scalar(lambda a: fn(ChainEng(a, "sA")))

    ESEC = {"V": V_, "A": A_}
    ESEM = {"V": "sV", "A": "sA"}

    def mm_evac_loop(n_iter, banks, emit_mm, emit_evac, evac_eng="A",
                     t_waits=None):
        esem = ESEM[evac_eng]
        e_base = cnt[esem]
        t_base = cnt["sT"]
        nb = len(banks)

        def tp(t):
            if t_waits:
                for (s, v) in t_waits:
                    w("T", t, s, v)
            for i in range(n_iter):
                b = banks[i % nb]
                if i >= nb:
                    w("T", t, esem, e_base + (i - nb) + 1)
                else:
                    bank_wait(t, b)
                emit_mm(t, i, b)
        T_(tp)
        assert cnt["sT"] == t_base + n_iter, "emit_mm must inc sT once"

        def ep(e):
            for i in range(n_iter):
                b = banks[i % nb]
                w(evac_eng, e, "sT", t_base + i + 1)
                emit_evac(e, i, b)
        ESEC[evac_eng](ep)
        assert cnt[esem] == e_base + n_iter, "emit_evac must inc once"
        for i in range(max(0, n_iter - nb), n_iter):
            bankg[banks[i % nb]].append((esem, e_base + i + 1))
        return cnt[esem]

    def ln_v1(v, src_ap, n, p=128):
        v.tensor_scalar(scr2k[0:p, 0:n], src_ap, 1.0, None, ALU.mult, ALU.add,
                        accum_out=stats[0:p, 0:1])
        v.scalar_tensor_tensor(scr2k[0:p, 0:n], src_ap, 1.0, src_ap,
                               ALU.mult, ALU.mult, accum_out=stats[0:p, 1:2])
        invn = 1.0 / float(n)
        v.tensor_scalar(stats[0:p, 2:3], stats[0:p, 0:1], invn, None, ALU.mult)
        v.tensor_scalar(stats[0:p, 3:4], stats[0:p, 1:2], invn, None, ALU.mult)
        v.tensor_tensor(stats[0:p, 6:7], stats[0:p, 2:3], stats[0:p, 2:3],
                        ALU.mult)
        return v.tensor_tensor(stats[0:p, 3:4], stats[0:p, 3:4],
                               stats[0:p, 6:7], ALU.subtract)

    def ln_full(src_ap, n, pre_v=None, p=128):
        def v1(v):
            if pre_v is not None:
                pre_v(v)
            inc(ln_v1(v, src_ap, n, p=p), "sV")
        V_(v1)
        stm = cnt["sV"]

        def a1(a, stm=stm):
            w("A", a, "sV", stm)
            inc(a.activation(stats[0:p, 4:5], stats[0:p, 3:4], AF.Sqrt,
                             bias=epsc[0:p, 0:1]), "sA")
        A_(a1)
        sqm = cnt["sA"]

        def v2(v, sqm=sqm):
            w("V", v, "sA", sqm)
            v.reciprocal(stats[0:p, 5:6], stats[0:p, 4:5])
        V_(v2)

    att = ExitStack()
    mask_sb = att.enter_context(nc.sbuf_tensor("mask_sb", [GR, KEYN], BF16))
    bqs = att.enter_context(nc.sbuf_tensor("bqs", [128, 16], F32))
    bkc = att.enter_context(nc.sbuf_tensor("bkc", [128, 16], F32))
    rows = att.enter_context(nc.sbuf_tensor("rows", [1, 2 * H], F32))
    bvbf = att.enter_context(nc.sbuf_tensor("bvbf", [1, H], BF16))
    xnsT = att.enter_context(nc.sbuf_tensor("xnsT", [128, 16], F32))
    xnsTb = att.enter_context(nc.sbuf_tensor("xnsTb", [128, 16], BF16))
    VSTb = att.enter_context(nc.sbuf_tensor("VSTb", [128, 16], BF16))
    QT = att.enter_context(nc.sbuf_tensor("QT", [128, NH * NM], BF16))
    ctxT = att.enter_context(nc.sbuf_tensor("ctxT", [128, 16 * 206], BF16))
    qacc = att.enter_context(nc.sbuf_tensor("qacc", [128, 4], F32))
    qtiles = []
    for _mi, _ap in ((0, w1s), (1, w3s)):
        for _ht in range(43):
            _r0 = _ht * 128
            qtiles.append((_mi, _ap, _r0, min(128, HID - _r0), 256))
    for _dt in range(16):
        qtiles.append((2, w2s, _dt * 128, 128, 688))
    q1tiles = []
    for _mi, _ap in ((0, w1s), (1, w3s)):
        for _ht in range(43):
            _r0 = _ht * 128
            q1tiles.append((_mi, _ap, _r0, min(128, HID - _r0), 0, 256))
    for _dt in range(16):
        q1tiles.append((2, w2s, _dt * 128, 128, 0, 344))
        q1tiles.append((2, w2s, _dt * 128, 128, 344, 344))
    q_state = {}

    q1_lptr = [0]

    def emit_q1_loads(kmax):
        n = len(q1tiles)
        j0 = q1_lptr[0]
        j1 = min(n, j0 + kmax)
        if j0 >= j1:
            return
        q1_lptr[0] = j1

        def qp1(g, j0=j0, j1=j1):
            if j0 > 0:
                g.wait_ge(sem["sQa"], 16 * j0)
            for j in range(j0, j1):
                mi, ap_, r0, nr, c0, ncol = q1tiles[j]
                r = j % 8
                if j >= 8:
                    g.wait_ge(sem["sP"], 2 * j - 13)
                inc(g.dma_start(out=qtl[r][0:nr, 0:ncol],
                                in_=ap_[r0:r0 + nr, c0:c0 + ncol]),
                    "sQa", 16)
        G_(qp1)

    q1_ptr = [0]

    def emit_q1_reduces(kmax):
        n = len(q1tiles)
        i0 = q1_ptr[0]
        i1 = min(n, i0 + kmax)
        if i0 >= i1:
            return
        q1_ptr[0] = i1

        def qred(v, i0=i0, i1=i1):
            if i0 == 0:
                v.memset(qacc[:], 0.0)
            v._eng.wait_ge(sem["sQa"], 16 * i1)
            for i in range(i0, i1):
                mi, ap_, r0, nr, c0, ncol = q1tiles[i]
                r = i % 8
                v.reduce_sum(qacc[0:nr, 3:4], qtl[r][0:nr, 0:ncol],
                             axis=AXX, apply_absolute_value=True)
                v.tensor_tensor(qacc[0:nr, mi:mi + 1],
                                qacc[0:nr, mi:mi + 1],
                                qacc[0:nr, 3:4], ALU.add)
        VP_(qred)

    def emit_q1_chain():
        emit_q1_loads(len(q1tiles))
        emit_q1_reduces(len(q1tiles))

        def qtot(t, pm=cnt["sP"]):
            w("T", t, "sP", pm)
            bank_wait(t, 7)
            for mi in range(3):
                m = t.matmul(PB[7][0:1, mi:mi + 1], qacc[:, mi:mi + 1],
                             ones32[0:128, 0:1], start=True, stop=True)
            inc(m, "sT")
        T_(qtot)
        qtm = cnt["sT"]

        def qrowv(v, qtm=qtm):
            w("V", v, "sT", qtm)
            v.memset(qrow[0:1, 0:4], 0.0)
            v.tensor_scalar(qrow[0:1, 0:1], PB[7][0:1, 0:1], 1.0 / W1CNT,
                            None, ALU.mult)
            v.tensor_scalar(qrow[0:1, 1:2], PB[7][0:1, 1:2], 1.0 / W1CNT,
                            None, ALU.mult)
            v.tensor_scalar(qrow[0:1, 2:3], PB[7][0:1, 2:3], 1.0 / W2CNT,
                            None, ALU.mult)
            bankg[7].append(("sV", cnt["sV"]))
        V_(qrowv)
        qrm = cnt["sV"]

        def qar(g, qrm=qrm):
            w("G", g, "sV", qrm)
            w("G", g, "sGc", cnt["sGc"])
            inc(g.dma_start(out=ab_in[:, :], in_=qrow[0:1, 0:4]), "sGc", 16)
            g.wait_ge(sem["sGc"], cnt["sGc"])
            inc(g.collective_compute(
                "AllReduce", ALU.add, replica_groups=[list(range(8))],
                ins=[ab_in.opt()], outs=[ab_out.opt()]), "sC")
            g.wait_ge(sem["sC"], cnt["sC"])
            inc(g.dma_start(out=qrow[0:1, 0:4], in_=ab_out[:, :]), "sGc", 16)
        G_(qar)
        arm = cnt["sGc"]

        def qsbt(t, arm=arm):
            w("T", t, "sGc", arm)
            bank_wait(t, 7)
            inc(t.matmul(PB[7][0:128, 0:4], ones32[0:1, 0:128],
                         qrow[0:1, 0:4], start=True, stop=True), "sT")
        T_(qsbt)
        qsb_m = cnt["sT"]

        def qsbv(v, qsb_m=qsb_m):
            w("V", v, "sT", qsb_m)
            v.tensor_copy(sc_b[:, 0:3], PB[7][0:128, 0:3])
            bankg[7].append(("sV", cnt["sV"]))
            v.tensor_scalar(sc_b[:, 3:6], sc_b[:, 0:3], 0.5, None, ALU.mult)
            v.tensor_scalar(sc_b[:, 6:9], sc_b[:, 0:3], -0.5, None, ALU.mult)
            v.tensor_tensor(sc_b[:, 9:10], sc_b[:, 2:3], sc_b[:, 2:3],
                            ALU.mult)
            v.tensor_scalar(sc_b[:, 10:13], sc_b[:, 0:3], -1.0, None, ALU.mult)
        V_(qsbv)
        q_state["scb_mark"] = cnt["sV"]

    def emit_quant_pass2():
        qtl2 = [qt2, qt3]
        scb_mark = q_state["scb_mark"]
        pool_marks = {}
        trm_prev = [0]
        wprev = [0]
        tidx = [0]
        passes = []
        for _mi, _ap, _sh in ((0, w1s, t1sh), (1, w3s, t3sh)):
            passes.append((_mi, _ap, _sh, 0, 22, 0, 2816))
            passes.append((_mi, _ap, _sh, 22, 21, 2816, 2688))
        passes.append((2, w2s, t2sh, 0, 8, 0, 1024))
        passes.append((2, w2s, t2sh, 8, 8, 1024, 1024))
        for (mi, ap_, sh_ap, k0, nt, c0, cw) in passes:
            if mi < 2:
                def qmz(a, wp=wprev[0]):
                    if wp:
                        w("A", a, "sW", wp)
                    a.memzero(asm[:, 0:5632])
                A_(qmz)
            for k in range(nt):
                i = tidx[0]
                tidx[0] += 1
                r = i % 2
                sq_ = "sQ2" if r == 0 else "sQb"
                if mi < 2:
                    r0 = (k0 + k) * 128
                    nr, ncol = min(128, HID - r0), 256
                else:
                    r0, nr, ncol = (k0 + k) * 128, 128, 688

                def qld2(s, r=r, ap_=ap_, r0=r0, nr=nr, ncol=ncol, sq_=sq_,
                         pm=pool_marks.get(i - 2, 0)):
                    if pm:
                        w("S", s, "sV", pm)
                    dma_settled(s, sq_)
                    inc(s.dma_start(out=qtl2[r][0:nr, 0:ncol],
                                    in_=ap_[r0:r0 + nr, :]), sq_, 16)
                S_(qld2)
                ldm = cnt[sq_]

                def qq(v, r=r, mi=mi, nr=nr, ncol=ncol, ldm=ldm, sq_=sq_,
                       trm=trm_prev[0], scb=scb_mark):
                    w("V", v, sq_, ldm)
                    w("V", v, "sV", scb)
                    if trm:
                        w("V", v, "sT", trm)
                    v.tensor_single_scalar(qa[0:nr, 0:ncol],
                                           qtl2[r][0:nr, 0:ncol],
                                           sc_b[0:nr, 3 + mi:4 + mi],
                                           ALU.is_gt)
                    v.scalar_tensor_tensor(tq[0:nr, 0:ncol],
                                           qtl2[r][0:nr, 0:ncol],
                                           sc_b[0:nr, 6 + mi:7 + mi],
                                           qa[0:nr, 0:ncol],
                                           ALU.is_lt, ALU.subtract)
                V_(qq)
                pool_marks[i] = cnt["sV"]

                if mi < 2:
                    def qtr(t, nr=nr, pm=pool_marks[i]):
                        w("T", t, "sV", pm)
                        for db in range(2):
                            bank_wait(t, db)
                            m = t.transpose(PB[db][0:128, 0:nr],
                                            tq[0:nr, db * 128:(db + 1) * 128],
                                            id32[0:nr, 0:nr])
                        inc(m, "sT")
                    T_(qtr)
                    trm_prev[0] = cnt["sT"]

                    def qte(a, k=k, nr=nr, trm=cnt["sT"]):
                        w("A", a, "sT", trm)
                        for db in range(2):
                            m = a.activation(
                                asm[:, db * 2816 + k * 128:
                                    db * 2816 + k * 128 + nr],
                                PB[db][0:128, 0:nr], AF.Copy)
                        ia = inc(m, "sA")
                        bankg[0].append(("sA", ia))
                        bankg[1].append(("sA", ia))
                    A_(qte)
                else:
                    a_base2 = cnt["sA"]
                    t_base2 = cnt["sT"]

                    def q2tr(t, pm=pool_marks[i], a_base2=a_base2):
                        w("T", t, "sV", pm)
                        for hb in range(6):
                            cntk = min(128, 688 - hb * 128)
                            if hb >= 2:
                                w("T", t, "sA", a_base2 + hb - 1)
                            else:
                                bank_wait(t, hb % 2)
                            inc(t.transpose(
                                PB[hb % 2][0:cntk, 0:128],
                                tq[0:128, hb * 128:hb * 128 + cntk],
                                id32[0:128, 0:128]), "sT")
                    T_(q2tr)
                    trm_prev[0] = cnt["sT"]

                    def q2te(a, k=k, t_base2=t_base2, wp=wprev[0]):
                        if k == 0 and wp:
                            w("A", a, "sW", wp)
                        for hb in range(6):
                            cntk = min(128, 688 - hb * 128)
                            w("A", a, "sT", t_base2 + hb + 1)
                            inc(a.activation(
                                asm[0:cntk, hb * 1024 + k * 128:
                                    hb * 1024 + (k + 1) * 128],
                                PB[hb % 2][0:cntk, 0:128], AF.Copy), "sA")
                        bankg[0].append(("sA", cnt["sA"]))
                        bankg[1].append(("sA", cnt["sA"]))
                    A_(q2te)
            pam = cnt["sA"]

            def qwr(g, mi=mi, sh_ap=sh_ap, c0=c0, cw=cw, pam=pam):
                w("G", g, "sA", pam)
                w("G", g, "sW")
                if mi < 2:
                    for db in range(2):
                        inc(g.dma_start(
                            out=sh_ap[db * 128:(db + 1) * 128, c0:c0 + cw],
                            in_=asm[:, db * 2816:db * 2816 + cw]), "sW", 16)
                else:
                    for hb in range(6):
                        cntk = min(128, 688 - hb * 128)
                        inc(g.dma_start(
                            out=sh_ap[hb * 128:hb * 128 + cntk,
                                      c0:c0 + 1024],
                            in_=asm[0:cntk, hb * 1024:(hb + 1) * 1024]),
                            "sW", 16)
            G_(qwr)
            wprev[0] = cnt["sW"]

        def qag(g):
            w("G", g, "sW")
            for (si, so) in ((t1sh, t1T), (t3sh, t3T), (t2sh, t2T)):
                inc(g.collective_compute(
                    "AllGather", ALU.bypass, replica_groups=[list(range(8))],
                    ins=[si.opt()], outs=[so.opt()]), "sC")
        G_(qag)
        q_state["ag_done"] = cnt["sC"]
        q_state["p2_end"] = [("sA", cnt["sA"]), ("sP", cnt["sP"]),
                             ("sT", cnt["sT"]), ("sW", cnt["sW"]),
                             ("sQa", cnt["sQa"]), ("sQb", cnt["sQb"])]

    def p0(s):
        for (dst, src) in ((id32[:], id32in[:, :]), (ones32[:], ones32in[:, :]),
                           (onesb[:], onesbin[:, :]), (onesr[:], onesrbin[:, :]),
                           (mask_sb[:], maskin[:, :])):
            inc(s.dma_start(out=dst, in_=src), "sLc", 16)
        for c in range(16):
            inc(s.dma_start(out=bqs[:, c:c + 1], in_=bq[c * 128:(c + 1) * 128]),
                "sLc", 16)
            inc(s.dma_start(out=bkc[:, c:c + 1], in_=bk[c * 128:(c + 1) * 128]),
                "sLc", 16)
        inc(s.dma_start(out=rows[0:1, 0:H], in_=bv[:]), "sLc", 16)
    S_(p0)
    p0_mark = cnt["sLc"]

    def p0g(g):
        inc(g.dma_start(out=bvbf[0:1, :], in_=bv[:]), "sGc", 16)
    G_(p0g)
    bvbf_mark = cnt["sGc"]

    def p0v(v):
        w("V", v, "sLc", p0_mark)
        v.memset(epsc[:], EPS)
        v.tensor_scalar(bqs[:], bqs[:], RSQ, None, ALU.mult)
        inc(v.tensor_scalar(rows[0:1, H:2 * H], rows[0:1, 0:H],
                            float(S_FULL), None, ALU.mult), "sV")
    V_(p0v)
    sbv_mark = cnt["sV"]

    p2x = ExitStack()
    xnT = p2x.enter_context(nc.sbuf_tensor("xnT", [128, 16 * T], BF16,
                                           is_transpose_dest=True))
    qwstk = ExitStack()
    wqb4 = qwstk.enter_context(nc.sbuf_tensor("wqb4", [128, 2 * 8192], BF16))
    qstk = ExitStack()
    qt2 = qstk.enter_context(nc.sbuf_tensor("qt2", [128, 688], F32))
    qt3 = qstk.enter_context(nc.sbuf_tensor("qt3", [128, 688], F32))
    qa = qstk.enter_context(nc.sbuf_tensor("qa", [128, 688], F32))
    tq = qstk.enter_context(nc.sbuf_tensor("tq", [128, 688], F32))
    asm = qstk.enter_context(nc.sbuf_tensor("asm", [128, 6144], BF16))
    qt8 = qstk.enter_context(nc.sbuf_tensor("qt8", [128, 8 * 344], F32))
    qtl = [qt8[:, _r * 344:(_r + 1) * 344] for _r in range(8)]

    wqm = []

    def wq_load(g, qh, sl, twait=0):
        if twait:
            w("G", g, "sT", twait)
        inc(g.dma_start(
            out=wqb4[:, sl * 8192:(sl + 1) * 8192].rearrange(
                "p (kc n) -> p kc n", n=512),
            in_=wq[:, qh * 512:(qh + 1) * 512].rearrange(
                "(kc p) n -> p kc n", p=128)), "sQw", 16)
        wqm.append(cnt["sQw"])

    def wq01(g):
        wq_load(g, 0, 0)
        g.wait_ge(sem["sQw"], 16)
        wq_load(g, 1, 1)
    G_(wq01)

    with (
        nc.sbuf_tensor("xt0", [128, H], F32) as xt0,
        nc.sbuf_tensor("xt1", [128, H], F32) as xt1,
        nc.sbuf_tensor("xnb0", [128, H], BF16) as xnb0,
        nc.sbuf_tensor("xnb1", [128, H], BF16) as xnb1,
    ):
        xt = [xt0, xt1]
        xnb = [xnb0, xnb1]
        xread = [[], []]
        xnread = [[], []]
        for ti in range(17):
            r = ti % 2
            row0 = ti * 128
            emit_q1_loads(7)

            sx = "sXa" if r == 0 else "sXb"

            def p1s(s, r=r, row0=row0, rd=tuple(xread[r]), sx=sx):
                for (sn, sv) in rd:
                    s.wait_ge(sem[sn], sv)
                dma_settled(s, sx)
                inc(s.dma_start(out=xt[r][:], in_=x_sh[row0:row0 + 128, :]),
                    sx, 16)
            S_(p1s)
            ld = cnt[sx]

            def pre(v, ld=ld, sx=sx):
                w("V", v, sx, ld)
            ln_full(xt[r][:], H, pre_v=pre)

            def p1v2(v, r=r, rd=tuple(xnread[r])):
                for (sn, sv) in rd:
                    w("V", v, sn, sv)
                inc(v.tensor_scalar(xnb[r][:], xt[r][:], stats[:, 2:3],
                                    stats[:, 5:6], ALU.subtract, ALU.mult), "sV")
            V_(p1v2)
            xnm = cnt["sV"]
            xread[r] = [("sV", xnm)]

            def p1t(t, r=r, ti=ti, xnm=xnm):
                w("T", t, "sV", xnm)
                rc = 128 if ti < 16 else 2
                for n in range(4):
                    if ti == 0:
                        bank_wait(t, n)
                    m = t.matmul(PB[n][0:1, 0:512], onesb[0:rc, 0:1],
                                 xnb[r][0:rc, n * 512:(n + 1) * 512],
                                 start=(ti == 0), stop=(ti == 16))
                inc(m, "sT")
            T_(p1t)
            tsm = cnt["sT"]

            sn_ = "sNa" if r == 0 else "sNb"

            def p1g(g, r=r, row0=row0, xnm=xnm, tsm=tsm, sn_=sn_):
                w("G", g, "sV", xnm)
                w("G", g, "sT", tsm)
                dma_settled(g, sn_)
                inc(g.dma_start(out=xn_scr[row0:row0 + 128, :], in_=xnb[r][:]),
                    sn_, 16)
            G_(p1g)
            xnread[r] = [(sn_, cnt[sn_]), ("sT", tsm)]
            emit_q1_reduces(7)
        xn_written = [("sNa", cnt["sNa"]), ("sNb", cnt["sNb"])]
        p1_marks = [("sV", cnt["sV"]), ("sT", cnt["sT"]), ("sA", cnt["sA"]),
                    ("sNa", cnt["sNa"]), ("sNb", cnt["sNb"])]

        xnsum_t = cnt["sT"]

        def p1xv(v, xnsum_t=xnsum_t):
            w("V", v, "sT", xnsum_t)
            for n in range(4):
                m = v.tensor_copy(rows[0:1, n * 512:(n + 1) * 512],
                                  PB[n][0:1, 0:512])
            iv = inc(m, "sV")
            for b in range(4):
                bankg[b].append(("sV", iv))
        V_(p1xv)
        xsrow = cnt["sV"]

        def p1xt(t, xsrow=xsrow):
            w("T", t, "sV", xsrow)
            bank_wait(t, 6)
            for c in range(16):
                m = t.transpose(PB[6][0:128, c:c + 1],
                                rows[0:1, c * 128:(c + 1) * 128],
                                id32[0:1, 0:1])
            inc(m, "sT")
        T_(p1xt)
        xst = cnt["sT"]

        def p1xv2(v, xst=xst):
            w("V", v, "sT", xst)
            iv = inc(v.tensor_copy(xnsT[:], PB[6][0:128, 0:16]), "sV")
            bankg[6].append(("sV", iv))
        V_(p1xv2)
        xstv = cnt["sV"]

        def p1g2(g, xstv=xstv):
            w("G", g, "sV", xstv)
            w("G", g, "sGc", cnt["sGc"])
            inc(g.dma_start(out=xs_in[:, :], in_=xnsT[:]), "sGc", 16)
            g.wait_ge(sem["sGc"], cnt["sGc"])
            inc(g.collective_compute(
                "AllReduce", ALU.add,
                replica_groups=[[0, 1], [2, 3], [4, 5], [6, 7]],
                ins=[xs_in.opt()], outs=[xs_out.opt()]), "sC")
            g.wait_ge(sem["sC"], cnt["sC"])
            inc(g.dma_start(out=xnsT[:], in_=xs_out[:, :]), "sGc", 16)
        G_(p1g2)
        ar1 = cnt["sGc"]

        def p1v3(v, ar1=ar1):
            w("V", v, "sGc", ar1)
            inc(v.tensor_copy(xnsTb[:], xnsT[:]), "sV")
        V_(p1v3)
        xnsb_mark = cnt["sV"]

    emit_q1_chain()

    if True:
        def p2s(s):
            for (sn, sv) in xn_written:
                s.wait_ge(sem[sn], sv)
            for (sn, sv) in p1_marks:
                w("S", s, sn, sv)
            w("S", s, "sLc", cnt["sLc"])
            for dc in range(16):
                inc(s.dma_start(out=xnT[:, dc * T:(dc + 1) * T],
                                in_=xn_scr[:, dc * 128:(dc + 1) * 128],
                                transpose=True), "sLc", 16)
        S_(p2s)
        xnT_mark = cnt["sLc"]

        a_base_q = cnt["sA"]
        t_base_q = cnt["sT"]

        def qpt(t):
            w("T", t, "sLc", xnT_mark)
            for hc in range(16):
                qh, b = hc // 4, hc % 4
                sl = qh % 2
                if hc % 4 == 0:
                    w("T", t, "sQw", wqm[qh] if qh < len(wqm)
                      else 16 * (qh + 1))
                if hc >= 4:
                    w("T", t, "sA", a_base_q + hc - 3)
                else:
                    bank_wait(t, b)
                for kc in range(16):
                    m = t.matmul(
                        PB[b][0:128, 0:NM],
                        wqb4[:, sl * 8192 + kc * 512 + (hc % 4) * 128:
                             sl * 8192 + kc * 512 + (hc % 4) * 128 + 128],
                        xnT[:, kc * T:kc * T + 2041:10],
                        start=(kc == 0), stop=(kc == 15))
                inc(m, "sT")
        T_(qpt)

        def qpa(a):
            for hc in range(16):
                b = hc % 4
                w("A", a, "sT", t_base_q + hc + 1)
                ia = inc(a.activation(QT[:, hc * NM:(hc + 1) * NM],
                                      PB[b][0:128, 0:NM], AF.Identity,
                                      bias=bqs[:, hc:hc + 1], scale=RSQ),
                         "sA")
                bankg[b].append(("sA", ia))
        A_(qpa)
        q_done = cnt["sA"]

        def wq23(g):
            wq_load(g, 2, 0, t_base_q + 4)
            g.wait_ge(sem["sQw"], 48)
            wq_load(g, 3, 1, t_base_q + 8)
        G_(wq23)

        emit_quant_pass2()
        ag_done = q_state["ag_done"]
        qstk.close()
        qwstk.close()

        def qbarV(v):
            w("V", v, "sP")
            w("V", v, "sW")
        V_(qbarV)

        at2 = ExitStack()
        ctx_g = at2.enter_context(nc.sbuf_tensor("ctx_g", [GR, 1024], F32))
        Xs = at2.enter_context(nc.sbuf_tensor("Xs", [GR, 2 * GK], F32))
        Es = Xs
        ETs = at2.enter_context(nc.sbuf_tensor("ETs", [128, 2 * 4 * GR], BF16))
        nm16 = at2.enter_context(nc.sbuf_tensor("nm16", [GR, 16], F32))
        tmpc = scr2k
        VS_B = at2.enter_context(nc.sbuf_tensor("VS_B", [64, H], F32))

        vs_done = 0
        for hp in range(2):
            hps = ExitStack()
            KTh = hps.enter_context(nc.sbuf_tensor(f"KT{hp}",
                                                   [128, 8 * KEYN], BF16))
            with nc.sbuf_tensor(f"wkb{hp}", [128, 16 * 1024], BF16) as wkb:
                def pkg(g, hp=hp, pa=cnt["sA"], pt=cnt["sT"]):
                    w("G", g, "sA", pa)
                    w("G", g, "sT", pt)
                    w("G", g, "sW")
                    w("G", g, "sQa")
                    w("G", g, "sQb")
                    w("G", g, "sQ2")
                    w("G", g, "sQw")
                    w("G", g, "sP")
                    w("G", g, "sGc", cnt["sGc"])
                    for kc in range(16):
                        inc(g.dma_start(
                            out=wkb[:, kc * 1024:(kc + 1) * 1024],
                            in_=wk[kc * 128:(kc + 1) * 128,
                                   hp * 1024:(hp + 1) * 1024]), "sGc", 16)
                G_(pkg)
                wk_mark = cnt["sGc"]

                def kmm(t, i, b, hp=hp):
                    hl, ns = divmod(i, 5)
                    for kc in range(16):
                        m = t.matmul(
                            PB[b][0:128, 0:GK],
                            wkb[:, kc * 1024 + hl * 128:
                                kc * 1024 + (hl + 1) * 128],
                            xnT[:, kc * T + ns * GK:kc * T + (ns + 1) * GK],
                            start=(kc == 0), stop=(kc == 15))
                    inc(m, "sT")

                def kev(a, i, b, hp=hp):
                    hl, ns = divmod(i, 5)
                    inc(a.activation(
                        KTh[:, hl * KEYN + ns * GK:hl * KEYN + (ns + 1) * GK],
                        PB[b][0:128, 0:GK], AF.Identity,
                        bias=bkc[:, hp * 8 + hl:hp * 8 + hl + 1]), "sA")

                mm_evac_loop(40, [0, 1, 2, 3], kmm, kev, "A",
                             t_waits=[("sGc", wk_mark), ("sLc", xnT_mark)])

            wvb = hps.enter_context(nc.sbuf_tensor(f"wvb{hp}",
                                                   [128, 16 * 1024], BF16))

            def pvg(g, hp=hp, pa=cnt["sA"], pt=cnt["sT"]):
                w("G", g, "sA", pa)
                w("G", g, "sT", pt)
                w("G", g, "sGc", cnt["sGc"])
                for kc in range(16):
                    inc(g.dma_start(
                        out=wvb[:, kc * 1024:(kc + 1) * 1024],
                        in_=wv[kc * 128:(kc + 1) * 128,
                               hp * 1024:(hp + 1) * 1024]), "sGc", 16)
            G_(pvg)
            wv_mark = cnt["sGc"]

            def vsmm(t, n, b, hp=hp):
                for kc in range(16):
                    m = t.matmul(PB[b][0:1, 0:512], xnsTb[:, kc:kc + 1],
                                 wvb[:, kc * 1024 + n * 512:
                                     kc * 1024 + (n + 1) * 512],
                                 start=(kc == 0), stop=(kc == 15))
                inc(m, "sT")

            def vsev(v, n, b, hp=hp):
                off = H + hp * 1024 + n * 512
                inc(v.scalar_tensor_tensor(
                    rows[0:1, off:off + 512], rows[0:1, off:off + 512], 1.0,
                    PB[b][0:1, 0:512], ALU.mult, ALU.add), "sV")

            mm_evac_loop(2, [4, 5], vsmm, vsev, "V",
                         t_waits=[("sGc", wv_mark), ("sV", xnsb_mark),
                                  ("sV", sbv_mark)])
            vs_done = cnt["sV"]

            def vbmm(t, n, b, hp=hp):
                off = H + hp * 1024 + n * 512
                inc(t.matmul(PB[b][0:64, 0:512], ones32[0:1, 0:64],
                             rows[0:1, off:off + 512], start=True, stop=True),
                    "sT")

            def vbev(v, n, b, hp=hp):
                off = hp * 1024 + n * 512
                inc(v.tensor_copy(VS_B[:, off:off + 512], PB[b][0:64, 0:512]),
                    "sV")

            mm_evac_loop(2, [4, 5], vbmm, vbev, "V",
                         t_waits=[("sV", vs_done)])

            Vg = hps.enter_context(nc.sbuf_tensor(f"Vg{hp}", [128, 4096], BF16))
            ctxg_read = []
            for g_i in range(NG):
                def vgmm(t, i, b, g_i=g_i, hp=hp):
                    c, n = divmod(i, 2)
                    cntk = VCHUNK[c]
                    key0 = GK * g_i + 128 * c
                    for kc in range(16):
                        t.matmul(
                            PB[b][0:cntk, 0:512],
                            xnT[:, kc * T + key0:kc * T + key0 + cntk],
                            wvb[:, kc * 1024 + n * 512:kc * 1024 + (n + 1) * 512],
                            start=(kc == 0), stop=False)
                    m = t.matmul(
                        PB[b][0:cntk, 0:512], onesr[0:1, 0:cntk],
                        bvbf[0:1, hp * 1024 + n * 512:hp * 1024 + (n + 1) * 512],
                        start=False, stop=True)
                    inc(m, "sT")

                def vgev(v, i, b, g_i=g_i, hp=hp):
                    c, n = divmod(i, 2)
                    cntk = VCHUNK[c]
                    inc(v.tensor_copy(
                        Vg[0:cntk, c * 1024 + n * 512:c * 1024 + (n + 1) * 512],
                        PB[b][0:cntk, 0:512]), "sV")

                tw = [("sA", q_done), ("sGc", bvbf_mark)]
                mm_evac_loop(8, [4, 5], vgmm, vgev, "V", t_waits=tw)
                vg_mark = cnt["sV"]

                for hb in range(4):
                    hl2 = (2 * hb, 2 * hb + 1)

                    def hsc(t, hl2=hl2, g_i=g_i, hp=hp, vg_mark=vg_mark):
                        w("T", t, "sV", vg_mark)
                        for j, hl in enumerate(hl2):
                            h = hp * 8 + hl
                            bank_wait(t, j)
                            inc(t.matmul(
                                PB[j][0:GR, 0:GK],
                                QT[:, h * NM + GR * g_i:h * NM + GR * (g_i + 1)],
                                KTh[:, hl * KEYN + GK * g_i:
                                    hl * KEYN + GK * (g_i + 1)],
                                start=True, stop=True), "sT")
                    T_(hsc)
                    scm = cnt["sT"]

                    def hmx(v, scm=scm):
                        w("V", v, "sT", scm)
                        for j in range(2):
                            v.reduce_max(nm16[:, 8 * j:8 * j + 1],
                                         PB[j][0:GR, 0:GK], axis=AXX,
                                         negate=True)
                            m = v.tensor_scalar_min(nm16[:, 8 * j + 1:8 * j + 2],
                                                    nm16[:, 8 * j:8 * j + 1],
                                                    0.0)
                        inc(m, "sV")
                    V_(hmx)
                    mxm = cnt["sV"]

                    def hex_(a, mxm=mxm):
                        w("A", a, "sV", mxm)
                        for j in range(2):
                            a.activation(nm16[:, 8 * j + 2:8 * j + 3],
                                         nm16[:, 8 * j + 1:8 * j + 2], AF.Exp)
                            m = a.activation(Xs[0:GR, j * GK:(j + 1) * GK],
                                             PB[j][0:GR, 0:GK], AF.Exp,
                                             bias=nm16[0:GR, 8 * j + 1:8 * j + 2])
                        ia = inc(m, "sA")
                        bankg[0].append(("sA", ia))
                        bankg[1].append(("sA", ia))
                    A_(hex_)
                    xm = cnt["sA"]

                    def he(v, g_i=g_i, xm=xm):
                        w("V", v, "sA", xm)
                        for j in range(2):
                            m = v.scalar_tensor_tensor(
                                Es[0:GR, j * GK:(j + 1) * GK],
                                Xs[0:GR, j * GK:(j + 1) * GK],
                                nm16[0:GR, 8 * j + 2:8 * j + 3],
                                mask_sb[0:GR, GK * g_i:GK * (g_i + 1)],
                                ALU.subtract, ALU.mult)
                        inc(m, "sV")
                    V_(he)
                    em_ = cnt["sV"]

                    def htr(t, em_=em_):
                        w("T", t, "sV", em_)
                        bank_wait(t, 6)
                        for j in range(2):
                            for c in range(4):
                                cntk = VCHUNK[c]
                                m = t.transpose(
                                    PB[6][0:cntk, (j * 4 + c) * GR:
                                          (j * 4 + c + 1) * GR],
                                    Es[0:GR, j * GK + c * 128:
                                       j * GK + c * 128 + cntk],
                                    id32[0:GR, 0:GR])
                        inc(m, "sT")
                    T_(htr)
                    trm = cnt["sT"]

                    def het(v, trm=trm):
                        w("V", v, "sT", trm)
                        for j in range(2):
                            for c in range(4):
                                cntk = VCHUNK[c]
                                m = v.tensor_copy(
                                    ETs[0:cntk, (j * 4 + c) * GR:
                                        (j * 4 + c + 1) * GR],
                                    PB[6][0:cntk, (j * 4 + c) * GR:
                                          (j * 4 + c + 1) * GR])
                        iv = inc(m, "sV")
                        bankg[6].append(("sV", iv))
                    V_(het)
                    etm = cnt["sV"]

                    def hco(t, hl2=hl2, etm=etm):
                        w("T", t, "sV", etm)
                        for j in range(2):
                            bank_wait(t, 2 + j)
                            bank_wait(t, 4 + j)
                            hl = hl2[j]
                            for c in range(4):
                                cntk = VCHUNK[c]
                                t.matmul(
                                    PB[2 + j][0:GR, 0:128],
                                    ETs[0:cntk, (j * 4 + c) * GR:
                                        (j * 4 + c + 1) * GR],
                                    Vg[0:cntk, c * 1024 + hl * 128:
                                       c * 1024 + (hl + 1) * 128],
                                    start=(c == 0), stop=(c == 3))
                                me_ = t.matmul(
                                    PB[4 + j][0:GR, 0:1],
                                    ETs[0:cntk, (j * 4 + c) * GR:
                                        (j * 4 + c + 1) * GR],
                                    onesb[0:cntk, 0:1],
                                    start=(c == 0), stop=(c == 3))
                        inc(me_, "sT")
                    T_(hco)
                    com = cnt["sT"]

                    def hz(v, hl2=hl2, hp=hp, com=com):
                        w("V", v, "sT", com)
                        for j in range(2):
                            hl = hl2[j]
                            h = hp * 8 + hl
                            o = 8 * j
                            v.scalar_tensor_tensor(
                                nm16[:, o + 3:o + 4], nm16[:, o + 2:o + 3],
                                float(S_FULL), PB[4 + j][0:GR, 0:1],
                                ALU.mult, ALU.add)
                            v.reciprocal(nm16[:, o + 4:o + 5],
                                         nm16[:, o + 3:o + 4])
                            v.tensor_tensor(nm16[:, o + 5:o + 6],
                                            nm16[:, o + 2:o + 3],
                                            nm16[:, o + 4:o + 5], ALU.mult)
                            v.tensor_scalar(tmpc[0:GR, j * 128:(j + 1) * 128],
                                            PB[2 + j][0:GR, 0:128],
                                            nm16[:, o + 4:o + 5], None,
                                            ALU.mult)
                            m = v.scalar_tensor_tensor(
                                ctx_g[0:GR, hl * 128:(hl + 1) * 128],
                                VS_B[0:GR, h * 128:(h + 1) * 128],
                                nm16[0:GR, o + 5:o + 6],
                                tmpc[0:GR, j * 128:(j + 1) * 128],
                                ALU.mult, ALU.add)
                        iv = inc(m, "sV")
                        for b in (2, 3, 4, 5):
                            bankg[b].append(("sV", iv))
                    V_(hz)
                ctx_mark = cnt["sV"]

                a_base = cnt["sV"]
                t_base7 = cnt["sT"]

                def ctp(t, ctx_mark=ctx_mark, a_base=a_base):
                    w("T", t, "sV", ctx_mark)
                    bank_wait(t, 6)
                    bank_wait(t, 7)
                    for i in range(8):
                        if i >= 2:
                            w("T", t, "sV", a_base + i - 1)
                        inc(t.transpose(PB[6 + (i % 2)][0:128, 0:GR],
                                        ctx_g[0:GR, i * 128:(i + 1) * 128],
                                        id32[0:GR, 0:GR]), "sT")
                T_(ctp)

                def cap(v, g_i=g_i, hp=hp, t_base7=t_base7):
                    for i in range(8):
                        w("V", v, "sT", t_base7 + i + 1)
                        h = hp * 8 + i
                        inc(v.tensor_copy(
                            ctxT[:, h * 206 + g_i * GR:h * 206 + (g_i + 1) * GR],
                            PB[6 + (i % 2)][0:128, 0:GR]), "sV")
                V_(cap)
                bankg[6].append(("sV", cnt["sV"]))
                bankg[7].append(("sV", cnt["sV"]))
                ctxg_read = [("sT", cnt["sT"])]
            hps.close()
        at2.close()

        def vstt(t, vs_done=vs_done):
            w("T", t, "sV", vs_done)
            bank_wait(t, 6)
            for c in range(16):
                m = t.transpose(PB[6][0:128, c:c + 1],
                                rows[0:1, H + c * 128:H + (c + 1) * 128],
                                id32[0:1, 0:1])
            inc(m, "sT")
        T_(vstt)
        vst_m = cnt["sT"]

        def vstv(v, vst_m=vst_m):
            w("V", v, "sT", vst_m)
            iv = inc(v.tensor_scalar(VSTb[:], PB[6][0:128, 0:16],
                                     1.0 / float(S_FULL), None, ALU.mult), "sV")
            bankg[6].append(("sV", iv))
            inc(v.tensor_copy(ctxT[:, 205:16 * 206:206], VSTb[:]), "sV")
        V_(vstv)
        ctxT_done = cnt["sV"]
    p2x.close()

    p2_done = [("sA", cnt["sA"]), ("sV", cnt["sV"]), ("sT", cnt["sT"])]

    bcastB = att.enter_context(nc.sbuf_tensor("bcastB", [128, H], F32))
    anl_sb = att.enter_context(nc.sbuf_tensor("anl_sb", [128, 2 * H], F32))
    x1m_sb = att.enter_context(nc.sbuf_tensor("x1m_sb", [128, 2 * H], F32))
    x1nm_sb = att.enter_context(nc.sbuf_tensor("x1nm_sb", [128, 2 * H], BF16))
    xm_sb = att.enter_context(nc.sbuf_tensor("xm_sb", [128, 2 * H], F32))
    MC = [(0, 128), (1, 77)]

    def p2bar(v, p2d=tuple(p2_done)):
        for (sn, sv) in p2d:
            w("V", v, sn, sv)
    V_(p2bar)

    def pbo(s):
        s.wait_ge(sem["sT"], cnt["sT"])
        w("S", s, "sLc", cnt["sLc"])
        inc(s.dma_start(out=rows[0:1, 0:H], in_=bo[:]), "sLc", 16)
    S_(pbo)
    bo_ld = cnt["sLc"]

    def bomm(t, i, b):
        inc(t.matmul(PB[b][0:128, 0:512], ones32[0:1, 0:128],
                     rows[0:1, i * 512:(i + 1) * 512], start=True, stop=True),
            "sT")

    def boev(v, i, b):
        inc(v.tensor_copy(bcastB[:, i * 512:(i + 1) * 512],
                          PB[b][0:128, 0:512]), "sV")

    mm_evac_loop(4, [6, 7], bomm, boev, "V", t_waits=[("sLc", bo_ld)])
    bo_b = cnt["sV"]

    def pxm(s, pt=cnt["sT"], p2d=tuple(p2_done)):
        w("S", s, "sT", pt)
        for (sn, sv) in p2d:
            w("S", s, sn, sv)
        w("S", s, "sLc", cnt["sLc"])
        inc(s.dma_start(out=xm_sb[:, 0:H], in_=x_sh[0:1280:10, :]), "sLc", 16)
        inc(s.dma_start(out=xm_sb[0:77, H:2 * H], in_=x_sh[1280:2050:10, :]),
            "sLc", 16)
    S_(pxm)
    xm_mark = cnt["sLc"]

    with nc.sbuf_tensor("wob", [128, 2 * H], BF16) as wob:
        wo_marks = []
        t_base_o = cnt["sT"]

        def pog(g):
            w("G", g, "sT", t_base_o)
            for kc in range(16):
                swo = "sWoa" if kc % 2 == 0 else "sWob"
                if kc >= 2:
                    g.wait_ge(sem["sT"], t_base_o + kc - 1)
                dma_settled(g, swo)
                inc(g.dma_start(out=wob[:, (kc % 2) * H:(kc % 2 + 1) * H],
                                in_=wo[kc * 128:(kc + 1) * 128, :]), swo, 16)
                wo_marks.append((swo, cnt[swo]))
        G_(pog)

        def pot(t):
            w("T", t, "sV", ctxT_done)
            for b in range(8):
                bank_wait(t, b)
            for kc in range(16):
                w("T", t, wo_marks[kc][0], wo_marks[kc][1])
                for mc, cw in MC:
                    rcnt = cw + (1 if mc == 1 else 0)
                    for n in range(4):
                        m = t.matmul(
                            PB[mc * 4 + n][0:rcnt, 0:512],
                            ctxT[:, kc * 206 + mc * 128:
                                 kc * 206 + mc * 128 + rcnt],
                            wob[:, (kc % 2) * H + n * 512:
                                (kc % 2) * H + (n + 1) * 512],
                            start=(kc == 0), stop=(kc == 15))
                inc(m, "sT")
        T_(pot)
        oproj_t = cnt["sT"]

        def poev(v):
            w("V", v, "sT", oproj_t)
            w("V", v, "sV", bo_b)
            for mc, cw in MC:
                rcnt = cw + (1 if mc == 1 else 0)
                for n in range(4):
                    m = v.tensor_tensor(
                        anl_sb[0:rcnt, mc * H + n * 512:mc * H + (n + 1) * 512],
                        PB[mc * 4 + n][0:rcnt, 0:512],
                        bcastB[0:rcnt, n * 512:(n + 1) * 512], ALU.add)
            iv = inc(m, "sV")
            for b in range(8):
                bankg[b].append(("sV", iv))
        V_(poev)

        for mc, cw in MC:
            rcnt = cw + (1 if mc == 1 else 0)
            ln_full(anl_sb[0:rcnt, mc * H:(mc + 1) * H], H, p=rcnt)

            def pnorm(v, mc=mc, rcnt=rcnt):
                inc(v.tensor_scalar(anl_sb[0:rcnt, mc * H:(mc + 1) * H],
                                    anl_sb[0:rcnt, mc * H:(mc + 1) * H],
                                    stats[0:rcnt, 2:3], stats[0:rcnt, 5:6],
                                    ALU.subtract, ALU.mult), "sV")
            V_(pnorm)
        anl_mark = cnt["sV"]

    def pgbase(g, anl_mark=anl_mark):
        w("G", g, "sV", anl_mark)
        w("G", g, "sGc", cnt["sGc"])
        inc(g.dma_start(out=rows[0:1, 0:H], in_=anl_sb[77:78, H:2 * H]),
            "sGc", 16)
    G_(pgbase)
    lnb_row = cnt["sGc"]

    def lbmm(t, i, b, lnb_row=lnb_row):
        if i == 0:
            w("T", t, "sGc", lnb_row)
        inc(t.matmul(PB[b][0:128, 0:512], ones32[0:1, 0:128],
                     rows[0:1, i * 512:(i + 1) * 512], start=True, stop=True),
            "sT")

    def lbev(v, i, b):
        inc(v.tensor_copy(bcastB[:, i * 512:(i + 1) * 512],
                          PB[b][0:128, 0:512]), "sV")

    mm_evac_loop(4, [6, 7], lbmm, lbev, "V", t_waits=[("sV", cnt["sV"])])
    lnbase_b = cnt["sV"]

    for mc, cw in MC:
        def pm1(v, mc=mc, cw=cw):
            w("V", v, "sLc", xm_mark)
            inc(v.tensor_tensor(x1m_sb[0:cw, mc * H:(mc + 1) * H],
                                anl_sb[0:cw, mc * H:(mc + 1) * H],
                                xm_sb[0:cw, mc * H:(mc + 1) * H], ALU.add),
                "sV")
        V_(pm1)
        ln_full(x1m_sb[0:cw, mc * H:(mc + 1) * H], H, p=cw)

        def pm2(v, mc=mc, cw=cw):
            inc(v.tensor_scalar(x1nm_sb[0:cw, mc * H:(mc + 1) * H],
                                x1m_sb[0:cw, mc * H:(mc + 1) * H],
                                stats[0:cw, 2:3], stats[0:cw, 5:6],
                                ALU.subtract, ALU.mult), "sV")
        V_(pm2)
    x1m_mark = cnt["sV"]

    with (
        nc.sbuf_tensor("bxt0", [128, H], F32) as bxt0,
        nc.sbuf_tensor("bxt1", [128, H], F32) as bxt1,
        nc.sbuf_tensor("bx10", [128, H], F32) as bx10,
        nc.sbuf_tensor("bx11", [128, H], F32) as bx11,
        nc.sbuf_tensor("bxn0", [128, H], BF16) as bxn0,
        nc.sbuf_tensor("bxn1", [128, H], BF16) as bxn1,
    ):
        bxt = [bxt0, bxt1]
        bx1 = [bx10, bx11]
        bxn = [bxn0, bxn1]
        bread = [[], []]
        bwrit = [[], []]
        for ti in range(17):
            r = ti % 2
            row0 = ti * 128

            sb_ = "sBa" if r == 0 else "sBb"

            def pbs(s, r=r, row0=row0, rd=tuple(bread[r]), sb_=sb_,
                    pt=cnt["sT"], p2d=tuple(p2_done)):
                if not rd:
                    w("S", s, "sT", pt)
                    for (sn, sv) in p2d:
                        w("S", s, sn, sv)
                for (sn, sv) in rd:
                    s.wait_ge(sem[sn], sv)
                dma_settled(s, sb_)
                inc(s.dma_start(out=bxt[r][:], in_=x_sh[row0:row0 + 128, :]),
                    sb_, 16)
            S_(pbs)
            ld = cnt[sb_]

            def pbv(v, r=r, ld=ld, rd=tuple(bwrit[r]), sb_=sb_):
                w("V", v, sb_, ld)
                w("V", v, "sV", lnbase_b)
                for (sn, sv) in rd:
                    w("V", v, sn, sv)
                inc(v.tensor_tensor(bx1[r][:], bxt[r][:], bcastB[:], ALU.add),
                    "sV")
            V_(pbv)
            bread[r] = [("sV", cnt["sV"])]
            ln_full(bx1[r][:], H)

            def pbv2(v, r=r):
                inc(v.tensor_scalar(bxn[r][:], bx1[r][:], stats[:, 2:3],
                                    stats[:, 5:6], ALU.subtract, ALU.mult),
                    "sV")
            V_(pbv2)
            vm = cnt["sV"]

            sn_ = "sNa" if r == 0 else "sNb"

            def pbg(g, r=r, row0=row0, vm=vm, sn_=sn_):
                w("G", g, "sV", vm)
                dma_settled(g, sn_)
                inc(g.dma_start(out=x1_scr[row0:row0 + 128, :], in_=bx1[r][:]),
                    sn_, 16)
                inc(g.dma_start(out=x1n_scr[row0:row0 + 128, :], in_=bxn[r][:]),
                    sn_, 16)
            G_(pbg)
            bwrit[r] = [(sn_, cnt[sn_])]
        base_written = [("sNa", cnt["sNa"]), ("sNb", cnt["sNb"])]

        def pmw(g):
            for (sn, sv) in base_written:
                g.wait_ge(sem[sn], sv)
            w("G", g, "sV", x1m_mark)
            inc(g.dma_start(out=x1_scr[0:1280:10, :], in_=x1m_sb[:, 0:H]),
                "sW", 16)
            inc(g.dma_start(out=x1_scr[1280:2050:10, :],
                            in_=x1m_sb[0:77, H:2 * H]), "sW", 16)
            inc(g.dma_start(out=x1n_scr[0:1280:10, :], in_=x1nm_sb[:, 0:H]),
                "sW", 16)
            inc(g.dma_start(out=x1n_scr[1280:2050:10, :],
                            in_=x1nm_sb[0:77, H:2 * H]), "sW", 16)
        G_(pmw)
    x1_all_written = [("sNa", cnt["sNa"]), ("sNb", cnt["sNb"]),
                      ("sW", cnt["sW"])]
    att_done = [("sV", cnt["sV"]), ("sT", cnt["sT"]), ("sA", cnt["sA"]),
                ("sLc", cnt["sLc"]), ("sGc", cnt["sGc"])]
    att.close()
    att_end = [(n, cnt[n]) for n in
               ("sV", "sT", "sA", "sLc", "sGc", "sW", "sNa", "sNb",
                "sQa", "sQb", "sWoa", "sWob", "sBa", "sBb")]


    SC1 = sc_b[:, 10:11]
    SC3 = sc_b[:, 11:12]
    SC2 = sc_b[:, 12:13]
    SC2SQ = sc_b[:, 9:10]
    out_marks = []
    gsub = [0]
    Y_TBS = [[(0, 1024)], [(0, 640), (640, 512)]]

    def t2b_copy(a):
        w("A", a, "sC", ag_done)
        for q in range(4):
            for hh in range(2):
                r0 = hh * 2752
                inc(a.dma_start(
                    out=t2B[q * HIDP + r0:q * HIDP + r0 + 2752, :],
                    in_=t2T[r0:r0 + 2752, q * 512:(q + 1) * 512]),
                    "sB2", 16)
    A_(t2b_copy)
    t2b_mark = cnt["sB2"]

    with nc.sbuf_tensor("gbuf", [128, NCH * 1152], BF16) as gbuf:
        g_read_t = [0]
        silu_marks = []
        hgrp_ctr = [0]
        prev_fim = [0]
        for half, (tok0, tw_half) in enumerate(FFN_HALVES):
            nts = (tw_half + 511) // 512
            ts_slices = [(k * 512, min(512, tw_half - k * 512))
                         for k in range(nts)]
            hstk = ExitStack()
            x1nT = hstk.enter_context(
                nc.sbuf_tensor(f"x1nT{half}", [128, 16 * tw_half], BF16,
                               is_transpose_dest=True))
            w1r = hstk.enter_context(
                nc.sbuf_tensor(f"w1r{half}", [128, 2 * 2048], BF16))
            w3r = hstk.enter_context(
                nc.sbuf_tensor(f"w3r{half}", [128, 2 * 2048], BF16))
            silu0 = hstk.enter_context(
                nc.sbuf_tensor(f"silu0{half}", [128, 512], BF16))
            silu1 = hstk.enter_context(
                nc.sbuf_tensor(f"silu1{half}", [128, 512], BF16))
            silub = [silu0, silu1]
            w13_guard = [0, 0]
            if True:
                def f0s(s, half=half, tok0=tok0, tw_half=tw_half,
                        pv_=cnt["sV"], pa_=cnt["sA"], poa_=cnt["sOa"],
                        pob_=cnt["sOb"]):
                    if half == 0:
                        for (sn, sv) in att_end:
                            s.wait_ge(sem[sn], sv)
                        for (sn, sv) in x1_all_written:
                            s.wait_ge(sem[sn], sv)
                    else:
                        s.wait_ge(sem["sT"], g_read_t[0])
                        w("S", s, "sV", pv_)
                        w("S", s, "sA", pa_)
                        w("S", s, "sOa", poa_)
                        w("S", s, "sOb", pob_)
                    w("S", s, "sLc", cnt["sLc"])
                    for dc in range(16):
                        inc(s.dma_start(
                            out=x1nT[:, dc * tw_half:(dc + 1) * tw_half],
                            in_=x1n_scr[tok0:tok0 + tw_half,
                                        dc * 128:(dc + 1) * 128],
                            transpose=True), "sLc", 16)
                S_(f0s)
                x1nT_m = cnt["sLc"]

                for ch in range(NCH):
                    r = ch % 2

                    sw13 = "sW13a" if r == 0 else "sW13b"

                    def fls(s, ch=ch, r=r, half=half,
                            wg=w13_guard[r], ag_done=ag_done, sw13=sw13):
                        if ch < 2 and half == 0:
                            s.wait_ge(sem["sC"], ag_done)
                        if wg:
                            s.wait_ge(sem["sT"], wg)
                        dma_settled(s, sw13)
                        inc(s.dma_start(
                            out=w1r[:, r * 2048:(r + 1) * 2048].rearrange(
                                "p (dc h) -> p dc h", h=128),
                            in_=t1T[:, ch * 128:(ch + 1) * 128].rearrange(
                                "(dc p) h -> p dc h", p=128)), sw13, 16)
                        inc(s.dma_start(
                            out=w3r[:, r * 2048:(r + 1) * 2048].rearrange(
                                "p (dc h) -> p dc h", h=128),
                            in_=t3T[:, ch * 128:(ch + 1) * 128].rearrange(
                                "(dc p) h -> p dc h", p=128)), sw13, 16)
                    S_(fls)
                    wm_ch = cnt[sw13]

                    groups = [ts_slices[0:2]]
                    if len(ts_slices) > 2:
                        groups.append(ts_slices[2:])
                    for gi, grp in enumerate(groups):
                        base = (hgrp_ctr[0] % 2) * 4
                        hgrp_ctr[0] += 1

                        def fh(t, r=r, grp=tuple(grp), base=base,
                               x1nT=x1nT, tw_half=tw_half, wm_ch=wm_ch,
                               x1nT_m=x1nT_m, sw13=sw13):
                            w("T", t, sw13, wm_ch)
                            w("T", t, "sLc", x1nT_m)
                            for k in range(len(grp)):
                                bank_wait(t, base + k)
                                bank_wait(t, base + 2 + k)
                            for dc in range(16):
                                for k, (t0, tw_) in enumerate(grp):
                                    m3 = t.matmul(
                                        PB[base + k][0:128, 0:tw_],
                                        w1r[:, r * 2048 + dc * 128:
                                            r * 2048 + (dc + 1) * 128],
                                        x1nT[:, dc * tw_half + t0:
                                             dc * tw_half + t0 + tw_],
                                        start=(dc == 0), stop=(dc == 15))
                                for k, (t0, tw_) in enumerate(grp):
                                    m3 = t.matmul(
                                        PB[base + 2 + k][0:128, 0:tw_],
                                        w3r[:, r * 2048 + dc * 128:
                                            r * 2048 + (dc + 1) * 128],
                                        x1nT[:, dc * tw_half + t0:
                                             dc * tw_half + t0 + tw_],
                                        start=(dc == 0), stop=(dc == 15))
                            inc(m3, "sT")
                        T_(fh)
                        hm = cnt["sT"]

                        def fsi(a, grp=tuple(grp), base=base, hm=hm,
                                sg=(silu_marks[-1] if silu_marks else 0)):
                            w("A", a, "sT", hm)
                            if sg:
                                w("A", a, "sV", sg)
                            for k, (t0, tw_) in enumerate(grp):
                                m = a.activation(silub[k][0:128, 0:tw_],
                                                 PB[base + k][0:128, 0:tw_],
                                                 AF.Silu, scale=SC1)
                            inc(m, "sA")
                        A_(fsi)
                        am = cnt["sA"]

                        def fg(v, ch=ch, grp=tuple(grp), base=base, am=am,
                               tw_half=tw_half):
                            w("V", v, "sA", am)
                            for k, (t0, tw_) in enumerate(grp):
                                m = v.scalar_tensor_tensor(
                                    gbuf[:, ch * tw_half + t0:
                                         ch * tw_half + t0 + tw_],
                                    PB[base + 2 + k][0:128, 0:tw_], SC3,
                                    silub[k][0:128, 0:tw_],
                                    ALU.mult, ALU.mult)
                            iv = inc(m, "sV")
                            for k in range(len(grp)):
                                bankg[base + k].append(("sV", iv))
                                bankg[base + 2 + k].append(("sV", iv))
                        V_(fg)
                        silu_marks.append(cnt["sV"])
                    w13_guard[r] = cnt["sT"]
                g_done = cnt["sV"]
                hTdone = cnt["sT"]
            hstk.close()

            tbs = Y_TBS[half]
            nsub_max = tbs[0][1] // 128
            ystk = ExitStack()
            ysb = ystk.enter_context(
                nc.sbuf_tensor(f"ysb{half}", [128, nsub_max * H], F32))
            x1ta = ystk.enter_context(
                nc.sbuf_tensor(f"x1ta{half}", [128, H], F32))
            x1tb = ystk.enter_context(
                nc.sbuf_tensor(f"x1tb{half}", [128, H], F32))
            outa = ystk.enter_context(
                nc.sbuf_tensor(f"outa{half}", [128, H], F32))
            outc = ystk.enter_context(
                nc.sbuf_tensor(f"outc{half}", [128, H], F32))
            w2r = ystk.enter_context(
                nc.sbuf_tensor(f"w2r{half}", [128, 4 * 512], BF16))
            if True:
                x1t = [x1ta, x1tb]
                outb = [outa, outc]
                w2_guard = [0, 0, 0, 0]
                first_q = [True, True]
                W2SEMS = ["sW2a", "sW2b", "sW2c", "sW2d"]
                for tb0, tbw in tbs:
                    nsub = tbw // 128
                    ev_mark = 0
                    for q in range(4):
                        for ch in range(NCH):
                            r = ch % 2
                            sl = ch % 4

                            sw2 = W2SEMS[sl]
                            EMIT = S_ if r == 0 else G_
                            ec = "S" if r == 0 else "G"

                            def fws(s, ch=ch, sl=sl, q=q, wg=w2_guard[sl],
                                    fq=first_q[r], sw2=sw2, ec=ec,
                                    gd=g_done, ht=hTdone):
                                if fq:
                                    w(ec, s, "sV", gd)
                                    w(ec, s, "sT", ht)
                                    w(ec, s, "sB2", t2b_mark)
                                if wg:
                                    s.wait_ge(sem["sT"], wg)
                                dma_settled(s, sw2)
                                inc(s.dma_start(
                                    out=w2r[:, sl * 512:(sl + 1) * 512],
                                    in_=t2B[(q * HIDP + ch * 128):
                                            (q * HIDP + (ch + 1) * 128), :]),
                                    sw2, 16)
                            EMIT(fws)
                            first_q[r] = False
                            w2m = cnt[sw2]

                            def fy(t, ch=ch, sl=sl, tb0=tb0, nsub=nsub,
                                   w2m=w2m, g_done=g_done, tw_half=tw_half,
                                   sw2=sw2):
                                w("T", t, sw2, w2m)
                                if ch == 0:
                                    w("T", t, "sV", g_done)
                                    for s_ in range(nsub):
                                        bank_wait(t, s_)
                                for s_ in range(nsub):
                                    m = t.matmul(
                                        PB[s_][0:128, 0:512],
                                        gbuf[:, ch * tw_half + tb0 +
                                             s_ * 128:
                                             ch * tw_half + tb0 +
                                             s_ * 128 + 128],
                                        w2r[:, sl * 512:(sl + 1) * 512],
                                        start=(ch == 0),
                                        stop=(ch == NCH - 1))
                                inc(m, "sT")
                            T_(fy)
                            w2_guard[sl] = cnt["sT"]
                        ym = cnt["sT"]

                        def fyev(a, q=q, nsub=nsub, ym=ym, pf=prev_fim[0]):
                            w("A", a, "sT", ym)
                            if pf:
                                w("A", a, "sV", pf)
                            for s_ in range(nsub):
                                m = a.activation(
                                    ysb[:, s_ * H + q * 512:
                                        s_ * H + (q + 1) * 512],
                                    PB[s_][0:128, 0:512], AF.Copy)
                            ia = inc(m, "sA")
                            for s_ in range(nsub):
                                bankg[s_].append(("sA", ia))
                        A_(fyev)
                        ev_mark = cnt["sA"]

                    for s_ in range(nsub):
                        k = gsub[0]
                        gsub[0] += 1
                        r = k % 2
                        tokg = tok0 + tb0 + s_ * 128

                        sf = "sFa" if r == 0 else "sFb"

                        def fx1(s, r=r, tokg=tokg, sf=sf,
                                om=(out_marks[k - 2] if k >= 2 else None)):
                            if om is not None:
                                s.wait_ge(sem[om[0]], om[1])
                            dma_settled(s, sf)
                            inc(s.dma_start(out=x1t[r][:],
                                            in_=x1_scr[tokg:tokg + 128, :]),
                                sf, 16)
                        S_(fx1)
                        x1lm = cnt[sf]

                        yslice = ysb[:, s_ * H:(s_ + 1) * H]

                        def fst(v, yslice=yslice, em=ev_mark):
                            w("V", v, "sA", em)
                            v.tensor_scalar(scr2k[:], yslice, 1.0, None,
                                            ALU.mult, ALU.add,
                                            accum_out=stats[:, 0:1])
                            v.scalar_tensor_tensor(scr2k[:], yslice, SC2SQ,
                                                   yslice, ALU.mult, ALU.mult,
                                                   accum_out=stats[:, 1:2])
                            v.tensor_scalar(stats[:, 2:3], stats[:, 0:1],
                                            INV_H, None, ALU.mult)
                            v.tensor_tensor(stats[:, 2:3], stats[:, 2:3], SC2,
                                            ALU.mult)
                            v.tensor_scalar(stats[:, 3:4], stats[:, 1:2],
                                            INV_H, None, ALU.mult)
                            v.tensor_tensor(stats[:, 6:7], stats[:, 2:3],
                                            stats[:, 2:3], ALU.mult)
                            inc(v.tensor_tensor(stats[:, 3:4], stats[:, 3:4],
                                                stats[:, 6:7], ALU.subtract),
                                "sV")
                        V_(fst)
                        svm = cnt["sV"]

                        def fsq(a, svm=svm):
                            w("A", a, "sV", svm)
                            inc(a.activation(stats[:, 4:5], stats[:, 3:4],
                                             AF.Sqrt, bias=epsc[:, 0:1]),
                                "sA")
                        A_(fsq)
                        sqm = cnt["sA"]

                        def ffin(v, r=r, yslice=yslice, sqm=sqm, x1lm=x1lm,
                                 sf=sf):
                            w("V", v, "sA", sqm)
                            v.reciprocal(stats[:, 5:6], stats[:, 4:5])
                            v.tensor_tensor(stats[:, 6:7], SC2, stats[:, 5:6],
                                            ALU.mult)
                            v.tensor_tensor(stats[:, 7:8], stats[:, 2:3],
                                            stats[:, 5:6], ALU.mult)
                            v.tensor_scalar(scr2k[:], yslice, stats[:, 6:7],
                                            stats[:, 7:8], ALU.mult,
                                            ALU.subtract)
                            w("V", v, sf, x1lm)
                            inc(v.tensor_tensor(outb[r][:], scr2k[:],
                                                x1t[r][:], ALU.add), "sV")
                        V_(ffin)
                        fim = cnt["sV"]

                        so = "sOa" if r == 0 else "sOb"

                        def fwr(g, r=r, tokg=tokg, fim=fim, so=so):
                            w("G", g, "sV", fim)
                            dma_settled(g, so)
                            inc(g.dma_start(out=out_sh[tokg:tokg + 128, :],
                                            in_=outb[r][:]), so, 16)
                        G_(fwr)
                        out_marks.append((so, cnt[so]))
                    prev_fim[0] = cnt["sV"]
                g_read_t[0] = cnt["sT"]
            ystk.close()

    foa, fob = cnt["sOa"], cnt["sOb"]

    def fend(g):
        g.wait_ge(sem["sOa"], foa)
        g.wait_ge(sem["sOb"], fob)
    G_(fend)

    es.close()
    return nc


_BF = ml_dtypes.bfloat16


def _build_core_inputs(inputs):
    xb = np.asarray(inputs["L0_attn_hidden_states"], dtype=np.float32)
    base = {
        "wq": np.ascontiguousarray(inputs["L0_attn_wq"], dtype=np.float32),
        "wk": np.ascontiguousarray(inputs["L0_attn_wk"], dtype=np.float32),
        "wv": np.ascontiguousarray(inputs["L0_attn_wv"], dtype=np.float32),
        "wo": np.ascontiguousarray(inputs["L0_attn_wo"], dtype=np.float32),
        "bq": np.ascontiguousarray(inputs["L0_attn_bq"], dtype=np.float32),
        "bk": np.ascontiguousarray(inputs["L0_attn_bk"], dtype=np.float32),
        "bv": np.ascontiguousarray(inputs["L0_attn_bv"], dtype=np.float32),
        "bo": np.ascontiguousarray(inputs["L0_attn_bo"], dtype=np.float32),
        "id32in": np.eye(128, dtype=np.float32),
        "ones32in": np.ones((128, 128), dtype=np.float32),
        "onesbin": np.ones((128, 1), dtype=_BF),
        "onesrbin": np.ones((1, 128), dtype=_BF),
    }
    w1 = np.asarray(inputs["L0_ffn_w1"], dtype=np.float32)
    w3 = np.asarray(inputs["L0_ffn_w3"], dtype=np.float32)
    w2 = np.asarray(inputs["L0_ffn_w2"], dtype=np.float32)

    masks = []
    for half in range(2):
        m = np.zeros((GR, KEYN), dtype=_BF)
        off = 0 if half == 0 else 2050
        for g in range(NG):
            for i in range(GR):
                q_local = 10 * (GR * g + i)
                q_glob = off + q_local
                wlen = min(10, S_FULL - q_glob)
                c0 = g * GK + (q_local - GK * g)
                m[i, c0:c0 + wlen] = 1.0
        masks.append(m)

    per_core = []
    for c in range(8):
        b, half = divmod(c, 2)
        xs = np.zeros((T, H), dtype=np.float32)
        if half == 0:
            xs[0:2050] = xb[b, 0:2050]
        else:
            xs[0:2046] = xb[b, 2050:4096]
        d = dict(base)
        d["x_sh"] = xs
        d["maskin"] = masks[half]
        d["w1s"] = np.ascontiguousarray(w1[:, 256 * c:256 * (c + 1)])
        d["w3s"] = np.ascontiguousarray(w3[:, 256 * c:256 * (c + 1)])
        w2c = np.zeros((H, 688), dtype=np.float32)
        lo = 688 * c
        hi = min(688 * (c + 1), HID)
        w2c[:, 0:hi - lo] = w2[:, lo:hi]
        d["w2s"] = w2c
        per_core.append(d)
    return per_core


_NC = None


def kernel(**inputs):
    global _NC
    per_core = _build_core_inputs(inputs)
    if _NC is None:
        _NC = build_program()
    res = run_bass_kernel_spmd(_NC, per_core, core_ids=list(range(8)))
    out = np.empty((4, S_FULL, H), dtype=np.float32)
    for c in range(8):
        b, half = divmod(c, 2)
        o = res.results[c]["out_sh"]
        if half == 0:
            out[b, 0:2050] = o[0:2050]
        else:
            out[b, 2050:4096] = o[0:2046]
    return out


if __name__ == "__main__":
    nc = build_program()
    n_inst = sum(1 for _ in nc.all_instructions())
    print("build ok, instructions:", n_inst)


# version 24 (final)
# speedup vs baseline: 1.0313x; 1.0012x over previous
import math
import sys
from contextlib import ExitStack

sys.path.insert(0, "/opt/trn_rl_repo")
import numpy as np
import ml_dtypes
import concourse.bass as bass
import concourse.mybir as mybir
from concourse.bass_utils import run_bass_kernel_spmd

F32 = mybir.dt.float32
BF16 = mybir.dt.bfloat16
AF = mybir.ActivationFunctionType
ALU = mybir.AluOpType
AXX = mybir.AxisListType.X

H = 2048
NH = 16
S_FULL = 4096
HID = 5461
HIDP = 5504
NCH = 43
T = 2176
KEYN = 2050
NM = 205
NG = 5
GR = 41
GK = 410
RSQ = 1.0 / math.sqrt(128.0)
EPS = 1e-5
INV_H = 1.0 / float(H)
W1CNT = float(HID * H)
W2CNT = float(H * HID)
VCHUNK = [128, 128, 128, 26]
FFN_HALVES = [(0, 1024), (1024, 1152)]


def build_program():
    nc = bass.Bass("TRN2", target_bir_lowering=False)

    x_sh = nc.dram_tensor("x_sh", [T, H], F32, kind="ExternalInput").ap()
    wq = nc.dram_tensor("wq", [H, H], F32, kind="ExternalInput").ap()
    wk = nc.dram_tensor("wk", [H, H], F32, kind="ExternalInput").ap()
    wv = nc.dram_tensor("wv", [H, H], F32, kind="ExternalInput").ap()
    wo = nc.dram_tensor("wo", [H, H], F32, kind="ExternalInput").ap()
    bq = nc.dram_tensor("bq", [H], F32, kind="ExternalInput").ap()
    bk = nc.dram_tensor("bk", [H], F32, kind="ExternalInput").ap()
    bv = nc.dram_tensor("bv", [H], F32, kind="ExternalInput").ap()
    bo = nc.dram_tensor("bo", [H], F32, kind="ExternalInput").ap()
    w1s = nc.dram_tensor("w1s", [HID, 256], F32, kind="ExternalInput").ap()
    w3s = nc.dram_tensor("w3s", [HID, 256], F32, kind="ExternalInput").ap()
    w2s = nc.dram_tensor("w2s", [H, 688], F32, kind="ExternalInput").ap()
    maskin = nc.dram_tensor("maskin", [GR, KEYN], BF16, kind="ExternalInput").ap()
    id32in = nc.dram_tensor("id32in", [128, 128], F32, kind="ExternalInput").ap()
    ones32in = nc.dram_tensor("ones32in", [128, 128], F32,
                              kind="ExternalInput").ap()
    onesbin = nc.dram_tensor("onesbin", [128, 1], BF16, kind="ExternalInput").ap()
    onesrbin = nc.dram_tensor("onesrbin", [1, 128], BF16,
                              kind="ExternalInput").ap()

    out_sh = nc.dram_tensor("out_sh", [T, H], F32, kind="ExternalOutput").ap()

    xn_scr = nc.dram_tensor("xn_scr", [T, H], BF16, kind="Internal").ap()
    x1_scr = nc.dram_tensor("x1_scr", [T, H], F32, kind="Internal").ap()
    x1n_scr = nc.dram_tensor("x1n_scr", [T, H], BF16, kind="Internal").ap()
    xs_in = nc.dram_tensor("xs_in", [128, 16], F32, kind="Internal").ap()
    xs_out = nc.dram_tensor("xs_out", [128, 16], F32, kind="Internal").ap()
    ab_in = nc.dram_tensor("ab_in", [1, 4], F32, kind="Internal").ap()
    ab_out = nc.dram_tensor("ab_out", [1, 4], F32, kind="Internal").ap()
    t1sh = nc.dram_tensor("t1sh", [256, HIDP], BF16, kind="Internal").ap()
    t3sh = nc.dram_tensor("t3sh", [256, HIDP], BF16, kind="Internal").ap()
    t2sh = nc.dram_tensor("t2sh", [688, H], BF16, kind="Internal").ap()
    t1T = nc.dram_tensor("t1T", [H, HIDP], BF16, kind="Internal").ap()
    t3T = nc.dram_tensor("t3T", [H, HIDP], BF16, kind="Internal").ap()
    t2T = nc.dram_tensor("t2T", [HIDP, H], BF16, kind="Internal").ap()
    t2B = nc.dram_tensor("t2B", [4 * HIDP, 512], BF16, kind="Internal").ap()

    es = ExitStack()
    sem_names = ["sT", "sV", "sA", "sC", "sW", "sLc", "sGc", "sP", "sQw",
                 "sQ2", "sB2",
                 "sXa", "sXb", "sBa", "sBb", "sW13a", "sW13b", "sW2a", "sW2b",
                 "sW2c", "sW2d",
                 "sWoa", "sWob", "sQa", "sQb", "sFa", "sFb", "sOa", "sOb",
                 "sNa", "sNb"]
    sem = {n: es.enter_context(nc.semaphore(n)) for n in sem_names}
    cnt = {n: 0 for n in sem_names}
    cur = {e: {} for e in "TVASG"}

    def w(ec, eng, s, val=None):
        v = cnt[s] if val is None else val
        if v <= 0:
            return
        if cur[ec].get(s, -1) < v:
            eng.wait_ge(sem[s], v)
            cur[ec][s] = v

    def inc(instr, s, amt=1):
        if s in ("sV", "sA"):
            return cnt[s]
        instr.then_inc(sem[s], amt)
        cnt[s] += amt
        return cnt[s]

    def dma_settled(eng, s):
        if cnt[s]:
            eng.wait_ge(sem[s], cnt[s])

    PB = [es.enter_context(nc.psum_tensor(f"pb{i}", [128, 512], F32))
          for i in range(8)]
    bankg = {i: [] for i in range(8)}

    def bank_wait(t, i):
        for (s, v) in bankg[i]:
            w("T", t, s, v)
        bankg[i] = []

    id32 = es.enter_context(nc.sbuf_tensor("id32", [128, 128], F32))
    ones32 = es.enter_context(nc.sbuf_tensor("ones32", [128, 128], F32))
    onesb = es.enter_context(nc.sbuf_tensor("onesb", [128, 1], BF16))
    onesr = es.enter_context(nc.sbuf_tensor("onesr", [1, 128], BF16))
    sc_b = es.enter_context(nc.sbuf_tensor("sc_b", [128, 14], F32))
    stats = es.enter_context(nc.sbuf_tensor("stats", [128, 8], F32))
    scr2k = es.enter_context(nc.sbuf_tensor("scr2k", [128, H], F32))
    qrow = es.enter_context(nc.sbuf_tensor("qrow", [1, 8], F32))
    epsc = es.enter_context(nc.sbuf_tensor("epsc", [128, 1], F32))

    blk = es.enter_context(nc.Block())

    _CHAIN_OPS = {
        "tensor_scalar", "scalar_tensor_tensor", "tensor_tensor",
        "tensor_copy", "reduce_max", "reduce_sum", "tensor_reduce",
        "tensor_single_scalar", "tensor_scalar_min", "tensor_scalar_mul",
        "tensor_scalar_add", "tensor_add", "tensor_sub", "tensor_mul",
        "reciprocal", "memset", "memzero", "activation",
    }

    class ChainEng:

        def __init__(self, eng, sname):
            self._eng = eng
            self._sname = sname

        def __getattr__(self, name):
            fn = getattr(self._eng, name)
            if name not in _CHAIN_OPS:
                return fn
            sname = self._sname

            def wrapped(*a, **k):
                self._eng.wait_ge(sem[sname], cnt[sname])
                instr = fn(*a, **k)
                instr.then_inc(sem[sname], 1)
                cnt[sname] += 1
                return instr
            return wrapped

    def S_(fn):
        blk.sync(fn)

    def G_(fn):
        blk.gpsimd(fn)

    def P_(fn):
        blk.gpsimd(lambda g: fn(ChainEng(g, "sP")))

    def VP_(fn):
        blk.vector(lambda v: fn(ChainEng(v, "sP")))

    def T_(fn):
        blk.tensor(fn)

    def V_(fn):
        blk.vector(lambda v: fn(ChainEng(v, "sV")))

    def A_(fn):
        blk.scalar(lambda a: fn(ChainEng(a, "sA")))

    ESEC = {"V": V_, "A": A_}
    ESEM = {"V": "sV", "A": "sA"}

    def mm_evac_loop(n_iter, banks, emit_mm, emit_evac, evac_eng="A",
                     t_waits=None):
        esem = ESEM[evac_eng]
        e_base = cnt[esem]
        t_base = cnt["sT"]
        nb = len(banks)

        def tp(t):
            if t_waits:
                for (s, v) in t_waits:
                    w("T", t, s, v)
            for i in range(n_iter):
                b = banks[i % nb]
                if i >= nb:
                    w("T", t, esem, e_base + (i - nb) + 1)
                else:
                    bank_wait(t, b)
                emit_mm(t, i, b)
        T_(tp)
        assert cnt["sT"] == t_base + n_iter, "emit_mm must inc sT once"

        def ep(e):
            for i in range(n_iter):
                b = banks[i % nb]
                w(evac_eng, e, "sT", t_base + i + 1)
                emit_evac(e, i, b)
        ESEC[evac_eng](ep)
        assert cnt[esem] == e_base + n_iter, "emit_evac must inc once"
        for i in range(max(0, n_iter - nb), n_iter):
            bankg[banks[i % nb]].append((esem, e_base + i + 1))
        return cnt[esem]

    def ln_v1(v, src_ap, n, p=128):
        v.tensor_scalar(scr2k[0:p, 0:n], src_ap, 1.0, None, ALU.mult, ALU.add,
                        accum_out=stats[0:p, 0:1])
        v.scalar_tensor_tensor(scr2k[0:p, 0:n], src_ap, 1.0, src_ap,
                               ALU.mult, ALU.mult, accum_out=stats[0:p, 1:2])
        invn = 1.0 / float(n)
        v.tensor_scalar(stats[0:p, 2:3], stats[0:p, 0:1], invn, None, ALU.mult)
        v.tensor_scalar(stats[0:p, 3:4], stats[0:p, 1:2], invn, None, ALU.mult)
        v.tensor_tensor(stats[0:p, 6:7], stats[0:p, 2:3], stats[0:p, 2:3],
                        ALU.mult)
        return v.tensor_tensor(stats[0:p, 3:4], stats[0:p, 3:4],
                               stats[0:p, 6:7], ALU.subtract)

    def ln_full(src_ap, n, pre_v=None, p=128):
        def v1(v):
            if pre_v is not None:
                pre_v(v)
            inc(ln_v1(v, src_ap, n, p=p), "sV")
        V_(v1)
        stm = cnt["sV"]

        def a1(a, stm=stm):
            w("A", a, "sV", stm)
            inc(a.activation(stats[0:p, 4:5], stats[0:p, 3:4], AF.Sqrt,
                             bias=epsc[0:p, 0:1]), "sA")
        A_(a1)
        sqm = cnt["sA"]

        def v2(v, sqm=sqm):
            w("V", v, "sA", sqm)
            v.reciprocal(stats[0:p, 5:6], stats[0:p, 4:5])
        V_(v2)

    att = ExitStack()
    mask_sb = att.enter_context(nc.sbuf_tensor("mask_sb", [GR, KEYN], BF16))
    bqs = att.enter_context(nc.sbuf_tensor("bqs", [128, 16], F32))
    bkc = att.enter_context(nc.sbuf_tensor("bkc", [128, 16], F32))
    rows = att.enter_context(nc.sbuf_tensor("rows", [1, 2 * H], F32))
    bvbf = att.enter_context(nc.sbuf_tensor("bvbf", [1, H], BF16))
    xnsT = att.enter_context(nc.sbuf_tensor("xnsT", [128, 16], F32))
    xnsTb = att.enter_context(nc.sbuf_tensor("xnsTb", [128, 16], BF16))
    VSTb = att.enter_context(nc.sbuf_tensor("VSTb", [128, 16], BF16))
    QT = att.enter_context(nc.sbuf_tensor("QT", [128, NH * NM], BF16))
    ctxT = att.enter_context(nc.sbuf_tensor("ctxT", [128, 16 * 206], BF16))
    qacc = att.enter_context(nc.sbuf_tensor("qacc", [128, 4], F32))
    qtiles = []
    for _mi, _ap in ((0, w1s), (1, w3s)):
        for _ht in range(43):
            _r0 = _ht * 128
            qtiles.append((_mi, _ap, _r0, min(128, HID - _r0), 256))
    for _dt in range(16):
        qtiles.append((2, w2s, _dt * 128, 128, 688))
    q1tiles = []
    for _mi, _ap in ((0, w1s), (1, w3s)):
        for _ht in range(43):
            _r0 = _ht * 128
            q1tiles.append((_mi, _ap, _r0, min(128, HID - _r0), 0, 256))
    for _dt in range(16):
        q1tiles.append((2, w2s, _dt * 128, 128, 0, 344))
        q1tiles.append((2, w2s, _dt * 128, 128, 344, 344))
    q_state = {}

    q1_lptr = [0]

    def emit_q1_loads(kmax):
        n = len(q1tiles)
        j0 = q1_lptr[0]
        j1 = min(n, j0 + kmax)
        if j0 >= j1:
            return
        q1_lptr[0] = j1

        def qp1(g, j0=j0, j1=j1):
            if j0 > 0:
                g.wait_ge(sem["sQa"], 16 * j0)
            for j in range(j0, j1):
                mi, ap_, r0, nr, c0, ncol = q1tiles[j]
                r = j % 8
                if j >= 8:
                    g.wait_ge(sem["sP"], 2 * j - 13)
                inc(g.dma_start(out=qtl[r][0:nr, 0:ncol],
                                in_=ap_[r0:r0 + nr, c0:c0 + ncol]),
                    "sQa", 16)
        G_(qp1)

    q1_ptr = [0]

    def emit_q1_reduces(kmax):
        n = len(q1tiles)
        i0 = q1_ptr[0]
        i1 = min(n, i0 + kmax)
        if i0 >= i1:
            return
        q1_ptr[0] = i1

        def qred(v, i0=i0, i1=i1):
            if i0 == 0:
                v.memset(qacc[:], 0.0)
            v._eng.wait_ge(sem["sQa"], 16 * i1)
            for i in range(i0, i1):
                mi, ap_, r0, nr, c0, ncol = q1tiles[i]
                r = i % 8
                v.reduce_sum(qacc[0:nr, 3:4], qtl[r][0:nr, 0:ncol],
                             axis=AXX, apply_absolute_value=True)
                v.tensor_tensor(qacc[0:nr, mi:mi + 1],
                                qacc[0:nr, mi:mi + 1],
                                qacc[0:nr, 3:4], ALU.add)
        VP_(qred)

    def emit_q1_chain():
        emit_q1_loads(len(q1tiles))
        emit_q1_reduces(len(q1tiles))

        def qtot(t, pm=cnt["sP"]):
            w("T", t, "sP", pm)
            bank_wait(t, 7)
            for mi in range(3):
                m = t.matmul(PB[7][0:1, mi:mi + 1], qacc[:, mi:mi + 1],
                             ones32[0:128, 0:1], start=True, stop=True)
            inc(m, "sT")
        T_(qtot)
        qtm = cnt["sT"]

        def qrowv(v, qtm=qtm):
            w("V", v, "sT", qtm)
            v.memset(qrow[0:1, 0:4], 0.0)
            v.tensor_scalar(qrow[0:1, 0:1], PB[7][0:1, 0:1], 1.0 / W1CNT,
                            None, ALU.mult)
            v.tensor_scalar(qrow[0:1, 1:2], PB[7][0:1, 1:2], 1.0 / W1CNT,
                            None, ALU.mult)
            v.tensor_scalar(qrow[0:1, 2:3], PB[7][0:1, 2:3], 1.0 / W2CNT,
                            None, ALU.mult)
            bankg[7].append(("sV", cnt["sV"]))
        V_(qrowv)
        qrm = cnt["sV"]

        def qar(g, qrm=qrm):
            w("G", g, "sV", qrm)
            w("G", g, "sGc", cnt["sGc"])
            inc(g.dma_start(out=ab_in[:, :], in_=qrow[0:1, 0:4]), "sGc", 16)
            g.wait_ge(sem["sGc"], cnt["sGc"])
            inc(g.collective_compute(
                "AllReduce", ALU.add, replica_groups=[list(range(8))],
                ins=[ab_in.opt()], outs=[ab_out.opt()]), "sC")
            g.wait_ge(sem["sC"], cnt["sC"])
            inc(g.dma_start(out=qrow[0:1, 0:4], in_=ab_out[:, :]), "sGc", 16)
        G_(qar)
        arm = cnt["sGc"]

        def qsbt(t, arm=arm):
            w("T", t, "sGc", arm)
            bank_wait(t, 7)
            inc(t.matmul(PB[7][0:128, 0:4], ones32[0:1, 0:128],
                         qrow[0:1, 0:4], start=True, stop=True), "sT")
        T_(qsbt)
        qsb_m = cnt["sT"]

        def qsbv(v, qsb_m=qsb_m):
            w("V", v, "sT", qsb_m)
            v.tensor_copy(sc_b[:, 0:3], PB[7][0:128, 0:3])
            bankg[7].append(("sV", cnt["sV"]))
            v.tensor_scalar(sc_b[:, 3:6], sc_b[:, 0:3], 0.5, None, ALU.mult)
            v.tensor_scalar(sc_b[:, 6:9], sc_b[:, 0:3], -0.5, None, ALU.mult)
            v.tensor_tensor(sc_b[:, 9:10], sc_b[:, 2:3], sc_b[:, 2:3],
                            ALU.mult)
            v.tensor_scalar(sc_b[:, 10:13], sc_b[:, 0:3], -1.0, None, ALU.mult)
        V_(qsbv)
        q_state["scb_mark"] = cnt["sV"]

    def emit_quant_pass2():
        qtl2 = [qt2, qt3]
        scb_mark = q_state["scb_mark"]
        pool_marks = {}
        trm_prev = [0]
        wprev = [0]
        tidx = [0]
        passes = []
        for _mi, _ap, _sh in ((0, w1s, t1sh), (1, w3s, t3sh)):
            passes.append((_mi, _ap, _sh, 0, 22, 0, 2816))
            passes.append((_mi, _ap, _sh, 22, 21, 2816, 2688))
        passes.append((2, w2s, t2sh, 0, 8, 0, 1024))
        passes.append((2, w2s, t2sh, 8, 8, 1024, 1024))
        for (mi, ap_, sh_ap, k0, nt, c0, cw) in passes:
            if mi < 2:
                def qmz(a, wp=wprev[0]):
                    if wp:
                        w("A", a, "sW", wp)
                    a.memzero(asm[:, 0:5632])
                A_(qmz)
            for k in range(nt):
                i = tidx[0]
                tidx[0] += 1
                r = i % 2
                sq_ = "sQ2" if r == 0 else "sQb"
                if mi < 2:
                    r0 = (k0 + k) * 128
                    nr, ncol = min(128, HID - r0), 256
                else:
                    r0, nr, ncol = (k0 + k) * 128, 128, 688

                def qld2(s, r=r, ap_=ap_, r0=r0, nr=nr, ncol=ncol, sq_=sq_,
                         pm=pool_marks.get(i - 2, 0)):
                    if pm:
                        w("S", s, "sV", pm)
                    dma_settled(s, sq_)
                    inc(s.dma_start(out=qtl2[r][0:nr, 0:ncol],
                                    in_=ap_[r0:r0 + nr, :]), sq_, 16)
                S_(qld2)
                ldm = cnt[sq_]

                def qq(v, r=r, mi=mi, nr=nr, ncol=ncol, ldm=ldm, sq_=sq_,
                       trm=trm_prev[0], scb=scb_mark):
                    w("V", v, sq_, ldm)
                    w("V", v, "sV", scb)
                    if trm:
                        w("V", v, "sT", trm)
                    v.tensor_single_scalar(qa[0:nr, 0:ncol],
                                           qtl2[r][0:nr, 0:ncol],
                                           sc_b[0:nr, 3 + mi:4 + mi],
                                           ALU.is_gt)
                    v.scalar_tensor_tensor(tq[0:nr, 0:ncol],
                                           qtl2[r][0:nr, 0:ncol],
                                           sc_b[0:nr, 6 + mi:7 + mi],
                                           qa[0:nr, 0:ncol],
                                           ALU.is_lt, ALU.subtract)
                V_(qq)
                pool_marks[i] = cnt["sV"]

                if mi < 2:
                    def qtr(t, nr=nr, pm=pool_marks[i]):
                        w("T", t, "sV", pm)
                        for db in range(2):
                            bank_wait(t, db)
                            m = t.transpose(PB[db][0:128, 0:nr],
                                            tq[0:nr, db * 128:(db + 1) * 128],
                                            id32[0:nr, 0:nr])
                        inc(m, "sT")
                    T_(qtr)
                    trm_prev[0] = cnt["sT"]

                    def qte(a, k=k, nr=nr, trm=cnt["sT"]):
                        w("A", a, "sT", trm)
                        for db in range(2):
                            m = a.activation(
                                asm[:, db * 2816 + k * 128:
                                    db * 2816 + k * 128 + nr],
                                PB[db][0:128, 0:nr], AF.Copy)
                        ia = inc(m, "sA")
                        bankg[0].append(("sA", ia))
                        bankg[1].append(("sA", ia))
                    A_(qte)
                else:
                    a_base2 = cnt["sA"]
                    t_base2 = cnt["sT"]

                    def q2tr(t, pm=pool_marks[i], a_base2=a_base2):
                        w("T", t, "sV", pm)
                        for hb in range(6):
                            cntk = min(128, 688 - hb * 128)
                            if hb >= 2:
                                w("T", t, "sA", a_base2 + hb - 1)
                            else:
                                bank_wait(t, hb % 2)
                            inc(t.transpose(
                                PB[hb % 2][0:cntk, 0:128],
                                tq[0:128, hb * 128:hb * 128 + cntk],
                                id32[0:128, 0:128]), "sT")
                    T_(q2tr)
                    trm_prev[0] = cnt["sT"]

                    def q2te(a, k=k, t_base2=t_base2, wp=wprev[0]):
                        if k == 0 and wp:
                            w("A", a, "sW", wp)
                        for hb in range(6):
                            cntk = min(128, 688 - hb * 128)
                            w("A", a, "sT", t_base2 + hb + 1)
                            inc(a.activation(
                                asm[0:cntk, hb * 1024 + k * 128:
                                    hb * 1024 + (k + 1) * 128],
                                PB[hb % 2][0:cntk, 0:128], AF.Copy), "sA")
                        bankg[0].append(("sA", cnt["sA"]))
                        bankg[1].append(("sA", cnt["sA"]))
                    A_(q2te)
            pam = cnt["sA"]

            def qwr(g, mi=mi, sh_ap=sh_ap, c0=c0, cw=cw, pam=pam):
                w("G", g, "sA", pam)
                w("G", g, "sW")
                if mi < 2:
                    for db in range(2):
                        inc(g.dma_start(
                            out=sh_ap[db * 128:(db + 1) * 128, c0:c0 + cw],
                            in_=asm[:, db * 2816:db * 2816 + cw]), "sW", 16)
                else:
                    for hb in range(6):
                        cntk = min(128, 688 - hb * 128)
                        inc(g.dma_start(
                            out=sh_ap[hb * 128:hb * 128 + cntk,
                                      c0:c0 + 1024],
                            in_=asm[0:cntk, hb * 1024:(hb + 1) * 1024]),
                            "sW", 16)
            G_(qwr)
            wprev[0] = cnt["sW"]

        def qag(g):
            w("G", g, "sW")
            for (si, so) in ((t1sh, t1T), (t3sh, t3T), (t2sh, t2T)):
                inc(g.collective_compute(
                    "AllGather", ALU.bypass, replica_groups=[list(range(8))],
                    ins=[si.opt()], outs=[so.opt()]), "sC")
        G_(qag)
        q_state["ag_done"] = cnt["sC"]
        q_state["p2_end"] = [("sA", cnt["sA"]), ("sP", cnt["sP"]),
                             ("sT", cnt["sT"]), ("sW", cnt["sW"]),
                             ("sQa", cnt["sQa"]), ("sQb", cnt["sQb"])]

    def p0(s):
        for (dst, src) in ((id32[:], id32in[:, :]), (ones32[:], ones32in[:, :]),
                           (onesb[:], onesbin[:, :]), (onesr[:], onesrbin[:, :]),
                           (mask_sb[:], maskin[:, :])):
            inc(s.dma_start(out=dst, in_=src), "sLc", 16)
        for c in range(16):
            inc(s.dma_start(out=bqs[:, c:c + 1], in_=bq[c * 128:(c + 1) * 128]),
                "sLc", 16)
            inc(s.dma_start(out=bkc[:, c:c + 1], in_=bk[c * 128:(c + 1) * 128]),
                "sLc", 16)
        inc(s.dma_start(out=rows[0:1, 0:H], in_=bv[:]), "sLc", 16)
    S_(p0)
    p0_mark = cnt["sLc"]

    def p0g(g):
        inc(g.dma_start(out=bvbf[0:1, :], in_=bv[:]), "sGc", 16)
    G_(p0g)
    bvbf_mark = cnt["sGc"]

    def p0v(v):
        w("V", v, "sLc", p0_mark)
        v.memset(epsc[:], EPS)
        v.tensor_scalar(bqs[:], bqs[:], RSQ, None, ALU.mult)
        inc(v.tensor_scalar(rows[0:1, H:2 * H], rows[0:1, 0:H],
                            float(S_FULL), None, ALU.mult), "sV")
    V_(p0v)
    sbv_mark = cnt["sV"]

    p2x = ExitStack()
    xnT = p2x.enter_context(nc.sbuf_tensor("xnT", [128, 16 * T], BF16,
                                           is_transpose_dest=True))
    qwstk = ExitStack()
    wqb4 = qwstk.enter_context(nc.sbuf_tensor("wqb4", [128, 2 * 8192], BF16))
    qstk = ExitStack()
    qt2 = qstk.enter_context(nc.sbuf_tensor("qt2", [128, 688], F32))
    qt3 = qstk.enter_context(nc.sbuf_tensor("qt3", [128, 688], F32))
    qa = qstk.enter_context(nc.sbuf_tensor("qa", [128, 688], F32))
    tq = qstk.enter_context(nc.sbuf_tensor("tq", [128, 688], F32))
    asm = qstk.enter_context(nc.sbuf_tensor("asm", [128, 6144], BF16))
    qt8 = qstk.enter_context(nc.sbuf_tensor("qt8", [128, 8 * 344], F32))
    qtl = [qt8[:, _r * 344:(_r + 1) * 344] for _r in range(8)]

    wqm = []

    def wq_load(g, qh, sl, twait=0):
        if twait:
            w("G", g, "sT", twait)
        inc(g.dma_start(
            out=wqb4[:, sl * 8192:(sl + 1) * 8192].rearrange(
                "p (kc n) -> p kc n", n=512),
            in_=wq[:, qh * 512:(qh + 1) * 512].rearrange(
                "(kc p) n -> p kc n", p=128)), "sQw", 16)
        wqm.append(cnt["sQw"])

    def wq01(g):
        wq_load(g, 0, 0)
        g.wait_ge(sem["sQw"], 16)
        wq_load(g, 1, 1)
    G_(wq01)

    with (
        nc.sbuf_tensor("xt0", [128, H], F32) as xt0,
        nc.sbuf_tensor("xt1", [128, H], F32) as xt1,
        nc.sbuf_tensor("xnb0", [128, H], BF16) as xnb0,
        nc.sbuf_tensor("xnb1", [128, H], BF16) as xnb1,
    ):
        xt = [xt0, xt1]
        xnb = [xnb0, xnb1]
        xread = [[], []]
        xnread = [[], []]
        for ti in range(17):
            r = ti % 2
            row0 = ti * 128
            emit_q1_loads(7)

            sx = "sXa" if r == 0 else "sXb"

            def p1s(s, r=r, row0=row0, rd=tuple(xread[r]), sx=sx):
                for (sn, sv) in rd:
                    s.wait_ge(sem[sn], sv)
                dma_settled(s, sx)
                inc(s.dma_start(out=xt[r][:], in_=x_sh[row0:row0 + 128, :]),
                    sx, 16)
            S_(p1s)
            ld = cnt[sx]

            def pre(v, ld=ld, sx=sx):
                w("V", v, sx, ld)
            ln_full(xt[r][:], H, pre_v=pre)

            def p1v2(v, r=r, rd=tuple(xnread[r])):
                for (sn, sv) in rd:
                    w("V", v, sn, sv)
                inc(v.tensor_scalar(xnb[r][:], xt[r][:], stats[:, 2:3],
                                    stats[:, 5:6], ALU.subtract, ALU.mult), "sV")
            V_(p1v2)
            xnm = cnt["sV"]
            xread[r] = [("sV", xnm)]

            def p1t(t, r=r, ti=ti, xnm=xnm):
                w("T", t, "sV", xnm)
                rc = 128 if ti < 16 else 2
                for n in range(4):
                    if ti == 0:
                        bank_wait(t, n)
                    m = t.matmul(PB[n][0:1, 0:512], onesb[0:rc, 0:1],
                                 xnb[r][0:rc, n * 512:(n + 1) * 512],
                                 start=(ti == 0), stop=(ti == 16))
                inc(m, "sT")
            T_(p1t)
            tsm = cnt["sT"]

            sn_ = "sNa" if r == 0 else "sNb"

            def p1g(g, r=r, row0=row0, xnm=xnm, tsm=tsm, sn_=sn_):
                w("G", g, "sV", xnm)
                w("G", g, "sT", tsm)
                dma_settled(g, sn_)
                inc(g.dma_start(out=xn_scr[row0:row0 + 128, :], in_=xnb[r][:]),
                    sn_, 16)
            G_(p1g)
            xnread[r] = [(sn_, cnt[sn_]), ("sT", tsm)]
            emit_q1_reduces(7)
        xn_written = [("sNa", cnt["sNa"]), ("sNb", cnt["sNb"])]
        p1_marks = [("sV", cnt["sV"]), ("sT", cnt["sT"]), ("sA", cnt["sA"]),
                    ("sNa", cnt["sNa"]), ("sNb", cnt["sNb"])]

        xnsum_t = cnt["sT"]

        def p1xv(v, xnsum_t=xnsum_t):
            w("V", v, "sT", xnsum_t)
            for n in range(4):
                m = v.tensor_copy(rows[0:1, n * 512:(n + 1) * 512],
                                  PB[n][0:1, 0:512])
            iv = inc(m, "sV")
            for b in range(4):
                bankg[b].append(("sV", iv))
        V_(p1xv)
        xsrow = cnt["sV"]

        def p1xt(t, xsrow=xsrow):
            w("T", t, "sV", xsrow)
            bank_wait(t, 6)
            for c in range(16):
                m = t.transpose(PB[6][0:128, c:c + 1],
                                rows[0:1, c * 128:(c + 1) * 128],
                                id32[0:1, 0:1])
            inc(m, "sT")
        T_(p1xt)
        xst = cnt["sT"]

        def p1xv2(v, xst=xst):
            w("V", v, "sT", xst)
            iv = inc(v.tensor_copy(xnsT[:], PB[6][0:128, 0:16]), "sV")
            bankg[6].append(("sV", iv))
        V_(p1xv2)
        xstv = cnt["sV"]

        def p1g2(g, xstv=xstv):
            w("G", g, "sV", xstv)
            w("G", g, "sGc", cnt["sGc"])
            inc(g.dma_start(out=xs_in[:, :], in_=xnsT[:]), "sGc", 16)
            g.wait_ge(sem["sGc"], cnt["sGc"])
            inc(g.collective_compute(
                "AllReduce", ALU.add,
                replica_groups=[[0, 1], [2, 3], [4, 5], [6, 7]],
                ins=[xs_in.opt()], outs=[xs_out.opt()]), "sC")
            g.wait_ge(sem["sC"], cnt["sC"])
            inc(g.dma_start(out=xnsT[:], in_=xs_out[:, :]), "sGc", 16)
        G_(p1g2)
        ar1 = cnt["sGc"]

        def p1v3(v, ar1=ar1):
            w("V", v, "sGc", ar1)
            inc(v.tensor_copy(xnsTb[:], xnsT[:]), "sV")
        V_(p1v3)
        xnsb_mark = cnt["sV"]

    emit_q1_chain()

    if True:
        def p2s(s):
            for (sn, sv) in xn_written:
                s.wait_ge(sem[sn], sv)
            for (sn, sv) in p1_marks:
                w("S", s, sn, sv)
            w("S", s, "sLc", cnt["sLc"])
            for dc in range(16):
                inc(s.dma_start(out=xnT[:, dc * T:(dc + 1) * T],
                                in_=xn_scr[:, dc * 128:(dc + 1) * 128],
                                transpose=True), "sLc", 16)
        S_(p2s)
        xnT_mark = cnt["sLc"]

        a_base_q = cnt["sA"]
        t_base_q = cnt["sT"]

        def qpt(t):
            w("T", t, "sLc", xnT_mark)
            for hc in range(16):
                qh, b = hc // 4, hc % 4
                sl = qh % 2
                if hc % 4 == 0:
                    w("T", t, "sQw", wqm[qh] if qh < len(wqm)
                      else 16 * (qh + 1))
                if hc >= 4:
                    w("T", t, "sA", a_base_q + hc - 3)
                else:
                    bank_wait(t, b)
                for kc in range(16):
                    m = t.matmul(
                        PB[b][0:128, 0:NM],
                        wqb4[:, sl * 8192 + kc * 512 + (hc % 4) * 128:
                             sl * 8192 + kc * 512 + (hc % 4) * 128 + 128],
                        xnT[:, kc * T:kc * T + 2041:10],
                        start=(kc == 0), stop=(kc == 15))
                inc(m, "sT")
        T_(qpt)

        def qpa(a):
            for hc in range(16):
                b = hc % 4
                w("A", a, "sT", t_base_q + hc + 1)
                ia = inc(a.activation(QT[:, hc * NM:(hc + 1) * NM],
                                      PB[b][0:128, 0:NM], AF.Identity,
                                      bias=bqs[:, hc:hc + 1], scale=RSQ),
                         "sA")
                bankg[b].append(("sA", ia))
        A_(qpa)
        q_done = cnt["sA"]

        def wq23(g):
            dma_settled(g, "sQw")
            wq_load(g, 2, 0, t_base_q + 4)
            g.wait_ge(sem["sQw"], 48)
            wq_load(g, 3, 1, t_base_q + 8)
        G_(wq23)

        emit_quant_pass2()
        ag_done = q_state["ag_done"]
        qstk.close()
        qwstk.close()

        def qbarV(v):
            w("V", v, "sP")
            w("V", v, "sW")
        V_(qbarV)

        at2 = ExitStack()
        ctx_g = at2.enter_context(nc.sbuf_tensor("ctx_g", [GR, 1024], F32))
        Xs = at2.enter_context(nc.sbuf_tensor("Xs", [GR, 2 * GK], F32))
        Es = Xs
        ETs = at2.enter_context(nc.sbuf_tensor("ETs", [128, 2 * 4 * GR], BF16))
        nm16 = at2.enter_context(nc.sbuf_tensor("nm16", [GR, 16], F32))
        tmpc = scr2k
        VS_B = at2.enter_context(nc.sbuf_tensor("VS_B", [64, H], F32))

        vs_done = 0
        for hp in range(2):
            hps = ExitStack()
            KTh = hps.enter_context(nc.sbuf_tensor(f"KT{hp}",
                                                   [128, 8 * KEYN], BF16))
            with nc.sbuf_tensor(f"wkb{hp}", [128, 16 * 1024], BF16) as wkb:
                def pkg(g, hp=hp, pa=cnt["sA"], pt=cnt["sT"]):
                    w("G", g, "sA", pa)
                    w("G", g, "sT", pt)
                    w("G", g, "sW")
                    w("G", g, "sQa")
                    w("G", g, "sQb")
                    w("G", g, "sQ2")
                    w("G", g, "sQw")
                    w("G", g, "sP")
                    w("G", g, "sGc", cnt["sGc"])
                    for kc in range(16):
                        inc(g.dma_start(
                            out=wkb[:, kc * 1024:(kc + 1) * 1024],
                            in_=wk[kc * 128:(kc + 1) * 128,
                                   hp * 1024:(hp + 1) * 1024]), "sGc", 16)
                G_(pkg)
                wk_mark = cnt["sGc"]

                def kmm(t, i, b, hp=hp):
                    hl, ns = divmod(i, 5)
                    for kc in range(16):
                        m = t.matmul(
                            PB[b][0:128, 0:GK],
                            wkb[:, kc * 1024 + hl * 128:
                                kc * 1024 + (hl + 1) * 128],
                            xnT[:, kc * T + ns * GK:kc * T + (ns + 1) * GK],
                            start=(kc == 0), stop=(kc == 15))
                    inc(m, "sT")

                def kev(a, i, b, hp=hp):
                    hl, ns = divmod(i, 5)
                    inc(a.activation(
                        KTh[:, hl * KEYN + ns * GK:hl * KEYN + (ns + 1) * GK],
                        PB[b][0:128, 0:GK], AF.Identity,
                        bias=bkc[:, hp * 8 + hl:hp * 8 + hl + 1]), "sA")

                mm_evac_loop(40, [0, 1, 2, 3], kmm, kev, "A",
                             t_waits=[("sGc", wk_mark), ("sLc", xnT_mark)])

            wvb = hps.enter_context(nc.sbuf_tensor(f"wvb{hp}",
                                                   [128, 16 * 1024], BF16))

            def pvg(g, hp=hp, pa=cnt["sA"], pt=cnt["sT"]):
                w("G", g, "sA", pa)
                w("G", g, "sT", pt)
                w("G", g, "sGc", cnt["sGc"])
                for kc in range(16):
                    inc(g.dma_start(
                        out=wvb[:, kc * 1024:(kc + 1) * 1024],
                        in_=wv[kc * 128:(kc + 1) * 128,
                               hp * 1024:(hp + 1) * 1024]), "sGc", 16)
            G_(pvg)
            wv_mark = cnt["sGc"]

            def vsmm(t, n, b, hp=hp):
                for kc in range(16):
                    m = t.matmul(PB[b][0:1, 0:512], xnsTb[:, kc:kc + 1],
                                 wvb[:, kc * 1024 + n * 512:
                                     kc * 1024 + (n + 1) * 512],
                                 start=(kc == 0), stop=(kc == 15))
                inc(m, "sT")

            def vsev(v, n, b, hp=hp):
                off = H + hp * 1024 + n * 512
                inc(v.scalar_tensor_tensor(
                    rows[0:1, off:off + 512], rows[0:1, off:off + 512], 1.0,
                    PB[b][0:1, 0:512], ALU.mult, ALU.add), "sV")

            mm_evac_loop(2, [4, 5], vsmm, vsev, "V",
                         t_waits=[("sGc", wv_mark), ("sV", xnsb_mark),
                                  ("sV", sbv_mark)])
            vs_done = cnt["sV"]

            def vbmm(t, n, b, hp=hp):
                off = H + hp * 1024 + n * 512
                inc(t.matmul(PB[b][0:64, 0:512], ones32[0:1, 0:64],
                             rows[0:1, off:off + 512], start=True, stop=True),
                    "sT")

            def vbev(v, n, b, hp=hp):
                off = hp * 1024 + n * 512
                inc(v.tensor_copy(VS_B[:, off:off + 512], PB[b][0:64, 0:512]),
                    "sV")

            mm_evac_loop(2, [4, 5], vbmm, vbev, "V",
                         t_waits=[("sV", vs_done)])

            Vg = hps.enter_context(nc.sbuf_tensor(f"Vg{hp}", [128, 4096], BF16))
            ctxg_read = []
            for g_i in range(NG):
                def vgmm(t, i, b, g_i=g_i, hp=hp):
                    c, n = divmod(i, 2)
                    cntk = VCHUNK[c]
                    key0 = GK * g_i + 128 * c
                    for kc in range(16):
                        t.matmul(
                            PB[b][0:cntk, 0:512],
                            xnT[:, kc * T + key0:kc * T + key0 + cntk],
                            wvb[:, kc * 1024 + n * 512:kc * 1024 + (n + 1) * 512],
                            start=(kc == 0), stop=False)
                    m = t.matmul(
                        PB[b][0:cntk, 0:512], onesr[0:1, 0:cntk],
                        bvbf[0:1, hp * 1024 + n * 512:hp * 1024 + (n + 1) * 512],
                        start=False, stop=True)
                    inc(m, "sT")

                def vgev(v, i, b, g_i=g_i, hp=hp):
                    c, n = divmod(i, 2)
                    cntk = VCHUNK[c]
                    inc(v.tensor_copy(
                        Vg[0:cntk, c * 1024 + n * 512:c * 1024 + (n + 1) * 512],
                        PB[b][0:cntk, 0:512]), "sV")

                tw = [("sA", q_done), ("sGc", bvbf_mark)]
                mm_evac_loop(8, [4, 5], vgmm, vgev, "V", t_waits=tw)
                vg_mark = cnt["sV"]

                for hb in range(4):
                    hl2 = (2 * hb, 2 * hb + 1)

                    def hsc(t, hl2=hl2, g_i=g_i, hp=hp, vg_mark=vg_mark):
                        w("T", t, "sV", vg_mark)
                        for j, hl in enumerate(hl2):
                            h = hp * 8 + hl
                            bank_wait(t, j)
                            inc(t.matmul(
                                PB[j][0:GR, 0:GK],
                                QT[:, h * NM + GR * g_i:h * NM + GR * (g_i + 1)],
                                KTh[:, hl * KEYN + GK * g_i:
                                    hl * KEYN + GK * (g_i + 1)],
                                start=True, stop=True), "sT")
                    T_(hsc)
                    scm = cnt["sT"]

                    def hmx(v, scm=scm):
                        w("V", v, "sT", scm)
                        for j in range(2):
                            v.reduce_max(nm16[:, 8 * j:8 * j + 1],
                                         PB[j][0:GR, 0:GK], axis=AXX,
                                         negate=True)
                            m = v.tensor_scalar_min(nm16[:, 8 * j + 1:8 * j + 2],
                                                    nm16[:, 8 * j:8 * j + 1],
                                                    0.0)
                        inc(m, "sV")
                    V_(hmx)
                    mxm = cnt["sV"]

                    def hex_(a, mxm=mxm):
                        w("A", a, "sV", mxm)
                        for j in range(2):
                            a.activation(nm16[:, 8 * j + 2:8 * j + 3],
                                         nm16[:, 8 * j + 1:8 * j + 2], AF.Exp)
                            m = a.activation(Xs[0:GR, j * GK:(j + 1) * GK],
                                             PB[j][0:GR, 0:GK], AF.Exp,
                                             bias=nm16[0:GR, 8 * j + 1:8 * j + 2])
                        ia = inc(m, "sA")
                        bankg[0].append(("sA", ia))
                        bankg[1].append(("sA", ia))
                    A_(hex_)
                    xm = cnt["sA"]

                    def he(v, g_i=g_i, xm=xm):
                        w("V", v, "sA", xm)
                        for j in range(2):
                            m = v.scalar_tensor_tensor(
                                Es[0:GR, j * GK:(j + 1) * GK],
                                Xs[0:GR, j * GK:(j + 1) * GK],
                                nm16[0:GR, 8 * j + 2:8 * j + 3],
                                mask_sb[0:GR, GK * g_i:GK * (g_i + 1)],
                                ALU.subtract, ALU.mult)
                        inc(m, "sV")
                    V_(he)
                    em_ = cnt["sV"]

                    def htr(t, em_=em_):
                        w("T", t, "sV", em_)
                        bank_wait(t, 6)
                        for j in range(2):
                            for c in range(4):
                                cntk = VCHUNK[c]
                                m = t.transpose(
                                    PB[6][0:cntk, (j * 4 + c) * GR:
                                          (j * 4 + c + 1) * GR],
                                    Es[0:GR, j * GK + c * 128:
                                       j * GK + c * 128 + cntk],
                                    id32[0:GR, 0:GR])
                        inc(m, "sT")
                    T_(htr)
                    trm = cnt["sT"]

                    def het(v, trm=trm):
                        w("V", v, "sT", trm)
                        for j in range(2):
                            for c in range(4):
                                cntk = VCHUNK[c]
                                m = v.tensor_copy(
                                    ETs[0:cntk, (j * 4 + c) * GR:
                                        (j * 4 + c + 1) * GR],
                                    PB[6][0:cntk, (j * 4 + c) * GR:
                                          (j * 4 + c + 1) * GR])
                        iv = inc(m, "sV")
                        bankg[6].append(("sV", iv))
                    V_(het)
                    etm = cnt["sV"]

                    def hco(t, hl2=hl2, etm=etm):
                        w("T", t, "sV", etm)
                        for j in range(2):
                            bank_wait(t, 2 + j)
                            bank_wait(t, 4 + j)
                            hl = hl2[j]
                            for c in range(4):
                                cntk = VCHUNK[c]
                                t.matmul(
                                    PB[2 + j][0:GR, 0:128],
                                    ETs[0:cntk, (j * 4 + c) * GR:
                                        (j * 4 + c + 1) * GR],
                                    Vg[0:cntk, c * 1024 + hl * 128:
                                       c * 1024 + (hl + 1) * 128],
                                    start=(c == 0), stop=(c == 3))
                                me_ = t.matmul(
                                    PB[4 + j][0:GR, 0:1],
                                    ETs[0:cntk, (j * 4 + c) * GR:
                                        (j * 4 + c + 1) * GR],
                                    onesb[0:cntk, 0:1],
                                    start=(c == 0), stop=(c == 3))
                        inc(me_, "sT")
                    T_(hco)
                    com = cnt["sT"]

                    def hz(v, hl2=hl2, hp=hp, com=com):
                        w("V", v, "sT", com)
                        for j in range(2):
                            hl = hl2[j]
                            h = hp * 8 + hl
                            o = 8 * j
                            v.scalar_tensor_tensor(
                                nm16[:, o + 3:o + 4], nm16[:, o + 2:o + 3],
                                float(S_FULL), PB[4 + j][0:GR, 0:1],
                                ALU.mult, ALU.add)
                            v.reciprocal(nm16[:, o + 4:o + 5],
                                         nm16[:, o + 3:o + 4])
                            v.tensor_tensor(nm16[:, o + 5:o + 6],
                                            nm16[:, o + 2:o + 3],
                                            nm16[:, o + 4:o + 5], ALU.mult)
                            v.tensor_scalar(tmpc[0:GR, j * 128:(j + 1) * 128],
                                            PB[2 + j][0:GR, 0:128],
                                            nm16[:, o + 4:o + 5], None,
                                            ALU.mult)
                            m = v.scalar_tensor_tensor(
                                ctx_g[0:GR, hl * 128:(hl + 1) * 128],
                                VS_B[0:GR, h * 128:(h + 1) * 128],
                                nm16[0:GR, o + 5:o + 6],
                                tmpc[0:GR, j * 128:(j + 1) * 128],
                                ALU.mult, ALU.add)
                        iv = inc(m, "sV")
                        for b in (2, 3, 4, 5):
                            bankg[b].append(("sV", iv))
                    V_(hz)
                ctx_mark = cnt["sV"]

                a_base = cnt["sV"]
                t_base7 = cnt["sT"]

                def ctp(t, ctx_mark=ctx_mark, a_base=a_base):
                    w("T", t, "sV", ctx_mark)
                    bank_wait(t, 6)
                    bank_wait(t, 7)
                    for i in range(8):
                        if i >= 2:
                            w("T", t, "sV", a_base + i - 1)
                        inc(t.transpose(PB[6 + (i % 2)][0:128, 0:GR],
                                        ctx_g[0:GR, i * 128:(i + 1) * 128],
                                        id32[0:GR, 0:GR]), "sT")
                T_(ctp)

                def cap(v, g_i=g_i, hp=hp, t_base7=t_base7):
                    for i in range(8):
                        w("V", v, "sT", t_base7 + i + 1)
                        h = hp * 8 + i
                        inc(v.tensor_copy(
                            ctxT[:, h * 206 + g_i * GR:h * 206 + (g_i + 1) * GR],
                            PB[6 + (i % 2)][0:128, 0:GR]), "sV")
                V_(cap)
                bankg[6].append(("sV", cnt["sV"]))
                bankg[7].append(("sV", cnt["sV"]))
                ctxg_read = [("sT", cnt["sT"])]
            hps.close()
        at2.close()

        def vstt(t, vs_done=vs_done):
            w("T", t, "sV", vs_done)
            bank_wait(t, 6)
            for c in range(16):
                m = t.transpose(PB[6][0:128, c:c + 1],
                                rows[0:1, H + c * 128:H + (c + 1) * 128],
                                id32[0:1, 0:1])
            inc(m, "sT")
        T_(vstt)
        vst_m = cnt["sT"]

        def vstv(v, vst_m=vst_m):
            w("V", v, "sT", vst_m)
            iv = inc(v.tensor_scalar(VSTb[:], PB[6][0:128, 0:16],
                                     1.0 / float(S_FULL), None, ALU.mult), "sV")
            bankg[6].append(("sV", iv))
            inc(v.tensor_copy(ctxT[:, 205:16 * 206:206], VSTb[:]), "sV")
        V_(vstv)
        ctxT_done = cnt["sV"]
    p2x.close()

    p2_done = [("sA", cnt["sA"]), ("sV", cnt["sV"]), ("sT", cnt["sT"])]

    bcastB = att.enter_context(nc.sbuf_tensor("bcastB", [128, H], F32))
    anl_sb = att.enter_context(nc.sbuf_tensor("anl_sb", [128, 2 * H], F32))
    x1m_sb = att.enter_context(nc.sbuf_tensor("x1m_sb", [128, 2 * H], F32))
    x1nm_sb = att.enter_context(nc.sbuf_tensor("x1nm_sb", [128, 2 * H], BF16))
    xm_sb = att.enter_context(nc.sbuf_tensor("xm_sb", [128, 2 * H], F32))
    MC = [(0, 128), (1, 77)]

    def p2bar(v, p2d=tuple(p2_done)):
        for (sn, sv) in p2d:
            w("V", v, sn, sv)
    V_(p2bar)

    def pbo(s):
        s.wait_ge(sem["sT"], cnt["sT"])
        w("S", s, "sLc", cnt["sLc"])
        inc(s.dma_start(out=rows[0:1, 0:H], in_=bo[:]), "sLc", 16)
    S_(pbo)
    bo_ld = cnt["sLc"]

    def bomm(t, i, b):
        inc(t.matmul(PB[b][0:128, 0:512], ones32[0:1, 0:128],
                     rows[0:1, i * 512:(i + 1) * 512], start=True, stop=True),
            "sT")

    def boev(v, i, b):
        inc(v.tensor_copy(bcastB[:, i * 512:(i + 1) * 512],
                          PB[b][0:128, 0:512]), "sV")

    mm_evac_loop(4, [6, 7], bomm, boev, "V", t_waits=[("sLc", bo_ld)])
    bo_b = cnt["sV"]

    def pxm(s, pt=cnt["sT"], p2d=tuple(p2_done)):
        w("S", s, "sT", pt)
        for (sn, sv) in p2d:
            w("S", s, sn, sv)
        w("S", s, "sLc", cnt["sLc"])
        inc(s.dma_start(out=xm_sb[:, 0:H], in_=x_sh[0:1280:10, :]), "sLc", 16)
        inc(s.dma_start(out=xm_sb[0:77, H:2 * H], in_=x_sh[1280:2050:10, :]),
            "sLc", 16)
    S_(pxm)
    xm_mark = cnt["sLc"]

    with nc.sbuf_tensor("wob", [128, 2 * H], BF16) as wob:
        wo_marks = []
        t_base_o = cnt["sT"]

        def pog(g):
            w("G", g, "sT", t_base_o)
            for kc in range(16):
                swo = "sWoa" if kc % 2 == 0 else "sWob"
                if kc >= 2:
                    g.wait_ge(sem["sT"], t_base_o + kc - 1)
                dma_settled(g, swo)
                inc(g.dma_start(out=wob[:, (kc % 2) * H:(kc % 2 + 1) * H],
                                in_=wo[kc * 128:(kc + 1) * 128, :]), swo, 16)
                wo_marks.append((swo, cnt[swo]))
        G_(pog)

        def pot(t):
            w("T", t, "sV", ctxT_done)
            for b in range(8):
                bank_wait(t, b)
            for kc in range(16):
                w("T", t, wo_marks[kc][0], wo_marks[kc][1])
                for mc, cw in MC:
                    rcnt = cw + (1 if mc == 1 else 0)
                    for n in range(4):
                        m = t.matmul(
                            PB[mc * 4 + n][0:rcnt, 0:512],
                            ctxT[:, kc * 206 + mc * 128:
                                 kc * 206 + mc * 128 + rcnt],
                            wob[:, (kc % 2) * H + n * 512:
                                (kc % 2) * H + (n + 1) * 512],
                            start=(kc == 0), stop=(kc == 15))
                inc(m, "sT")
        T_(pot)
        oproj_t = cnt["sT"]

        def poev(v):
            w("V", v, "sT", oproj_t)
            w("V", v, "sV", bo_b)
            for mc, cw in MC:
                rcnt = cw + (1 if mc == 1 else 0)
                for n in range(4):
                    m = v.tensor_tensor(
                        anl_sb[0:rcnt, mc * H + n * 512:mc * H + (n + 1) * 512],
                        PB[mc * 4 + n][0:rcnt, 0:512],
                        bcastB[0:rcnt, n * 512:(n + 1) * 512], ALU.add)
            iv = inc(m, "sV")
            for b in range(8):
                bankg[b].append(("sV", iv))
        V_(poev)

        for mc, cw in MC:
            rcnt = cw + (1 if mc == 1 else 0)
            ln_full(anl_sb[0:rcnt, mc * H:(mc + 1) * H], H, p=rcnt)

            def pnorm(v, mc=mc, rcnt=rcnt):
                inc(v.tensor_scalar(anl_sb[0:rcnt, mc * H:(mc + 1) * H],
                                    anl_sb[0:rcnt, mc * H:(mc + 1) * H],
                                    stats[0:rcnt, 2:3], stats[0:rcnt, 5:6],
                                    ALU.subtract, ALU.mult), "sV")
            V_(pnorm)
        anl_mark = cnt["sV"]

    def pgbase(g, anl_mark=anl_mark):
        w("G", g, "sV", anl_mark)
        w("G", g, "sGc", cnt["sGc"])
        inc(g.dma_start(out=rows[0:1, 0:H], in_=anl_sb[77:78, H:2 * H]),
            "sGc", 16)
    G_(pgbase)
    lnb_row = cnt["sGc"]

    def lbmm(t, i, b, lnb_row=lnb_row):
        if i == 0:
            w("T", t, "sGc", lnb_row)
        inc(t.matmul(PB[b][0:128, 0:512], ones32[0:1, 0:128],
                     rows[0:1, i * 512:(i + 1) * 512], start=True, stop=True),
            "sT")

    def lbev(v, i, b):
        inc(v.tensor_copy(bcastB[:, i * 512:(i + 1) * 512],
                          PB[b][0:128, 0:512]), "sV")

    mm_evac_loop(4, [6, 7], lbmm, lbev, "V", t_waits=[("sV", cnt["sV"])])
    lnbase_b = cnt["sV"]

    for mc, cw in MC:
        def pm1(v, mc=mc, cw=cw):
            w("V", v, "sLc", xm_mark)
            inc(v.tensor_tensor(x1m_sb[0:cw, mc * H:(mc + 1) * H],
                                anl_sb[0:cw, mc * H:(mc + 1) * H],
                                xm_sb[0:cw, mc * H:(mc + 1) * H], ALU.add),
                "sV")
        V_(pm1)
        ln_full(x1m_sb[0:cw, mc * H:(mc + 1) * H], H, p=cw)

        def pm2(v, mc=mc, cw=cw):
            inc(v.tensor_scalar(x1nm_sb[0:cw, mc * H:(mc + 1) * H],
                                x1m_sb[0:cw, mc * H:(mc + 1) * H],
                                stats[0:cw, 2:3], stats[0:cw, 5:6],
                                ALU.subtract, ALU.mult), "sV")
        V_(pm2)
    x1m_mark = cnt["sV"]

    with (
        nc.sbuf_tensor("bxt0", [128, H], F32) as bxt0,
        nc.sbuf_tensor("bxt1", [128, H], F32) as bxt1,
        nc.sbuf_tensor("bx10", [128, H], F32) as bx10,
        nc.sbuf_tensor("bx11", [128, H], F32) as bx11,
        nc.sbuf_tensor("bxn0", [128, H], BF16) as bxn0,
        nc.sbuf_tensor("bxn1", [128, H], BF16) as bxn1,
    ):
        bxt = [bxt0, bxt1]
        bx1 = [bx10, bx11]
        bxn = [bxn0, bxn1]
        bread = [[], []]
        bwrit = [[], []]
        for ti in range(17):
            r = ti % 2
            row0 = ti * 128

            sb_ = "sBa" if r == 0 else "sBb"

            def pbs(s, r=r, row0=row0, rd=tuple(bread[r]), sb_=sb_,
                    pt=cnt["sT"], p2d=tuple(p2_done)):
                if not rd:
                    w("S", s, "sT", pt)
                    for (sn, sv) in p2d:
                        w("S", s, sn, sv)
                for (sn, sv) in rd:
                    s.wait_ge(sem[sn], sv)
                dma_settled(s, sb_)
                inc(s.dma_start(out=bxt[r][:], in_=x_sh[row0:row0 + 128, :]),
                    sb_, 16)
            S_(pbs)
            ld = cnt[sb_]

            def pbv(v, r=r, ld=ld, rd=tuple(bwrit[r]), sb_=sb_):
                w("V", v, sb_, ld)
                w("V", v, "sV", lnbase_b)
                for (sn, sv) in rd:
                    w("V", v, sn, sv)
                inc(v.tensor_tensor(bx1[r][:], bxt[r][:], bcastB[:], ALU.add),
                    "sV")
            V_(pbv)
            bread[r] = [("sV", cnt["sV"])]
            ln_full(bx1[r][:], H)

            def pbv2(v, r=r):
                inc(v.tensor_scalar(bxn[r][:], bx1[r][:], stats[:, 2:3],
                                    stats[:, 5:6], ALU.subtract, ALU.mult),
                    "sV")
            V_(pbv2)
            vm = cnt["sV"]

            sn_ = "sNa" if r == 0 else "sNb"

            def pbg(g, r=r, row0=row0, vm=vm, sn_=sn_):
                w("G", g, "sV", vm)
                dma_settled(g, sn_)
                inc(g.dma_start(out=x1_scr[row0:row0 + 128, :], in_=bx1[r][:]),
                    sn_, 16)
                inc(g.dma_start(out=x1n_scr[row0:row0 + 128, :], in_=bxn[r][:]),
                    sn_, 16)
            G_(pbg)
            bwrit[r] = [(sn_, cnt[sn_])]
        base_written = [("sNa", cnt["sNa"]), ("sNb", cnt["sNb"])]

        def pmw(g):
            for (sn, sv) in base_written:
                g.wait_ge(sem[sn], sv)
            w("G", g, "sV", x1m_mark)
            inc(g.dma_start(out=x1_scr[0:1280:10, :], in_=x1m_sb[:, 0:H]),
                "sW", 16)
            inc(g.dma_start(out=x1_scr[1280:2050:10, :],
                            in_=x1m_sb[0:77, H:2 * H]), "sW", 16)
            inc(g.dma_start(out=x1n_scr[0:1280:10, :], in_=x1nm_sb[:, 0:H]),
                "sW", 16)
            inc(g.dma_start(out=x1n_scr[1280:2050:10, :],
                            in_=x1nm_sb[0:77, H:2 * H]), "sW", 16)
        G_(pmw)
    x1_all_written = [("sNa", cnt["sNa"]), ("sNb", cnt["sNb"]),
                      ("sW", cnt["sW"])]
    att_done = [("sV", cnt["sV"]), ("sT", cnt["sT"]), ("sA", cnt["sA"]),
                ("sLc", cnt["sLc"]), ("sGc", cnt["sGc"])]
    att.close()
    att_end = [(n, cnt[n]) for n in
               ("sV", "sT", "sA", "sLc", "sGc", "sW", "sNa", "sNb",
                "sQa", "sQb", "sWoa", "sWob", "sBa", "sBb")]


    SC1 = sc_b[:, 10:11]
    SC3 = sc_b[:, 11:12]
    SC2 = sc_b[:, 12:13]
    SC2SQ = sc_b[:, 9:10]
    out_marks = []
    gsub = [0]
    Y_TBS = [[(0, 1024)], [(0, 640), (640, 512)]]

    def t2b_copy(a):
        w("A", a, "sC", ag_done)
        for q in range(4):
            for hh in range(2):
                r0 = hh * 2752
                inc(a.dma_start(
                    out=t2B[q * HIDP + r0:q * HIDP + r0 + 2752, :],
                    in_=t2T[r0:r0 + 2752, q * 512:(q + 1) * 512]),
                    "sB2", 16)
    A_(t2b_copy)
    t2b_mark = cnt["sB2"]

    with nc.sbuf_tensor("gbuf", [128, NCH * 1152], BF16) as gbuf:
        g_read_t = [0]
        silu_marks = []
        hgrp_ctr = [0]
        prev_fim = [0]
        for half, (tok0, tw_half) in enumerate(FFN_HALVES):
            nts = (tw_half + 511) // 512
            ts_slices = [(k * 512, min(512, tw_half - k * 512))
                         for k in range(nts)]
            hstk = ExitStack()
            x1nT = hstk.enter_context(
                nc.sbuf_tensor(f"x1nT{half}", [128, 16 * tw_half], BF16,
                               is_transpose_dest=True))
            w1r = hstk.enter_context(
                nc.sbuf_tensor(f"w1r{half}", [128, 2 * 2048], BF16))
            w3r = hstk.enter_context(
                nc.sbuf_tensor(f"w3r{half}", [128, 2 * 2048], BF16))
            silu0 = hstk.enter_context(
                nc.sbuf_tensor(f"silu0{half}", [128, 512], BF16))
            silu1 = hstk.enter_context(
                nc.sbuf_tensor(f"silu1{half}", [128, 512], BF16))
            silub = [silu0, silu1]
            w13_guard = [0, 0]
            if True:
                def f0s(s, half=half, tok0=tok0, tw_half=tw_half,
                        pv_=cnt["sV"], pa_=cnt["sA"], poa_=cnt["sOa"],
                        pob_=cnt["sOb"]):
                    if half == 0:
                        for (sn, sv) in att_end:
                            s.wait_ge(sem[sn], sv)
                        for (sn, sv) in x1_all_written:
                            s.wait_ge(sem[sn], sv)
                    else:
                        s.wait_ge(sem["sT"], g_read_t[0])
                        w("S", s, "sV", pv_)
                        w("S", s, "sA", pa_)
                        w("S", s, "sOa", poa_)
                        w("S", s, "sOb", pob_)
                    w("S", s, "sLc", cnt["sLc"])
                    for dc in range(16):
                        inc(s.dma_start(
                            out=x1nT[:, dc * tw_half:(dc + 1) * tw_half],
                            in_=x1n_scr[tok0:tok0 + tw_half,
                                        dc * 128:(dc + 1) * 128],
                            transpose=True), "sLc", 16)
                S_(f0s)
                x1nT_m = cnt["sLc"]

                for ch in range(NCH):
                    r = ch % 2

                    sw13 = "sW13a" if r == 0 else "sW13b"

                    def fls(s, ch=ch, r=r, half=half,
                            wg=w13_guard[r], ag_done=ag_done, sw13=sw13):
                        if ch < 2 and half == 0:
                            s.wait_ge(sem["sC"], ag_done)
                        if wg:
                            s.wait_ge(sem["sT"], wg)
                        dma_settled(s, sw13)
                        inc(s.dma_start(
                            out=w1r[:, r * 2048:(r + 1) * 2048].rearrange(
                                "p (dc h) -> p dc h", h=128),
                            in_=t1T[:, ch * 128:(ch + 1) * 128].rearrange(
                                "(dc p) h -> p dc h", p=128)), sw13, 16)
                        inc(s.dma_start(
                            out=w3r[:, r * 2048:(r + 1) * 2048].rearrange(
                                "p (dc h) -> p dc h", h=128),
                            in_=t3T[:, ch * 128:(ch + 1) * 128].rearrange(
                                "(dc p) h -> p dc h", p=128)), sw13, 16)
                    S_(fls)
                    wm_ch = cnt[sw13]

                    groups = [ts_slices[0:2]]
                    if len(ts_slices) > 2:
                        groups.append(ts_slices[2:])
                    for gi, grp in enumerate(groups):
                        base = (hgrp_ctr[0] % 2) * 4
                        hgrp_ctr[0] += 1

                        def fh(t, r=r, grp=tuple(grp), base=base,
                               x1nT=x1nT, tw_half=tw_half, wm_ch=wm_ch,
                               x1nT_m=x1nT_m, sw13=sw13):
                            w("T", t, sw13, wm_ch)
                            w("T", t, "sLc", x1nT_m)
                            for k in range(len(grp)):
                                bank_wait(t, base + k)
                                bank_wait(t, base + 2 + k)
                            for dc in range(16):
                                for k, (t0, tw_) in enumerate(grp):
                                    m3 = t.matmul(
                                        PB[base + k][0:128, 0:tw_],
                                        w1r[:, r * 2048 + dc * 128:
                                            r * 2048 + (dc + 1) * 128],
                                        x1nT[:, dc * tw_half + t0:
                                             dc * tw_half + t0 + tw_],
                                        start=(dc == 0), stop=(dc == 15))
                                for k, (t0, tw_) in enumerate(grp):
                                    m3 = t.matmul(
                                        PB[base + 2 + k][0:128, 0:tw_],
                                        w3r[:, r * 2048 + dc * 128:
                                            r * 2048 + (dc + 1) * 128],
                                        x1nT[:, dc * tw_half + t0:
                                             dc * tw_half + t0 + tw_],
                                        start=(dc == 0), stop=(dc == 15))
                            inc(m3, "sT")
                        T_(fh)
                        hm = cnt["sT"]

                        def fsi(a, grp=tuple(grp), base=base, hm=hm,
                                sg=(silu_marks[-1] if silu_marks else 0)):
                            w("A", a, "sT", hm)
                            if sg:
                                w("A", a, "sV", sg)
                            for k, (t0, tw_) in enumerate(grp):
                                m = a.activation(silub[k][0:128, 0:tw_],
                                                 PB[base + k][0:128, 0:tw_],
                                                 AF.Silu, scale=SC1)
                            inc(m, "sA")
                        A_(fsi)
                        am = cnt["sA"]

                        def fg(v, ch=ch, grp=tuple(grp), base=base, am=am,
                               tw_half=tw_half):
                            w("V", v, "sA", am)
                            for k, (t0, tw_) in enumerate(grp):
                                m = v.scalar_tensor_tensor(
                                    gbuf[:, ch * tw_half + t0:
                                         ch * tw_half + t0 + tw_],
                                    PB[base + 2 + k][0:128, 0:tw_], SC3,
                                    silub[k][0:128, 0:tw_],
                                    ALU.mult, ALU.mult)
                            iv = inc(m, "sV")
                            for k in range(len(grp)):
                                bankg[base + k].append(("sV", iv))
                                bankg[base + 2 + k].append(("sV", iv))
                        V_(fg)
                        silu_marks.append(cnt["sV"])
                    w13_guard[r] = cnt["sT"]
                g_done = cnt["sV"]
                hTdone = cnt["sT"]
            hstk.close()

            tbs = Y_TBS[half]
            nsub_max = tbs[0][1] // 128
            ystk = ExitStack()
            ysb = ystk.enter_context(
                nc.sbuf_tensor(f"ysb{half}", [128, nsub_max * H], F32))
            x1ta = ystk.enter_context(
                nc.sbuf_tensor(f"x1ta{half}", [128, H], F32))
            x1tb = ystk.enter_context(
                nc.sbuf_tensor(f"x1tb{half}", [128, H], F32))
            outa = ystk.enter_context(
                nc.sbuf_tensor(f"outa{half}", [128, H], F32))
            outc = ystk.enter_context(
                nc.sbuf_tensor(f"outc{half}", [128, H], F32))
            w2r = ystk.enter_context(
                nc.sbuf_tensor(f"w2r{half}", [128, 4 * 512], BF16))
            if True:
                x1t = [x1ta, x1tb]
                outb = [outa, outc]
                w2_guard = [0, 0, 0, 0]
                first_q = [True, True]
                W2SEMS = ["sW2a", "sW2b", "sW2c", "sW2d"]
                for tb0, tbw in tbs:
                    nsub = tbw // 128
                    ev_mark = 0
                    for q in range(4):
                        for ch in range(NCH):
                            r = ch % 2
                            sl = ch % 4

                            sw2 = W2SEMS[sl]
                            EMIT = S_ if r == 0 else G_
                            ec = "S" if r == 0 else "G"

                            def fws(s, ch=ch, sl=sl, q=q, wg=w2_guard[sl],
                                    fq=first_q[r], sw2=sw2, ec=ec,
                                    gd=g_done, ht=hTdone):
                                if fq:
                                    w(ec, s, "sV", gd)
                                    w(ec, s, "sT", ht)
                                    w(ec, s, "sB2", t2b_mark)
                                if wg:
                                    s.wait_ge(sem["sT"], wg)
                                dma_settled(s, sw2)
                                inc(s.dma_start(
                                    out=w2r[:, sl * 512:(sl + 1) * 512],
                                    in_=t2B[(q * HIDP + ch * 128):
                                            (q * HIDP + (ch + 1) * 128), :]),
                                    sw2, 16)
                            EMIT(fws)
                            first_q[r] = False
                            w2m = cnt[sw2]

                            def fy(t, ch=ch, sl=sl, tb0=tb0, nsub=nsub,
                                   w2m=w2m, g_done=g_done, tw_half=tw_half,
                                   sw2=sw2):
                                w("T", t, sw2, w2m)
                                if ch == 0:
                                    w("T", t, "sV", g_done)
                                    for s_ in range(nsub):
                                        bank_wait(t, s_)
                                for s_ in range(nsub):
                                    m = t.matmul(
                                        PB[s_][0:128, 0:512],
                                        gbuf[:, ch * tw_half + tb0 +
                                             s_ * 128:
                                             ch * tw_half + tb0 +
                                             s_ * 128 + 128],
                                        w2r[:, sl * 512:(sl + 1) * 512],
                                        start=(ch == 0),
                                        stop=(ch == NCH - 1))
                                inc(m, "sT")
                            T_(fy)
                            w2_guard[sl] = cnt["sT"]
                        ym = cnt["sT"]

                        def fyev(a, q=q, nsub=nsub, ym=ym, pf=prev_fim[0]):
                            w("A", a, "sT", ym)
                            if pf:
                                w("A", a, "sV", pf)
                            for s_ in range(nsub):
                                m = a.activation(
                                    ysb[:, s_ * H + q * 512:
                                        s_ * H + (q + 1) * 512],
                                    PB[s_][0:128, 0:512], AF.Copy)
                            ia = inc(m, "sA")
                            for s_ in range(nsub):
                                bankg[s_].append(("sA", ia))
                        A_(fyev)
                        ev_mark = cnt["sA"]

                    for s_ in range(nsub):
                        k = gsub[0]
                        gsub[0] += 1
                        r = k % 2
                        tokg = tok0 + tb0 + s_ * 128

                        sf = "sFa" if r == 0 else "sFb"

                        def fx1(s, r=r, tokg=tokg, sf=sf,
                                om=(out_marks[k - 2] if k >= 2 else None)):
                            if om is not None:
                                s.wait_ge(sem[om[0]], om[1])
                            dma_settled(s, sf)
                            inc(s.dma_start(out=x1t[r][:],
                                            in_=x1_scr[tokg:tokg + 128, :]),
                                sf, 16)
                        S_(fx1)
                        x1lm = cnt[sf]

                        yslice = ysb[:, s_ * H:(s_ + 1) * H]

                        def fst(v, yslice=yslice, em=ev_mark):
                            w("V", v, "sA", em)
                            v.tensor_scalar(scr2k[:], yslice, 1.0, None,
                                            ALU.mult, ALU.add,
                                            accum_out=stats[:, 0:1])
                            v.scalar_tensor_tensor(scr2k[:], yslice, SC2SQ,
                                                   yslice, ALU.mult, ALU.mult,
                                                   accum_out=stats[:, 1:2])
                            v.tensor_scalar(stats[:, 2:3], stats[:, 0:1],
                                            INV_H, None, ALU.mult)
                            v.tensor_tensor(stats[:, 2:3], stats[:, 2:3], SC2,
                                            ALU.mult)
                            v.tensor_scalar(stats[:, 3:4], stats[:, 1:2],
                                            INV_H, None, ALU.mult)
                            v.tensor_tensor(stats[:, 6:7], stats[:, 2:3],
                                            stats[:, 2:3], ALU.mult)
                            inc(v.tensor_tensor(stats[:, 3:4], stats[:, 3:4],
                                                stats[:, 6:7], ALU.subtract),
                                "sV")
                        V_(fst)
                        svm = cnt["sV"]

                        def fsq(a, svm=svm):
                            w("A", a, "sV", svm)
                            inc(a.activation(stats[:, 4:5], stats[:, 3:4],
                                             AF.Sqrt, bias=epsc[:, 0:1]),
                                "sA")
                        A_(fsq)
                        sqm = cnt["sA"]

                        def ffin(v, r=r, yslice=yslice, sqm=sqm, x1lm=x1lm,
                                 sf=sf):
                            w("V", v, "sA", sqm)
                            v.reciprocal(stats[:, 5:6], stats[:, 4:5])
                            v.tensor_tensor(stats[:, 6:7], SC2, stats[:, 5:6],
                                            ALU.mult)
                            v.tensor_tensor(stats[:, 7:8], stats[:, 2:3],
                                            stats[:, 5:6], ALU.mult)
                            v.tensor_scalar(scr2k[:], yslice, stats[:, 6:7],
                                            stats[:, 7:8], ALU.mult,
                                            ALU.subtract)
                            w("V", v, sf, x1lm)
                            inc(v.tensor_tensor(outb[r][:], scr2k[:],
                                                x1t[r][:], ALU.add), "sV")
                        V_(ffin)
                        fim = cnt["sV"]

                        so = "sOa" if r == 0 else "sOb"

                        def fwr(g, r=r, tokg=tokg, fim=fim, so=so):
                            w("G", g, "sV", fim)
                            dma_settled(g, so)
                            inc(g.dma_start(out=out_sh[tokg:tokg + 128, :],
                                            in_=outb[r][:]), so, 16)
                        G_(fwr)
                        out_marks.append((so, cnt[so]))
                    prev_fim[0] = cnt["sV"]
                g_read_t[0] = cnt["sT"]
            ystk.close()

    foa, fob = cnt["sOa"], cnt["sOb"]

    def fend(g):
        g.wait_ge(sem["sOa"], foa)
        g.wait_ge(sem["sOb"], fob)
    G_(fend)

    es.close()
    return nc


_BF = ml_dtypes.bfloat16


def _build_core_inputs(inputs):
    xb = np.asarray(inputs["L0_attn_hidden_states"], dtype=np.float32)
    base = {
        "wq": np.ascontiguousarray(inputs["L0_attn_wq"], dtype=np.float32),
        "wk": np.ascontiguousarray(inputs["L0_attn_wk"], dtype=np.float32),
        "wv": np.ascontiguousarray(inputs["L0_attn_wv"], dtype=np.float32),
        "wo": np.ascontiguousarray(inputs["L0_attn_wo"], dtype=np.float32),
        "bq": np.ascontiguousarray(inputs["L0_attn_bq"], dtype=np.float32),
        "bk": np.ascontiguousarray(inputs["L0_attn_bk"], dtype=np.float32),
        "bv": np.ascontiguousarray(inputs["L0_attn_bv"], dtype=np.float32),
        "bo": np.ascontiguousarray(inputs["L0_attn_bo"], dtype=np.float32),
        "id32in": np.eye(128, dtype=np.float32),
        "ones32in": np.ones((128, 128), dtype=np.float32),
        "onesbin": np.ones((128, 1), dtype=_BF),
        "onesrbin": np.ones((1, 128), dtype=_BF),
    }
    w1 = np.asarray(inputs["L0_ffn_w1"], dtype=np.float32)
    w3 = np.asarray(inputs["L0_ffn_w3"], dtype=np.float32)
    w2 = np.asarray(inputs["L0_ffn_w2"], dtype=np.float32)

    masks = []
    for half in range(2):
        m = np.zeros((GR, KEYN), dtype=_BF)
        off = 0 if half == 0 else 2050
        for g in range(NG):
            for i in range(GR):
                q_local = 10 * (GR * g + i)
                q_glob = off + q_local
                wlen = min(10, S_FULL - q_glob)
                c0 = g * GK + (q_local - GK * g)
                m[i, c0:c0 + wlen] = 1.0
        masks.append(m)

    per_core = []
    for c in range(8):
        b, half = divmod(c, 2)
        xs = np.zeros((T, H), dtype=np.float32)
        if half == 0:
            xs[0:2050] = xb[b, 0:2050]
        else:
            xs[0:2046] = xb[b, 2050:4096]
        d = dict(base)
        d["x_sh"] = xs
        d["maskin"] = masks[half]
        d["w1s"] = np.ascontiguousarray(w1[:, 256 * c:256 * (c + 1)])
        d["w3s"] = np.ascontiguousarray(w3[:, 256 * c:256 * (c + 1)])
        w2c = np.zeros((H, 688), dtype=np.float32)
        lo = 688 * c
        hi = min(688 * (c + 1), HID)
        w2c[:, 0:hi - lo] = w2[:, lo:hi]
        d["w2s"] = w2c
        per_core.append(d)
    return per_core


_NC = None


def kernel(**inputs):
    global _NC
    per_core = _build_core_inputs(inputs)
    if _NC is None:
        _NC = build_program()
    res = run_bass_kernel_spmd(_NC, per_core, core_ids=list(range(8)))
    out = np.empty((4, S_FULL, H), dtype=np.float32)
    for c in range(8):
        b, half = divmod(c, 2)
        o = res.results[c]["out_sh"]
        if half == 0:
            out[b, 0:2050] = o[0:2050]
        else:
            out[b, 2050:4096] = o[0:2046]
    return out


if __name__ == "__main__":
    nc = build_program()
    n_inst = sum(1 for _ in nc.all_instructions())
    print("build ok, instructions:", n_inst)
